# version 01
# baseline (speedup 1.0000x reference)
import math
import numpy as np
import concourse.bass as bass
import concourse.mybir as mybir
from concourse.bass_utils import run_bass_kernel_spmd
from concourse.aot_env import allow_ici_all_to_all
from contextlib import ExitStack

F32 = mybir.dt.float32
BF16 = mybir.dt.bfloat16
FP16 = mybir.dt.float16
AF = mybir.ActivationFunctionType
ALU = mybir.AluOpType
AX = mybir.AxisListType.X

NCORE = 8
SEQ = 4096
NSEQ = 2
DM = 2048
OWN = 512
OWNT = NSEQ * OWN
DSL = 256
NST = 64
RK = 128
HD = 128
NE = 16
EI = 1408
SI = 2816
EPS_LN = 1e-5
EPS_RMS = 1e-6
NDB = DM // 128
NIB = EI // 128
NSB = SI // 128
QSCALE = 1.0 / math.sqrt(HD)

ENGS = ("sync", "gpsimd", "tensor", "vector", "scalar")


class S:
    def __init__(self, nc, name):
        self.sem = nc.alloc_semaphore(name)
        self.n = 0
        self.name = name


class Em:
    def __init__(self):
        self.q = {k: [] for k in ENGS}

    def w(self, eng, sem, val=None):
        v = sem.n if val is None else val
        if v > 0:
            self.q[eng].append(lambda e, s=sem.sem, v=v: e.wait_ge(s, v))

    def o(self, eng, f, inc=None, inck=1):
        if inc is None:
            self.q[eng].append(f)
        else:
            inc.n += inck
            self.q[eng].append(lambda e, f=f, s=inc.sem, k=inck: f(e).then_inc(s, k))

    def dma(self, eng, f, sem):
        self.o(eng, f, inc=sem, inck=16)

    def barrier(self, bars, engs=ENGS):
        marks = {}
        for e in engs:
            self.o(e, lambda x: x.drain(), inc=bars[e])
            marks[e] = bars[e].n
        for e in engs:
            for e2 in engs:
                if e2 != e:
                    self.w(e, bars[e2], marks[e2])


def build(phases="ABCDE", dbg=()):
    nc = bass.Bass("TRN2", target_bir_lowering=False)

    def inp(name, shape, dtype=F32):
        return nc.dram_tensor(name, shape, dtype, kind="ExternalInput").ap()

    x_own = inp("x_own", [OWNT, DM])
    A_slc = inp("A_slc", [DSL, NST])
    Dcol = inp("Dcol", [128, NDB])
    nwrow = inp("nwrow", [1, DM])
    nbrow = inp("nbrow", [1, DM])
    dtbrow = inp("dtbrow", [1, DM])
    l1row = inp("l1row", [1, DM])
    l2row = inp("l2row", [1, DM])
    xpwT = inp("xpwT", [DM, 2 * NST + RK])
    dtwT = inp("dtwT", [RK, DM])
    opwT = inp("opwT", [DM, DM])
    wq_own = inp("wq_own", [DM, 2 * HD])
    wk_own = inp("wk_own", [DM, 2 * HD])
    wv_own = inp("wv_own", [DM, 2 * HD])
    wo_own = inp("wo_own", [2 * HD, DM])
    gwT = inp("gwT", [DM, NE])
    egT = inp("egT", [2, DM, EI])
    euT = inp("euT", [2, DM, EI])
    edT = inp("edT", [2, EI, DM])
    sgT = inp("sgT", [DM, SI])
    suT = inp("suT", [DM, SI])
    sdT = inp("sdT", [SI, DM])

    out_own = nc.dram_tensor("out_own", [OWNT, DM], F32, kind="ExternalOutput").ap()
    dbg_outs = {}
    for nm, shp in (("lx", [OWNT, DM]), ("xn", [OWNT, DM]), ("dt", [OWNT, DM]),
                    ("y", [NSEQ * DSL, SEQ]), ("x1", [OWNT, DM]),
                    ("hln", [OWNT, DM]), ("attn", [OWNT, DM]),
                    ("cw", [OWNT, NE]), ("x2i", [OWNT, DM]),
                    ("qt", [128, 2 * SEQ]), ("kt", [128, 2 * SEQ]),
                    ("htb", [128, NDB * 512]), ("hti", [128, 512]), ("hbt", [128, DM]), ("stg", [128, DM]), ("gl", [OWNT, NE]), ("shr", [OWNT, DM]), ("hhs", [NSEQ * 128, 2048]),
                    ("vv", [128, 32 * 256]), ("aot", [128, 2 * SEQ]),
                    ("routed", [OWNT, DM]), ("shared", [OWNT, DM])):
        if nm in dbg:
            dbg_outs[nm] = nc.dram_tensor("dbg_" + nm, shp, F32,
                                          kind="ExternalOutput").ap()

    e1_in = nc.dram_tensor("e1_in", [NCORE * 640, 1024], FP16)
    e1_out = nc.dram_tensor("e1_out", [NCORE * 640, 1024], FP16)
    e2_in = [nc.dram_tensor(f"e2_in{s}", [DM, OWN], BF16) for s in range(NSEQ)]
    e2_out = [nc.dram_tensor(f"e2_out{s}", [DM, OWN], BF16) for s in range(NSEQ)]
    ht_in = [nc.dram_tensor(f"ht_in{s}", [DM, OWN], BF16) for s in range(NSEQ)]
    ht_out = [nc.dram_tensor(f"ht_out{s}", [NCORE * DM, OWN], BF16) for s in range(NSEQ)]
    ars_in = [nc.dram_tensor(f"ars_in{s}", [SEQ, DM], BF16) for s in range(NSEQ)]
    ars_out = [nc.dram_tensor(f"ars_out{s}", [OWN, DM], BF16) for s in range(NSEQ)]
    tt_in = [nc.dram_tensor(f"tt_in{s}", [DM, OWN], BF16) for s in range(NSEQ)]
    tt_out = [nc.dram_tensor(f"tt_out{s}", [NCORE * DM, OWN], BF16) for s in range(NSEQ)]
    cw_in = [nc.dram_tensor(f"cw_in{s}", [NE, OWN], F32) for s in range(NSEQ)]
    cw_out = [nc.dram_tensor(f"cw_out{s}", [NE, OWN], F32) for s in range(NSEQ)]
    mrs_in = [nc.dram_tensor(f"mrs_in{s}", [NCORE * DM, OWN], BF16) for s in range(NSEQ)]
    mrs_out = [nc.dram_tensor(f"mrs_out{s}", [DM, OWN], BF16) for s in range(NSEQ)]
    sp_lx = nc.dram_tensor("sp_lx", [OWNT, DM], F32)
    sp_x1 = nc.dram_tensor("sp_x1", [OWNT, DM], F32)
    sp_lx1 = nc.dram_tensor("sp_lx1", [OWNT, DM], F32)
    sp_x2i = nc.dram_tensor("sp_x2i", [OWNT, DM], F32)

    ident_dram = nc.inline_tensor(np.eye(128, dtype=np.float32), "identf")

    qS = S(nc, "qS")
    qG = S(nc, "qG")
    cc = S(nc, "cc")
    sT = S(nc, "sT")
    sV = S(nc, "sV")
    sC = S(nc, "sC")
    sSy = S(nc, "sSy")
    sGp = S(nc, "sGp")
    BAR = {"sync": sSy, "gpsimd": sGp, "tensor": sT, "vector": sV, "scalar": sC}

    em = Em()

    stk = ExitStack()
    ident_f = stk.enter_context(nc.sbuf_tensor("ident_f", [128, 128], F32))
    ident_b = stk.enter_context(nc.sbuf_tensor("ident_b", [128, 128], BF16))
    ident_h = stk.enter_context(nc.sbuf_tensor("ident_h", [128, 128], FP16))
    Dcol_sb = stk.enter_context(nc.sbuf_tensor("Dcol_sb", [128, NDB], F32))
    A_sb = stk.enter_context(nc.sbuf_tensor("A_sb", [128, 2 * NST], F32))
    st = stk.enter_context(nc.sbuf_tensor("stats", [128, 48], F32))
    kcol = stk.enter_context(nc.sbuf_tensor("kcol", [128, 4], F32))
    ps = [stk.enter_context(nc.psum_tensor(f"ps{i}", [128, 512], F32)) for i in range(4)]
    ptA = stk.enter_context(nc.psum_tensor("ptA", [128, 128], F32))
    ptB = stk.enter_context(nc.psum_tensor("ptB", [128, 128], F32))
    pt16 = stk.enter_context(nc.psum_tensor("pt16", [128, 128], FP16))
    ptbf = stk.enter_context(nc.psum_tensor("ptbf", [128, 128], BF16))

    ps_guard = {}

    def mm_guard(pst):
        g = ps_guard.get(id(pst))
        if g:
            em.w("tensor", g[0], g[1])

    def consume(pst, sem):
        ps_guard[id(pst)] = (sem, sem.n)

    em.dma("sync", lambda e: e.dma_start(out=ident_f[:, :], in_=ident_dram.ap()), qS)
    em.dma("sync", lambda e: e.dma_start(out=Dcol_sb[:, :], in_=Dcol), qS)
    em.dma("sync", lambda e: e.dma_start(
        out=A_sb[:, :].rearrange("p (b n) -> p b n", b=2),
        in_=A_slc.rearrange("(b p) n -> p b n", p=128)), qS)
    em.w("vector", qS)
    em.o("vector", lambda e: e.tensor_copy(ident_b[:, :], ident_f[:, :]))
    em.o("vector", lambda e: e.memset(kcol[:, 0:1], EPS_LN))
    em.o("vector", lambda e: e.memset(kcol[:, 1:2], EPS_RMS))
    em.o("vector", lambda e: e.memset(kcol[:, 2:3], QSCALE))
    em.o("vector", lambda e: e.tensor_copy(ident_h[:, :], ident_f[:, :]), inc=sV)
    em.w("scalar", sV)
    em.w("scalar", qS)
    em.o("scalar", lambda e: e.activation(A_sb[:, :], A_sb[:, :], AF.Exp))
    em.o("scalar", lambda e: e.activation(A_sb[:, :], A_sb[:, :], AF.Copy, scale=-1.0),
         inc=sC)

    def tv(f):
        em.w("vector", sV)
        em.o("vector", f, inc=sV)

    def ln_stats(xt, F, c0, eps, scratch):
        em.w("scalar", sV)
        em.w("vector", sC)
        em.o("vector", lambda e: e.reduce_sum(st[:, c0:c0 + 1], xt[:, 0:F], axis=AX),
             inc=sV)
        em.o("scalar", lambda e: e.activation(
            scratch[:, 0:F], xt[:, 0:F], AF.Square, accum_out=st[:, c0 + 1:c0 + 2]),
            inc=sC)
        em.w("vector", sC)
        tv(lambda e: e.tensor_scalar(
            st[:, c0 + 2:c0 + 3], st[:, c0:c0 + 1], 1.0 / F, None, ALU.mult))
        tv(lambda e: e.tensor_mul(
            st[:, c0 + 3:c0 + 4], st[:, c0 + 2:c0 + 3], st[:, c0 + 2:c0 + 3]))
        tv(lambda e: e.tensor_scalar(
            st[:, c0 + 4:c0 + 5], st[:, c0 + 1:c0 + 2], 1.0 / F, None, ALU.mult))
        tv(lambda e: e.tensor_sub(
            st[:, c0 + 5:c0 + 6], st[:, c0 + 4:c0 + 5], st[:, c0 + 3:c0 + 4]))
        em.w("scalar", sV)
        em.o("scalar", lambda e: e.activation(
            st[:, c0 + 4:c0 + 5], st[:, c0 + 5:c0 + 6], AF.Sqrt, bias=kcol[:, 0:1]),
            inc=sC)
        em.w("vector", sC)
        tv(lambda e: e.reciprocal(st[:, c0 + 5:c0 + 6], st[:, c0 + 4:c0 + 5]))
        em.w("vector", sV)

    def rms_stats(xt, F, c0, eps, scratch):
        em.w("scalar", sV)
        em.o("scalar", lambda e: e.activation(
            scratch[:, 0:F], xt[:, 0:F], AF.Square, accum_out=st[:, c0:c0 + 1]),
            inc=sC)
        em.w("vector", sC)
        tv(lambda e: e.tensor_scalar(
            st[:, c0 + 2:c0 + 3], st[:, c0:c0 + 1], 1.0 / F, None, ALU.mult))
        em.w("scalar", sV)
        em.o("scalar", lambda e: e.activation(
            st[:, c0:c0 + 1], st[:, c0 + 2:c0 + 3], AF.Sqrt, bias=kcol[:, 1:2]), inc=sC)
        em.w("vector", sC)
        tv(lambda e: e.reciprocal(st[:, c0 + 1:c0 + 2], st[:, c0:c0 + 1]))
        em.w("vector", sV)

    if "A" in phases:
        with ExitStack() as pk:
            A = lambda n, s, d: pk.enter_context(nc.sbuf_tensor(n, s, d))
            xa = A("xa", [128, DM], F32)
            lxa = A("lxa", [128, DM], F32)
            sqa = A("sqa", [128, DM], F32)
            ln2a = A("ln2a", [128, DM], F32)
            xna = A("xna", [128, DM], F32)
            xna_b = A("xna_b", [128, DM], BF16)
            xnT_b = A("xnT_b", [128, NDB * 128], BF16)
            nw_rep = A("nw_rep", [128, DM], F32)
            nb_rep = A("nb_rep", [128, DM], F32)
            dtb_rep = A("dtb_rep", [128, DM], F32)
            xpw_b = A("xpw_b", [128, NDB * 256], BF16)
            dtw_b = A("dtw_b", [128, DM], BF16)
            pj_del = A("pj_del", [128, 128], BF16)
            pbc_h = A("pbc_h", [128, 128], FP16)
            dta = A("dta", [128, DM], F32)
            spa = A("spa", [128, DM], F32)
            dtxa = A("dtxa", [128, DM], F32)
            dt_h = A("dt_h", [128, DM], FP16)
            dtx_h = A("dtx_h", [128, DM], FP16)
            stag = A("stag", [128, NCORE * 5 * 128], FP16)

            em.dma("sync", lambda e: e.dma_start(out=nw_rep[:, :], in_=nwrow.broadcast_to([128, DM])), qS)
            em.dma("sync", lambda e: e.dma_start(out=nb_rep[:, :], in_=nbrow.broadcast_to([128, DM])), qS)
            em.dma("sync", lambda e: e.dma_start(out=dtb_rep[:, :], in_=dtbrow.broadcast_to([128, DM])), qS)
            em.dma("gpsimd", lambda e: e.dma_start(
                out=xpw_b[:, :].rearrange("p (b n) -> p b n", b=NDB),
                in_=xpwT.rearrange("(b p) n -> p b n", p=128)), qG)
            em.dma("gpsimd", lambda e: e.dma_start(out=dtw_b[:, :], in_=dtwT), qG)

            for t in range(8):
                em.w("sync", sV)
                em.w("sync", sC)
                em.dma("sync", lambda e, t=t: e.dma_start(
                    out=xa[:, :], in_=x_own[t * 128:(t + 1) * 128, :]), qS)
                em.w("vector", qS)
                em.w("vector", sT)
                em.w("scalar", qS)
                em.w("scalar", sT)
                ln_stats(xa, DM, 0, EPS_LN, sqa)
                em.o("vector", lambda e: e.tensor_scalar(
                    lxa[:, :], xa[:, :], st[:, 2:3], st[:, 5:6],
                    ALU.subtract, ALU.mult), inc=sV)
                em.w("sync", sV)
                em.dma("sync", lambda e, t=t: e.dma_start(
                    out=sp_lx.ap()[t * 128:(t + 1) * 128, :], in_=lxa[:, :]), qS)
                if "lx" in dbg_outs:
                    em.dma("sync", lambda e, t=t: e.dma_start(
                        out=dbg_outs["lx"][t * 128:(t + 1) * 128, :], in_=lxa[:, :]), qS)
                ln_stats(lxa, DM, 6, EPS_LN, sqa)
                em.o("vector", lambda e: e.tensor_scalar(
                    ln2a[:, :], lxa[:, :], st[:, 8:9], st[:, 11:12],
                    ALU.subtract, ALU.mult))
                em.o("vector", lambda e: e.tensor_mul(sqa[:, :], ln2a[:, :], nw_rep[:, :]))
                em.o("vector", lambda e: e.tensor_add(xna[:, :], sqa[:, :], nb_rep[:, :]))
                em.o("vector", lambda e: e.tensor_copy(xna_b[:, :], xna[:, :]), inc=sV)
                if "xn" in dbg_outs:
                    em.w("sync", sV)
                    em.dma("sync", lambda e, t=t: e.dma_start(
                        out=dbg_outs["xn"][t * 128:(t + 1) * 128, :], in_=xna[:, :]), qS)
                em.w("tensor", sV)
                em.w("tensor", qG)
                for db in range(NDB):
                    mm_guard(ptbf)
                    em.o("tensor", lambda e, db=db: e.transpose(
                        ptbf[:, :], xna_b[:, db * 128:(db + 1) * 128], ident_b[:, :]),
                        inc=sT)
                    em.w("scalar", sT)
                    em.o("scalar", lambda e, db=db: e.copy(
                        xnT_b[:, db * 128:(db + 1) * 128], ptbf[:, :]), inc=sC)
                    consume(ptbf, sC)
                em.w("tensor", sC)
                for pr in range(2):
                    pst = ptA if pr == 0 else ptB
                    mm_guard(pst)
                    for db in range(NDB):
                        em.o("tensor", lambda e, pr=pr, db=db, pst=pst: e.matmul(
                            pst[:, :],
                            xpw_b[:, db * 256 + pr * 128: db * 256 + (pr + 1) * 128],
                            xnT_b[:, db * 128:(db + 1) * 128],
                            start=(db == 0), stop=(db == NDB - 1)),
                            inc=(sT if db == NDB - 1 else None))
                    em.w("vector", sT)
                    dst = pj_del if pr == 0 else pbc_h
                    em.o("vector", lambda e, dst=dst, pst=pst: e.tensor_copy(
                        dst[:, :], pst[:, :]), inc=sV)
                    consume(pst, sV)
                em.w("scalar", sV)
                for sh in range(NCORE):
                    em.o("scalar", lambda e, sh=sh: e.copy(
                        stag[:, (sh * 5 + 4) * 128:(sh * 5 + 5) * 128], pbc_h[:, :]),
                        inc=(sC if sh == NCORE - 1 else None))
                em.w("tensor", sV)
                for dch in range(4):
                    pst = ps[dch]
                    mm_guard(pst)
                    em.o("tensor", lambda e, dch=dch, pst=pst: e.matmul(
                        pst[:, :], pj_del[:, :], dtw_b[:, dch * 512:(dch + 1) * 512],
                        start=True, stop=True), inc=sT)
                em.w("vector", sT)
                for dch in range(4):
                    em.o("vector", lambda e, dch=dch: e.tensor_add(
                        dta[:, dch * 512:(dch + 1) * 512], ps[dch][:, :],
                        dtb_rep[:, dch * 512:(dch + 1) * 512]),
                        inc=(sV if dch == 3 else None))
                for dch in range(4):
                    consume(ps[dch], sV)
                em.w("scalar", sV)
                em.o("scalar", lambda e: e.activation(spa[:, :], dta[:, :], AF.Abs))
                em.o("scalar", lambda e: e.activation(spa[:, :], spa[:, :], AF.Exp, scale=-1.0))
                em.o("scalar", lambda e: e.activation(spa[:, :], spa[:, :], AF.Ln, bias=1.0),
                     inc=sC)
                em.w("vector", sC)
                em.o("vector", lambda e: e.tensor_relu(sqa[:, :], dta[:, :]))
                em.o("vector", lambda e: e.tensor_add(dta[:, :], sqa[:, :], spa[:, :]),
                     inc=sV)
                if "dt" in dbg_outs:
                    em.w("sync", sV)
                    em.dma("sync", lambda e, t=t: e.dma_start(
                        out=dbg_outs["dt"][t * 128:(t + 1) * 128, :], in_=dta[:, :]), qS)
                em.o("vector", lambda e: e.tensor_mul(dtxa[:, :], dta[:, :], xna[:, :]))
                em.o("vector", lambda e: e.tensor_copy(dt_h[:, :], dta[:, :]))
                em.o("vector", lambda e: e.tensor_copy(dtx_h[:, :], dtxa[:, :]), inc=sV)
                em.w("tensor", sV)
                em.w("scalar", qS)
                for which, src in ((0, dt_h), (1, dtx_h)):
                    for db in range(NDB):
                        sh, half = db // 2, db % 2
                        piece = which * 2 + half
                        mm_guard(pt16)
                        em.o("tensor", lambda e, src=src, db=db: e.transpose(
                            pt16[:, :], src[:, db * 128:(db + 1) * 128], ident_h[:, :]),
                            inc=sT)
                        em.w("scalar", sT)
                        em.o("scalar", lambda e, sh=sh, piece=piece: e.copy(
                            stag[:, (sh * 5 + piece) * 128:(sh * 5 + piece + 1) * 128],
                            pt16[:, :]), inc=sC)
                        consume(pt16, sC)
                em.w("sync", sC)
                em.w("sync", qS)
                em.dma("sync", lambda e, t=t: e.dma_start(
                    out=e1_in.ap().rearrange("(f p) n -> p f n", p=128)
                    [:, :, t * 128:(t + 1) * 128],
                    in_=stag[:, :].rearrange("p (f n) -> p f n", n=128)), qS)
                em.w("sync", qS)

            em.w("gpsimd", qS)
            with allow_ici_all_to_all():
                em.w("gpsimd", cc)
                em.o("gpsimd", lambda e: e.collective_compute(
                    "AllToAll", ALU.bypass, replica_groups=[list(range(NCORE))],
                    ins=[e1_in.ap()], outs=[e1_out.ap()]), inc=cc)
            em.barrier(BAR)

    if "B" in phases:
        cc_e1 = cc.n
        with ExitStack() as pk:
            A = lambda n, s, d: pk.enter_context(nc.sbuf_tensor(n, s, d))
            dtT = A("dtT", [128, 2 * SEQ], FP16)
            dtxT = A("dtxT", [128, 2 * SEQ], FP16)
            Brep = A("Brep", [128, SEQ], FP16)
            Crep = A("Crep", [128, SEQ], FP16)
            dAb = A("dAb", [128, SEQ], F32)
            ub = A("ub", [128, SEQ], F32)
            hb = A("hb", [128, SEQ], F32)
            qscan = A("qscan", [128, SEQ], F32)
            yb = A("yb", [128, 4 * SEQ], F32)
            ybf = A("ybf", [128, SEQ], BF16)
            zero1 = A("zero1", [128, 1], F32)
            em.o("vector", lambda e: e.memset(zero1[:, :], 0.0), inc=sV)

            e1o = e1_out.ap().rearrange("(r m) n -> r m n", m=640)
            for s_seq in range(NSEQ):
                em.w("sync", cc, cc_e1)
                em.w("sync", sV)
                for k in range(2):
                    em.dma("sync", lambda e, k=k, s=s_seq: e.dma_start(
                        out=dtT[:, k * SEQ:(k + 1) * SEQ].rearrange("p (r n) -> p r n", r=8),
                        in_=e1o[:, k * 128:(k + 1) * 128, s * 512:(s + 1) * 512]
                        .rearrange("r m n -> m r n")), qS)
                    em.dma("sync", lambda e, k=k, s=s_seq: e.dma_start(
                        out=dtxT[:, k * SEQ:(k + 1) * SEQ].rearrange("p (r n) -> p r n", r=8),
                        in_=e1o[:, 256 + k * 128:256 + (k + 1) * 128, s * 512:(s + 1) * 512]
                        .rearrange("r m n -> m r n")), qS)
                for k in range(2):
                    em.o("vector", lambda e, k=k: e.memset(
                        yb[:, (k * 2) * SEQ:(k * 2 + 1) * SEQ], 0.0),
                        inc=(sV if k == 1 else None))
                ld_done = qS.n
                for n in range(NST):
                    em.w("sync", sV)
                    em.dma("sync", lambda e, n=n, s=s_seq: e.dma_start(
                        out=Brep[:, :].rearrange("p (r n) -> p r n", r=8),
                        in_=e1o[:, 512 + n:513 + n, s * 512:(s + 1) * 512]
                        .rearrange("r m n -> m r n").broadcast_to([128, 8, 512])), qS)
                    em.dma("sync", lambda e, n=n, s=s_seq: e.dma_start(
                        out=Crep[:, :].rearrange("p (r n) -> p r n", r=8),
                        in_=e1o[0:1, 512 + NST + n:513 + NST + n, s * 512:(s + 1) * 512]
                        .broadcast_to([128, 8, 512])), qS)
                    for k in range(2):
                        em.w("scalar", qS, ld_done)
                        em.w("scalar", sV)
                        em.o("scalar", lambda e, k=k, n=n: e.activation(
                            dAb[:, :], dtT[:, k * SEQ:(k + 1) * SEQ], AF.Exp,
                            scale=A_sb[:, k * NST + n:k * NST + n + 1]), inc=sC)
                        em.w("vector", sC)
                        em.w("vector", qS)
                        em.o("vector", lambda e, k=k: e.tensor_mul(
                            ub[:, :], dtxT[:, k * SEQ:(k + 1) * SEQ], Brep[:, :]))
                        em.o("vector", lambda e: e.tensor_tensor_scan(
                            hb[:, :], dAb[:, :], ub[:, :], zero1[:, 0:1],
                            ALU.mult, ALU.add))
                        em.o("vector", lambda e: e.tensor_mul(qscan[:, :], hb[:, :], Crep[:, :]))
                        pp, qq = (n % 2), ((n + 1) % 2)
                        em.o("vector", lambda e, k=k, pp=pp, qq=qq: e.tensor_add(
                            yb[:, (k * 2 + qq) * SEQ:(k * 2 + qq + 1) * SEQ],
                            yb[:, (k * 2 + pp) * SEQ:(k * 2 + pp + 1) * SEQ], qscan[:, :]),
                            inc=sV)
                for k in range(2):
                    em.o("vector", lambda e, k=k: e.tensor_copy(
                        ybf[:, :], yb[:, (k * 2) * SEQ:(k * 2 + 1) * SEQ]), inc=sV)
                    if "y" in dbg_outs:
                        em.w("sync", sV)
                        em.dma("sync", lambda e, k=k, s=s_seq: e.dma_start(
                            out=dbg_outs["y"][s * DSL + k * 128: s * DSL + (k + 1) * 128, :],
                            in_=yb[:, (k * 2) * SEQ:(k * 2 + 1) * SEQ]), qS)
                    em.w("sync", sV)
                    em.dma("sync", lambda e, k=k, s=s_seq: e.dma_start(
                        out=e2_in[s].ap().rearrange("(j m) n -> j m n", m=256)
                        [:, k * 128:(k + 1) * 128, :].rearrange("j m n -> m j n"),
                        in_=ybf[:, :].rearrange("p (j n) -> p j n", j=8)), qS)
                    em.w("vector", qS)
                em.w("gpsimd", qS)
                with allow_ici_all_to_all():
                    em.w("gpsimd", cc)
                    em.o("gpsimd", lambda e, s=s_seq: e.collective_compute(
                        "AllToAll", ALU.bypass, replica_groups=[list(range(NCORE))],
                        ins=[e2_in[s].ap()], outs=[e2_out[s].ap()]), inc=cc)
            em.barrier(BAR)

    if "C" in phases:
        cc_e2 = cc.n
        with ExitStack() as pk:
            A = lambda n, s, d: pk.enter_context(nc.sbuf_tensor(n, s, d))
            ypx_g = A("ypx_g", [128, NDB * 512], BF16)
            lxc = A("lxc", [128, DM], F32)
            ypxT_b = A("ypxT_b", [128, NDB * 512], BF16)
            opw_b = A("opw_b", [128, NDB * 512], BF16)
            mo_all = A("mo_all", [128, 4 * DM], F32)
            xc = A("xc", [128, DM], F32)
            sqc = A("sqc", [128, DM], F32)
            monl = A("monl", [128, DM], F32)
            x1c = A("x1c", [128, DM], F32)

            for s_seq in range(NSEQ):
                em.w("sync", cc, cc_e2)
                em.w("sync", sT)
                em.w("sync", sV)
                em.dma("sync", lambda e, s=s_seq: e.dma_start(
                    out=ypx_g[:, :].rearrange("p (b n) -> p b n", b=NDB),
                    in_=e2_out[s].ap().rearrange("(b p) n -> p b n", p=128)), qS)
                for lb in range(4):
                    em.w("sync", sT)
                    em.dma("sync", lambda e, s=s_seq, lb=lb: e.dma_start(
                        out=lxc[:, :],
                        in_=sp_lx.ap()[s * 512 + lb * 128: s * 512 + (lb + 1) * 128, :]), qS)
                    em.w("tensor", qS)
                    em.w("vector", qS)
                    for db in range(NDB):
                        pst = ptA if db % 2 == 0 else ptB
                        mm_guard(pst)
                        em.o("tensor", lambda e, db=db, pst=pst: e.transpose(
                            pst[:, :], lxc[:, db * 128:(db + 1) * 128], ident_f[:, :]),
                            inc=sT)
                        em.w("vector", sT)
                        em.o("vector", lambda e, db=db, lb=lb, pst=pst: e.scalar_tensor_tensor(
                            ypxT_b[:, db * 512 + lb * 128: db * 512 + (lb + 1) * 128],
                            pst[:, :], Dcol_sb[:, db:db + 1],
                            ypx_g[:, db * 512 + lb * 128: db * 512 + (lb + 1) * 128],
                            ALU.mult, ALU.add), inc=sV)
                        consume(pst, sV)
                for dch in range(4):
                    em.w("gpsimd", sT)
                    em.dma("gpsimd", lambda e, dch=dch: e.dma_start(
                        out=opw_b[:, :].rearrange("p (b n) -> p b n", b=NDB),
                        in_=opwT[:, dch * 512:(dch + 1) * 512]
                        .rearrange("(b p) n -> p b n", p=128)), qG)
                    em.w("tensor", qG)
                    em.w("tensor", sV)
                    for lb in range(4):
                        pst = ps[lb]
                        mm_guard(pst)
                        for db in range(NDB):
                            em.o("tensor", lambda e, db=db, lb=lb, pst=pst: e.matmul(
                                pst[:, :],
                                ypxT_b[:, db * 512 + lb * 128: db * 512 + (lb + 1) * 128],
                                opw_b[:, db * 512:(db + 1) * 512],
                                start=(db == 0), stop=(db == NDB - 1)),
                                inc=(sT if db == NDB - 1 else None))
                    em.w("scalar", sT)
                    for lb in range(4):
                        em.o("scalar", lambda e, lb=lb, dch=dch: e.copy(
                            mo_all[:, lb * DM + dch * 512: lb * DM + (dch + 1) * 512],
                            ps[lb][:, :]), inc=(sC if lb == 3 else None))
                    for lb in range(4):
                        consume(ps[lb], sC)
                for lb in range(4):
                    em.w("sync", sV)
                    em.dma("sync", lambda e, s=s_seq, lb=lb: e.dma_start(
                        out=xc[:, :],
                        in_=x_own[s * 512 + lb * 128: s * 512 + (lb + 1) * 128, :]), qS)
                    em.w("vector", qS)
                    em.w("vector", sC)
                    mot = mo_all[:, lb * DM:(lb + 1) * DM]
                    ln_stats(mot, DM, 12, EPS_LN, sqc)
                    em.o("vector", lambda e, mot=mot: e.tensor_scalar(
                        monl[:, :], mot[:, :], st[:, 14:15], st[:, 17:18],
                        ALU.subtract, ALU.mult))
                    em.o("vector", lambda e: e.tensor_add(x1c[:, :], xc[:, :], monl[:, :]),
                         inc=sV)
                    em.w("sync", sV)
                    em.dma("sync", lambda e, s=s_seq, lb=lb: e.dma_start(
                        out=sp_x1.ap()[s * 512 + lb * 128: s * 512 + (lb + 1) * 128, :],
                        in_=x1c[:, :]), qS)
                    if "x1" in dbg_outs:
                        em.dma("sync", lambda e, s=s_seq, lb=lb: e.dma_start(
                            out=dbg_outs["x1"][s * 512 + lb * 128: s * 512 + (lb + 1) * 128, :],
                            in_=x1c[:, :]), qS)
            em.barrier(BAR)

    if "D" in phases:
        for s_seq in range(NSEQ):
            with ExitStack() as sk:
                with ExitStack() as pk:
                    A = lambda n, s, d: pk.enter_context(nc.sbuf_tensor(n + str(s_seq), s, d))
                    xd = A("xd", [128, DM], F32)
                    sqd = A("sqd", [128, DM], F32)
                    hb_t = A("hb_t", [128, DM], BF16)
                    l1_rep = A("l1_rep", [128, DM], F32)
                    stg = A("stg", [128, NDB * 128], BF16)
                    wq_b = A("wq_b", [128, NDB * 256], BF16)
                    wk_b = A("wk_b", [128, NDB * 256], BF16)
                    wv_b = A("wv_b", [128, NDB * 256], BF16)
                    wo_b = A("wo_b", [128, 2 * DM], BF16)
                    hTb = A("hTb", [128, NDB * 512], BF16)
                    qT_sb = A("qT_sb", [128, 2 * SEQ], BF16)
                    kT_sb = A("kT_sb", [128, 2 * SEQ], BF16)
                    vtmp = A("vtmp", [128, 512], BF16)
                    v_sb = A("v_sb", [128, 32 * 256], BF16)
                    sc_sb = A("sc_sb", [128, SEQ], F32)
                    p2_sb = A("p2_sb", [128, SEQ], BF16)
                    p2b_sb = A("p2b_sb", [128, SEQ], BF16)
                    pT_sb = A("pT_sb", [128, 128], BF16)
                    aoT_sb = A("aoT_sb", [128, 2 * SEQ], BF16)
                    arstg = A("arstg", [128, DM], BF16)
                    ars_sb = A("ars_sb", [128, DM], BF16)
                    mask_sb = A("mask_sb", [128, 4 * 512], BF16)
                    lx1t = A("lx1t", [128, DM], F32)
                    x2it = A("x2it", [128, DM], F32)

                    import ml_dtypes as _mld
                    mrow = np.zeros((4, 128, 512), np.float32)
                    for qo in range(4):
                        for i in range(128):
                            mrow[qo, i, qo * 128 + i + 1:] = -1e9
                    mask_dram = nc.inline_tensor(
                        mrow.transpose(1, 0, 2).reshape(128, 4 * 512)
                        .astype(_mld.bfloat16), f"mask{s_seq}")
                    em.dma("sync", lambda e: e.dma_start(
                        out=mask_sb[:, :], in_=mask_dram.ap()), qS)
                    em.dma("sync", lambda e: e.dma_start(
                        out=l1_rep[:, :], in_=l1row.broadcast_to([128, DM])), qS)
                    em.dma("gpsimd", lambda e: e.dma_start(
                        out=wq_b[:, :].rearrange("p (b n) -> p b n", b=NDB),
                        in_=wq_own.rearrange("(b p) n -> p b n", p=128)), qG)
                    em.dma("gpsimd", lambda e: e.dma_start(
                        out=wk_b[:, :].rearrange("p (b n) -> p b n", b=NDB),
                        in_=wk_own.rearrange("(b p) n -> p b n", p=128)), qG)
                    em.dma("gpsimd", lambda e: e.dma_start(
                        out=wv_b[:, :].rearrange("p (b n) -> p b n", b=NDB),
                        in_=wv_own.rearrange("(b p) n -> p b n", p=128)), qG)
                    em.dma("gpsimd", lambda e: e.dma_start(
                        out=wo_b[:, :].rearrange("p (b n) -> p b n", b=2),
                        in_=wo_own.rearrange("(b p) n -> p b n", p=128)), qG)

                    for lb in range(4):
                        em.w("sync", sV)
                        em.w("sync", sC)
                        em.dma("sync", lambda e, s=s_seq, lb=lb: e.dma_start(
                            out=xd[:, :],
                            in_=sp_x1.ap()[s * 512 + lb * 128: s * 512 + (lb + 1) * 128, :]), qS)
                        em.w("vector", qS)
                        em.w("vector", sT)
                        em.w("scalar", qS)
                        ln_stats(xd, DM, 18, EPS_LN, sqd)
                        em.o("vector", lambda e: e.tensor_scalar(
                            lx1t[:, :], xd[:, :], st[:, 20:21], st[:, 23:24],
                            ALU.subtract, ALU.mult), inc=sV)
                        em.w("sync", sV)
                        em.dma("sync", lambda e, s=s_seq, lb=lb: e.dma_start(
                            out=sp_lx1.ap()[s * 512 + lb * 128: s * 512 + (lb + 1) * 128, :],
                            in_=lx1t[:, :]), qS)
                        rms_stats(lx1t, DM, 24, EPS_RMS, sqd)
                        em.o("vector", lambda e: e.tensor_scalar(
                            sqd[:, :], lx1t[:, :], st[:, 25:26], None, ALU.mult))
                        em.o("vector", lambda e: e.tensor_mul(xd[:, :], sqd[:, :], l1_rep[:, :]))
                        em.o("vector", lambda e: e.tensor_copy(hb_t[:, :], xd[:, :]), inc=sV)
                        if "hln" in dbg_outs:
                            em.w("sync", sV)
                            em.dma("sync", lambda e, s=s_seq, lb=lb: e.dma_start(
                                out=dbg_outs["hln"][s * 512 + lb * 128: s * 512 + (lb + 1) * 128, :],
                                in_=xd[:, :]), qS)
                        em.w("tensor", sV)
                        em.w("scalar", qS)
                        if s_seq == 0 and lb == 0 and "hbt" in dbg_outs:
                            em.w("vector", sV)
                            em.o("vector", lambda e: e.tensor_copy(sqd[:, :], hb_t[:, :]),
                                 inc=sV)
                            em.w("sync", sV)
                            em.dma("sync", lambda e: e.dma_start(
                                out=dbg_outs["hbt"], in_=sqd[:, :]), qS)
                            em.w("vector", qS)
                        for db in range(NDB):
                            mm_guard(ptbf)
                            em.o("tensor", lambda e, db=db: e.transpose(
                                ptbf[:, :], hb_t[:, db * 128:(db + 1) * 128], ident_b[:, :]),
                                inc=sT)
                            em.w("scalar", sT)
                            em.o("scalar", lambda e, db=db: e.copy(
                                stg[:, db * 128:(db + 1) * 128], ptbf[:, :]), inc=sC)
                            consume(ptbf, sC)
                        if s_seq == 0 and lb == 0 and "stg" in dbg_outs:
                            em.w("vector", sC)
                            em.o("vector", lambda e: e.tensor_copy(sqd[:, :], stg[:, :]),
                                 inc=sV)
                            em.w("sync", sV)
                            em.dma("sync", lambda e: e.dma_start(
                                out=dbg_outs["stg"], in_=sqd[:, :]), qS)
                            em.w("vector", qS)
                            em.w("scalar", qS)
                        em.w("sync", sC)
                        em.w("sync", qS)
                        em.dma("sync", lambda e, s=s_seq, lb=lb: e.dma_start(
                            out=ht_in[s].ap().rearrange("(b p) n -> p b n", p=128)
                            [:, :, lb * 128:(lb + 1) * 128],
                            in_=stg[:, :].rearrange("p (b n) -> p b n", b=NDB)), qS)
                        em.w("sync", qS)
                    if s_seq == 0 and "hti" in dbg_outs:
                        em.w("sync", qS)
                        em.dma("sync", lambda e: e.dma_start(
                            out=hTb[:, 0:512].rearrange("p n -> p n"),
                            in_=ht_in[0].ap()[0:128, :]), qS)
                        em.w("sync", qS)
                        for j in range(1):
                            em.w("vector", qS)
                            em.o("vector", lambda e: e.tensor_copy(
                                sqd[:, 0:512], hTb[:, 0:512]), inc=sV)
                            em.w("sync", sV)
                            em.dma("sync", lambda e: e.dma_start(
                                out=dbg_outs["hti"], in_=sqd[:, 0:512]), qS)
                            em.w("vector", qS)
                    em.w("gpsimd", qS)
                    em.w("gpsimd", cc)
                    em.o("gpsimd", lambda e, s=s_seq: e.collective_compute(
                        "AllGather", ALU.bypass, replica_groups=[list(range(NCORE))],
                        ins=[ht_in[s].ap()], outs=[ht_out[s].ap()]), inc=cc)
                    cc_ht = cc.n

                    hto = ht_out[s_seq].ap().rearrange("(r b p) n -> r p b n", p=128, b=NDB)
                    for r in range(NCORE):
                        em.w("sync", cc, cc_ht)
                        em.w("sync", sT)
                        em.dma("sync", lambda e, r=r, hto=hto: e.dma_start(
                            out=hTb[:, :].rearrange("p (b n) -> p b n", b=NDB),
                            in_=hto[r]), qS)
                        if r == 0 and s_seq == 0 and "htb" in dbg_outs:
                            for j in range(4):
                                em.w("vector", qS)
                                em.o("vector", lambda e, j=j: e.tensor_copy(
                                    sqd[:, :], hTb[:, j * 2048:(j + 1) * 2048]), inc=sV)
                                em.w("sync", sV)
                                em.dma("sync", lambda e, j=j: e.dma_start(
                                    out=dbg_outs["htb"][:, j * 2048:(j + 1) * 2048],
                                    in_=sqd[:, :]), qS)
                                em.w("vector", qS)
                        em.w("tensor", qS)
                        em.w("tensor", qG)
                        for ti, (wt, dst, scale) in enumerate(
                                ((wq_b, qT_sb, QSCALE), (wk_b, kT_sb, 1.0), (wv_b, None, 1.0))):
                            for h2 in range(2):
                                pst = ps[(ti * 2 + h2) % 4]
                                mm_guard(pst)
                                for db in range(NDB):
                                    em.o("tensor", lambda e, db=db, h2=h2, wt=wt, pst=pst: e.matmul(
                                        pst[:, :],
                                        wt[:, db * 256 + h2 * 128: db * 256 + (h2 + 1) * 128],
                                        hTb[:, db * 512:(db + 1) * 512],
                                        start=(db == 0), stop=(db == NDB - 1)),
                                        inc=(sT if db == NDB - 1 else None))
                                em.w("scalar", sT)
                                if ti < 2:
                                    em.o("scalar", lambda e, h2=h2, r=r, dst=dst, scale=scale, pst=pst:
                                         e.activation(
                                             dst[:, h2 * SEQ + r * 512: h2 * SEQ + (r + 1) * 512],
                                             pst[:, :], AF.Copy,
                                             scale=(kcol[:, 2:3] if scale != 1.0 else 1.0)), inc=sC)
                                    consume(pst, sC)
                                else:
                                    em.w("scalar", sT)
                                    em.w("scalar", sT)
                                    em.o("scalar", lambda e, pst=pst: e.copy(
                                        vtmp[:, :], pst[:, :]), inc=sC)
                                    consume(pst, sC)
                                    em.w("tensor", sC)
                                    for j in range(4):
                                        mm_guard(ptbf)
                                        em.o("tensor", lambda e, j=j: e.transpose(
                                            ptbf[:, :], vtmp[:, j * 128:(j + 1) * 128],
                                            ident_b[:, :]), inc=sT)
                                        em.w("scalar", sT)
                                        em.o("scalar", lambda e, r=r, j=j, h2=h2: e.copy(
                                            v_sb[:, (r * 4 + j) * 256 + h2 * 128:
                                                 (r * 4 + j) * 256 + (h2 + 1) * 128],
                                            ptbf[:, :]), inc=sC)
                                        consume(ptbf, sC)

                    em.w("tensor", sC)
                    for h2 in range(2):
                        for qb in range(32):
                            kvlen = (qb + 1) * 128
                            nch = (kvlen + 511) // 512
                            em.w("tensor", sV)
                            for ch in range(nch):
                                cl = min(512, kvlen - ch * 512)
                                pst = ps[ch % 2]
                                mm_guard(pst)
                                em.o("tensor", lambda e, h2=h2, qb=qb, ch=ch, cl=cl, pst=pst:
                                     e.matmul(
                                         pst[:, 0:cl],
                                         qT_sb[:, h2 * SEQ + qb * 128:h2 * SEQ + (qb + 1) * 128],
                                         kT_sb[:, h2 * SEQ + ch * 512: h2 * SEQ + ch * 512 + cl],
                                         start=True, stop=True), inc=sT)
                                em.w("vector", sT)
                                is_diag = (ch == qb // 4)
                                if is_diag:
                                    qo = qb % 4
                                    em.o("vector", lambda e, ch=ch, cl=cl, qo=qo, pst=pst:
                                         e.tensor_add(
                                             sc_sb[:, ch * 512:ch * 512 + cl], pst[:, 0:cl],
                                             mask_sb[:, qo * 512:qo * 512 + cl]), inc=sV)
                                else:
                                    em.o("vector", lambda e, ch=ch, cl=cl, pst=pst:
                                         e.tensor_copy(
                                             sc_sb[:, ch * 512:ch * 512 + cl], pst[:, 0:cl]),
                                         inc=sV)
                                consume(pst, sV)
                            em.o("vector", lambda e, kvlen=kvlen: e.reduce_max(
                                st[:, 44:45], sc_sb[:, 0:kvlen], axis=AX, negate=True),
                                inc=sV)
                            em.w("scalar", sV)
                            em.o("scalar", lambda e, kvlen=kvlen: e.activation(
                                p2_sb[:, 0:kvlen], sc_sb[:, 0:kvlen], AF.Exp,
                                bias=st[:, 44:45], accum_out=st[:, 45:46]), inc=sC)
                            em.w("vector", sC)
                            em.o("vector", lambda e: e.reciprocal(st[:, 46:47], st[:, 45:46]),
                                 inc=sV)
                            em.w("vector", sV)
                            em.o("vector", lambda e, kvlen=kvlen: e.tensor_scalar(
                                p2b_sb[:, 0:kvlen], p2_sb[:, 0:kvlen], st[:, 46:47], None,
                                ALU.mult), inc=sV)
                            em.w("tensor", sV)
                            pao = ps[2]
                            mm_guard(pao)
                            for kb in range(qb + 1):
                                mm_guard(ptbf)
                                em.o("tensor", lambda e, kb=kb: e.transpose(
                                    ptbf[:, :], p2b_sb[:, kb * 128:(kb + 1) * 128],
                                    ident_b[:, :]), inc=sT)
                                em.w("scalar", sT)
                                em.o("scalar", lambda e: e.copy(pT_sb[:, :], ptbf[:, :]),
                                     inc=sC)
                                consume(ptbf, sC)
                                em.w("tensor", sC)
                                em.o("tensor", lambda e, kb=kb, h2=h2, qb=qb, pao=pao: e.matmul(
                                    pao[:, 0:128],
                                    v_sb[:, kb * 256 + h2 * 128: kb * 256 + (h2 + 1) * 128],
                                    pT_sb[:, :],
                                    start=(kb == 0), stop=(kb == qb)),
                                    inc=(sT if kb == qb else None))
                            em.w("scalar", sT)
                            em.o("scalar", lambda e, h2=h2, qb=qb, pao=pao: e.copy(
                                aoT_sb[:, h2 * SEQ + qb * 128: h2 * SEQ + (qb + 1) * 128],
                                pao[:, 0:128]), inc=sC)
                            consume(pao, sC)

                    if s_seq == 0:
                        for nm, buf in (("qt", qT_sb), ("kt", kT_sb), ("vv", v_sb),
                                        ("aot", aoT_sb)):
                            if nm in dbg_outs:
                                for j in range(4):
                                    em.w("vector", sC)
                                    em.o("vector", lambda e, buf=buf, j=j: e.tensor_copy(
                                        sqd[:, :], buf[:, j * 2048:(j + 1) * 2048]),
                                        inc=sV)
                                    em.w("sync", sV)
                                    em.dma("sync", lambda e, nm=nm, j=j: e.dma_start(
                                        out=dbg_outs[nm][:, j * 2048:(j + 1) * 2048],
                                        in_=sqd[:, :]), qS)
                                    em.w("vector", qS)
                    em.w("tensor", sC)
                    for qb in range(32):
                        for dch in range(4):
                            pst = ps[dch % 2]
                            mm_guard(pst)
                            for h2 in range(2):
                                em.o("tensor", lambda e, qb=qb, dch=dch, h2=h2, pst=pst:
                                     e.matmul(
                                         pst[:, :],
                                         aoT_sb[:, h2 * SEQ + qb * 128: h2 * SEQ + (qb + 1) * 128],
                                         wo_b[:, h2 * DM + dch * 512: h2 * DM + (dch + 1) * 512],
                                         start=(h2 == 0), stop=(h2 == 1)),
                                     inc=(sT if h2 == 1 else None))
                            em.w("scalar", sT)
                            em.w("scalar", qS)
                            em.o("scalar", lambda e, dch=dch, pst=pst: e.copy(
                                arstg[:, dch * 512:(dch + 1) * 512], pst[:, :]),
                                inc=sC)
                            consume(pst, sC)
                        em.w("sync", sC)
                        em.dma("sync", lambda e, s=s_seq, qb=qb: e.dma_start(
                            out=ars_in[s].ap()[qb * 128:(qb + 1) * 128, :],
                            in_=arstg[:, :]), qS)
                    em.w("gpsimd", qS)
                    em.w("gpsimd", cc)
                    em.o("gpsimd", lambda e, s=s_seq: e.collective_compute(
                        "ReduceScatter", ALU.add, replica_groups=[list(range(NCORE))],
                        ins=[ars_in[s].ap()], outs=[ars_out[s].ap()]), inc=cc)
                    cc_ars = cc.n
                    for lb in range(4):
                        em.w("sync", cc, cc_ars)
                        em.w("sync", sV)
                        em.dma("sync", lambda e, s=s_seq, lb=lb: e.dma_start(
                            out=ars_sb[:, :],
                            in_=ars_out[s].ap()[lb * 128:(lb + 1) * 128, :]), qS)
                        em.dma("sync", lambda e, s=s_seq, lb=lb: e.dma_start(
                            out=lx1t[:, :],
                            in_=sp_lx1.ap()[s * 512 + lb * 128: s * 512 + (lb + 1) * 128, :]), qS)
                        em.w("vector", qS)
                        em.o("vector", lambda e, lb=lb: e.tensor_add(
                            x2it[:, :], lx1t[:, :], ars_sb[:, :]), inc=sV)
                        em.w("sync", sV)
                        em.dma("sync", lambda e, s=s_seq, lb=lb: e.dma_start(
                            out=sp_x2i.ap()[s * 512 + lb * 128: s * 512 + (lb + 1) * 128, :],
                            in_=x2it[:, :]), qS)
                        if "x2i" in dbg_outs:
                            em.dma("sync", lambda e, s=s_seq, lb=lb: e.dma_start(
                                out=dbg_outs["x2i"][s * 512 + lb * 128: s * 512 + (lb + 1) * 128, :],
                                in_=x2it[:, :]), qS)
                    em.barrier(BAR)
                if "E" in phases:
                    _phase_E(nc, em, BAR, s_seq, sp_x2i, dict(
                        l2row=l2row, gwT=gwT, egT=egT, euT=euT, edT=edT,
                        sgT=sgT, suT=suT, sdT=sdT, kcol=kcol,
                        tt_in=tt_in, tt_out=tt_out, cw_in=cw_in, cw_out=cw_out,
                        mrs_in=mrs_in, mrs_out=mrs_out, sp_x1=sp_x1,
                        out_own=out_own, dbg_outs=dbg_outs,
                        ident_f=ident_f, ident_b=ident_b, st=st,
                        ps=ps, ptA=ptA, ptB=ptB, ptbf=ptbf,
                        qS=qS, qG=qG, cc=cc, sT=sT, sV=sV, sC=sC,
                        mm_guard=mm_guard, consume=consume,
                        ln_stats=ln_stats, rms_stats=rms_stats))
                em.barrier(BAR)

    em.w("sync", qG)
    em.w("sync", cc)
    em.w("sync", qS)

    with allow_ici_all_to_all():
        with nc.Block() as blk:
            for name in ENGS:
                getattr(blk, name)(lambda eng, ops=tuple(em.q[name]): [f(eng) for f in ops])
    stk.close()
    return nc, dbg_outs


def _phase_E(nc, em, BAR, s_seq, sp_x2i, z):
    qS, qG, cc, sT, sV, sC = z["qS"], z["qG"], z["cc"], z["sT"], z["sV"], z["sC"]
    mm_guard, consume = z["mm_guard"], z["consume"]
    ps, ptA, ptB, ptbf = z["ps"], z["ptA"], z["ptB"], z["ptbf"]
    st = z["st"]
    ident_f, ident_b = z["ident_f"], z["ident_b"]
    dbg_outs = z["dbg_outs"]
    ln_stats, rms_stats = z["ln_stats"], z["rms_stats"]

    with ExitStack() as pk:
        A = lambda n, s, d: pk.enter_context(nc.sbuf_tensor(n + "E" + str(s_seq), s, d))
        l2_rep = A("l2_rep", [128, DM], F32)
        x2e = A("x2e", [128, DM], F32)
        te = A("te", [128, DM], F32)
        tb = A("tb", [128, DM], BF16)
        sqe = A("sqe", [128, DM], F32)
        tT_b = A("tT_b", [128, NDB * 512], BF16)
        gw_b = A("gw_b", [128, NDB * NE], BF16)
        glog = A("glog", [128, NE], F32)
        mx8 = A("mx8", [128, 8], F32)
        ind = A("ind", [128, NE], F32)
        wsel = A("wsel", [128, NE], F32)
        cwt = A("cwt", [128, NE], F32)
        cwT_sb = A("cwT_sb", [128, OWN], F32)

        em.dma("sync", lambda e: e.dma_start(
            out=l2_rep[:, :], in_=z["l2row"].broadcast_to([128, DM])), qS)
        em.dma("gpsimd", lambda e: e.dma_start(
            out=gw_b[:, :].rearrange("p (b n) -> p b n", b=NDB),
            in_=z["gwT"].rearrange("(b p) n -> p b n", p=128)), qG)

        for lb in range(4):
            em.w("sync", sV)
            em.w("sync", sC)
            em.dma("sync", lambda e, s=s_seq, lb=lb: e.dma_start(
                out=x2e[:, :],
                in_=sp_x2i.ap()[s * 512 + lb * 128: s * 512 + (lb + 1) * 128, :]), qS)
            em.w("vector", qS)
            em.w("scalar", qS)
            em.w("vector", sT)
            rms_stats(x2e, DM, 28, EPS_RMS, sqe)
            em.o("vector", lambda e: e.tensor_scalar(
                sqe[:, :], x2e[:, :], st[:, 29:30], None, ALU.mult))
            em.o("vector", lambda e: e.tensor_mul(te[:, :], sqe[:, :], l2_rep[:, :]))
            em.o("vector", lambda e: e.tensor_copy(tb[:, :], te[:, :]), inc=sV)
            em.w("tensor", sV)
            for db in range(NDB):
                mm_guard(ptbf)
                em.o("tensor", lambda e, db=db: e.transpose(
                    ptbf[:, :], tb[:, db * 128:(db + 1) * 128], ident_b[:, :]), inc=sT)
                em.w("scalar", sT)
                em.o("scalar", lambda e, db=db, lb=lb: e.copy(
                    tT_b[:, db * 512 + lb * 128: db * 512 + (lb + 1) * 128],
                    ptbf[:, :]), inc=sC)
                consume(ptbf, sC)
            em.w("tensor", sC)
            em.w("tensor", qG)
            mm_guard(ptA)
            for db in range(NDB):
                em.o("tensor", lambda e, db=db, lb=lb: e.matmul(
                    ptA[:, 0:NE],
                    tT_b[:, db * 512 + lb * 128: db * 512 + (lb + 1) * 128],
                    gw_b[:, db * NE:(db + 1) * NE],
                    start=(db == 0), stop=(db == NDB - 1)),
                    inc=(sT if db == NDB - 1 else None))
            em.w("vector", sT)
            em.o("vector", lambda e: e.tensor_copy(glog[:, :], ptA[:, 0:NE]), inc=sV)
            consume(ptA, sV)
            em.w("vector", sV)
            if "gl" in dbg_outs:
                em.w("sync", sV)
                em.dma("sync", lambda e, s=s_seq, lb=lb: e.dma_start(
                    out=dbg_outs["gl"][s * 512 + lb * 128: s * 512 + (lb + 1) * 128, :],
                    in_=glog[:, :]), qS)
            em.o("vector", lambda e: e.max(mx8[:, :], glog[:, :]), inc=sV)
            em.w("vector", sV)
            em.o("vector", lambda e: e.tensor_scalar(
                st[:, 40:41], mx8[:, 0:1], -1.0, None, ALU.mult), inc=sV)
            em.w("vector", sV)
            em.o("vector", lambda e: e.tensor_scalar(
                ind[:, :], glog[:, :], mx8[:, 5:6], None, ALU.is_ge), inc=sV)
            em.w("scalar", sV)
            em.o("scalar", lambda e: e.activation(
                wsel[:, :], glog[:, :], AF.Exp, bias=st[:, 40:41]), inc=sC)
            em.w("vector", sC)
            em.o("vector", lambda e: e.tensor_mul(glog[:, :], wsel[:, :], ind[:, :]),
                 inc=sV)
            em.w("vector", sV)
            em.o("vector", lambda e: e.reduce_sum(st[:, 41:42], glog[:, :], axis=AX),
                 inc=sV)
            em.w("vector", sV)
            em.o("vector", lambda e: e.reciprocal(st[:, 42:43], st[:, 41:42]), inc=sV)
            em.w("vector", sV)
            em.o("vector", lambda e: e.tensor_scalar(
                cwt[:, :], glog[:, :], st[:, 42:43], None, ALU.mult), inc=sV)
            if "cw" in dbg_outs:
                em.w("sync", sV)
                em.dma("sync", lambda e, s=s_seq, lb=lb: e.dma_start(
                    out=dbg_outs["cw"][s * 512 + lb * 128: s * 512 + (lb + 1) * 128, :],
                    in_=cwt[:, :]), qS)
            em.w("tensor", sV)
            mm_guard(ptB)
            em.o("tensor", lambda e: e.transpose(
                ptB[0:NE, :], cwt[:, :], ident_f[:, :]), inc=sT)
            em.w("vector", sT)
            em.o("vector", lambda e, lb=lb: e.tensor_copy(
                cwT_sb[0:NE, lb * 128:(lb + 1) * 128], ptB[0:NE, 0:128]), inc=sV)
            consume(ptB, sV)
        em.w("sync", sC)
        em.w("sync", sV)
        em.dma("sync", lambda e, s=s_seq: e.dma_start(
            out=z["tt_in"][s].ap().rearrange("(b p) n -> p b n", p=128),
            in_=tT_b[:, :].rearrange("p (b n) -> p b n", b=NDB)), qS)
        em.dma("sync", lambda e, s=s_seq: e.dma_start(
            out=z["cw_in"][s].ap(), in_=cwT_sb[0:NE, :]), qS)
        em.w("gpsimd", qS)
        em.w("gpsimd", cc)
        em.o("gpsimd", lambda e, s=s_seq: e.collective_compute(
            "AllGather", ALU.bypass, replica_groups=[list(range(NCORE))],
            ins=[z["tt_in"][s].ap()], outs=[z["tt_out"][s].ap()]), inc=cc)
        with allow_ici_all_to_all():
            em.w("gpsimd", cc)
            em.o("gpsimd", lambda e, s=s_seq: e.collective_compute(
                "AllToAll", ALU.bypass, replica_groups=[list(range(NCORE))],
                ins=[z["cw_in"][s].ap()], outs=[z["cw_out"][s].ap()]), inc=cc)
        cc_moe = cc.n

        with ExitStack() as rk:
            B = lambda n, s, d: rk.enter_context(nc.sbuf_tensor(n + "R" + str(s_seq), s, d))
            tT_all = B("tT_all", [128, NDB * 1024], BF16)
            hh_w = B("hh_w", [128, 2 * NIB * 1024], BF16)
            eg_b = B("eg_b", [128, NDB * 128], BF16)
            eu_b = B("eu_b", [128, NDB * 128], BF16)
            ed_b = B("ed_b", [128, 2 * EI], BF16)
            cwrep = B("cwrep", [128, 1024], F32)
            sil_b = B("sil_b", [128, 512], BF16)
            uw_b = B("uw_b", [128, 512], BF16)
            mstg = B("mstg", [128, 1024], BF16)

            tto = z["tt_out"][s_seq].ap().rearrange("(r b p) n -> p b r n", p=128, b=NDB)
            cwo = z["cw_out"][s_seq].ap().rearrange("(r e) n -> e r n", e=2)
            for qt in range(4):
                em.w("sync", cc, cc_moe)
                em.w("sync", sT)
                for rr in range(2):
                    em.dma("sync", lambda e, qt=qt, rr=rr: e.dma_start(
                        out=tT_all[:, :].rearrange("p (b m) -> p b m", b=NDB)
                        [:, :, rr * 512:(rr + 1) * 512],
                        in_=tto[:, :, 2 * qt + rr, :]), qS)
                for e2 in range(2):
                    em.w("sync", sV)
                    em.dma("sync", lambda e, e2=e2, qt=qt: e.dma_start(
                        out=cwrep[:, :].rearrange("p (r n) -> p r n", r=2),
                        in_=cwo[e2:e2 + 1, 2 * qt:2 * qt + 2, :]
                        .broadcast_to([128, 2, 512])), qS)
                    for ib in range(NIB):
                        em.w("gpsimd", sT)
                        em.dma("gpsimd", lambda e, e2=e2, ib=ib: e.dma_start(
                            out=eg_b[:, :].rearrange("p (b n) -> p b n", b=NDB),
                            in_=z["egT"][e2:e2 + 1, :, ib * 128:(ib + 1) * 128]
                            .rearrange("o (b p) n -> p b (o n)", p=128)), qG)
                        em.dma("gpsimd", lambda e, e2=e2, ib=ib: e.dma_start(
                            out=eu_b[:, :].rearrange("p (b n) -> p b n", b=NDB),
                            in_=z["euT"][e2:e2 + 1, :, ib * 128:(ib + 1) * 128]
                            .rearrange("o (b p) n -> p b (o n)", p=128)), qG)
                        em.w("tensor", qG)
                        em.w("tensor", qS)
                        for lch in range(2):
                            pg, pu = ps[0], ps[1]
                            mm_guard(pg)
                            for db in range(NDB):
                                em.o("tensor", lambda e, db=db, lch=lch, pg=pg: e.matmul(
                                    pg[:, :], eg_b[:, db * 128:(db + 1) * 128],
                                    tT_all[:, db * 1024 + lch * 512: db * 1024 + (lch + 1) * 512],
                                    start=(db == 0), stop=(db == NDB - 1)),
                                    inc=(sT if db == NDB - 1 else None))
                            mm_guard(pu)
                            for db in range(NDB):
                                em.o("tensor", lambda e, db=db, lch=lch, pu=pu: e.matmul(
                                    pu[:, :], eu_b[:, db * 128:(db + 1) * 128],
                                    tT_all[:, db * 1024 + lch * 512: db * 1024 + (lch + 1) * 512],
                                    start=(db == 0), stop=(db == NDB - 1)),
                                    inc=(sT if db == NDB - 1 else None))
                            em.w("scalar", sT)
                            em.w("scalar", sV)
                            em.o("scalar", lambda e, pg=pg: e.activation(
                                sil_b[:, :], pg[:, :], AF.Silu), inc=sC)
                            consume(pg, sC)
                            em.w("vector", sT)
                            em.w("vector", sC)
                            em.o("vector", lambda e, lch=lch, pu=pu: e.scalar_tensor_tensor(
                                uw_b[:, :], pu[:, :], 1.0,
                                cwrep[:, lch * 512:(lch + 1) * 512],
                                ALU.mult, ALU.mult), inc=sV)
                            consume(pu, sV)
                            em.o("vector", lambda e, e2=e2, ib=ib, lch=lch: e.tensor_mul(
                                hh_w[:, (e2 * NIB + ib) * 1024 + lch * 512:
                                     (e2 * NIB + ib) * 1024 + (lch + 1) * 512],
                                uw_b[:, :], sil_b[:, :]), inc=sV)
                for dob in range(NDB):
                    em.w("gpsimd", sT)
                    for ee in range(2):
                        em.dma("gpsimd", lambda e, dob=dob, ee=ee: e.dma_start(
                            out=ed_b[:, ee * EI:(ee + 1) * EI]
                            .rearrange("p (b n) -> p b n", b=NIB),
                            in_=z["edT"][ee:ee + 1, :, dob * 128:(dob + 1) * 128]
                            .rearrange("o (b p) n -> p b (o n)", p=128)), qG)
                    em.w("tensor", qG)
                    em.w("tensor", sV)
                    for lch in range(2):
                        pst = ps[2 + lch]
                        mm_guard(pst)
                        for e2 in range(2):
                            for ib in range(NIB):
                                em.o("tensor", lambda e, e2=e2, ib=ib, lch=lch, pst=pst:
                                     e.matmul(
                                         pst[:, :],
                                         ed_b[:, e2 * EI + ib * 128: e2 * EI + (ib + 1) * 128],
                                         hh_w[:, (e2 * NIB + ib) * 1024 + lch * 512:
                                              (e2 * NIB + ib) * 1024 + (lch + 1) * 512],
                                         start=(e2 == 0 and ib == 0),
                                         stop=(e2 == 1 and ib == NIB - 1)),
                                     inc=(sT if (e2 == 1 and ib == NIB - 1) else None))
                        em.w("vector", sT)
                        em.w("vector", qS)
                        em.o("vector", lambda e, lch=lch, pst=pst: e.tensor_copy(
                            mstg[:, lch * 512:(lch + 1) * 512], pst[:, :]), inc=sV)
                        consume(pst, sV)
                    em.w("sync", sV)
                    em.dma("sync", lambda e, s=s_seq, dob=dob, qt=qt: e.dma_start(
                        out=z["mrs_in"][s].ap().rearrange("(j m) n -> j m n", m=DM)
                        [2 * qt:2 * qt + 2, dob * 128:(dob + 1) * 128, :]
                        .rearrange("j m n -> m j n"),
                        in_=mstg[:, :].rearrange("p (j n) -> p j n", j=2)), qS)
            em.w("gpsimd", qS)
            em.w("gpsimd", cc)
            em.o("gpsimd", lambda e, s=s_seq: e.collective_compute(
                "ReduceScatter", ALU.add, replica_groups=[list(range(NCORE))],
                ins=[z["mrs_in"][s].ap()], outs=[z["mrs_out"][s].ap()]), inc=cc)
        cc_mrs = cc.n

        with ExitStack() as hk:
            C = lambda n, s, d: hk.enter_context(nc.sbuf_tensor(n + "S" + str(s_seq), s, d))
            sg_b = C("sg_b", [128, NDB * 128], BF16)
            su_b = C("su_b", [128, NDB * 128], BF16)
            sd_b = C("sd_b", [128, DM], BF16)
            hhs = C("hhs", [128, NSB * 512], BF16)
            ssil = C("ssil", [128, 512], BF16)
            comb = C("comb", [128, 4 * DM], F32)
            x2ic = C("x2ic", [128, 512], F32)
            rt_sb = C("rt_sb", [128, DM], BF16)
            mrs_sb = C("mrs_sb", [128, DM], BF16)
            ffn_t = C("ffn_t", [128, DM], F32)
            monl2 = C("monl2", [128, DM], F32)
            x1e = C("x1e", [128, DM], F32)
            sqe2 = C("sqe2", [128, DM], F32)

            for sib in range(NSB):
                em.w("gpsimd", sT)
                em.dma("gpsimd", lambda e, sib=sib: e.dma_start(
                    out=sg_b[:, :].rearrange("p (b n) -> p b n", b=NDB),
                    in_=z["sgT"][:, sib * 128:(sib + 1) * 128]
                    .rearrange("(b p) n -> p b n", p=128)), qG)
                em.dma("gpsimd", lambda e, sib=sib: e.dma_start(
                    out=su_b[:, :].rearrange("p (b n) -> p b n", b=NDB),
                    in_=z["suT"][:, sib * 128:(sib + 1) * 128]
                    .rearrange("(b p) n -> p b n", p=128)), qG)
                em.w("tensor", qG)
                pg, pu = ps[0], ps[1]
                mm_guard(pg)
                for db in range(NDB):
                    em.o("tensor", lambda e, db=db, pg=pg: e.matmul(
                        pg[:, :], sg_b[:, db * 128:(db + 1) * 128],
                        tT_b[:, db * 512:(db + 1) * 512],
                        start=(db == 0), stop=(db == NDB - 1)),
                        inc=(sT if db == NDB - 1 else None))
                mm_guard(pu)
                for db in range(NDB):
                    em.o("tensor", lambda e, db=db, pu=pu: e.matmul(
                        pu[:, :], su_b[:, db * 128:(db + 1) * 128],
                        tT_b[:, db * 512:(db + 1) * 512],
                        start=(db == 0), stop=(db == NDB - 1)),
                        inc=(sT if db == NDB - 1 else None))
                em.w("scalar", sT)
                em.w("scalar", sV)
                em.o("scalar", lambda e, pg=pg: e.activation(
                    ssil[:, :], pg[:, :], AF.Silu), inc=sC)
                consume(pg, sC)
                em.w("vector", sT)
                em.w("vector", sC)
                em.o("vector", lambda e, sib=sib, pu=pu: e.scalar_tensor_tensor(
                    hhs[:, sib * 512:(sib + 1) * 512], pu[:, :], 1.0,
                    ssil[:, :], ALU.mult, ALU.mult), inc=sV)
                consume(pu, sV)
            if "hhs" in dbg_outs:
                for j in range(4):
                    em.w("vector", sV)
                    em.o("vector", lambda e, j=j: e.tensor_copy(
                        sqe2[:, 0:512], hhs[:, j * 512:(j + 1) * 512]), inc=sV)
                    em.w("sync", sV)
                    em.dma("sync", lambda e, s=s_seq, j=j: e.dma_start(
                        out=dbg_outs["hhs"][s * 128:(s + 1) * 128, j * 512:(j + 1) * 512],
                        in_=sqe2[:, 0:512]), qS)
                    em.w("vector", qS)
            for dq in range(4):
                for sib in range(NSB):
                    em.w("gpsimd", sT)
                    em.dma("gpsimd", lambda e, sib=sib, dq=dq: e.dma_start(
                        out=sd_b[:, 0:512],
                        in_=z["sdT"][sib * 128:(sib + 1) * 128, dq * 512:(dq + 1) * 512]), qG)
                    em.w("tensor", qG)
                    em.w("tensor", sV)
                    for lb in range(4):
                        pst = ps[lb]
                        if sib == 0:
                            mm_guard(pst)
                        em.o("tensor", lambda e, sib=sib, lb=lb, pst=pst: e.matmul(
                            pst[:, :], hhs[:, sib * 512 + lb * 128: sib * 512 + (lb + 1) * 128],
                            sd_b[:, 0:512],
                            start=(sib == 0), stop=(sib == NSB - 1)),
                            inc=(sT if lb == 3 else None))
                em.w("vector", sT)
                for lb in range(4):
                    em.w("sync", sV)
                    em.dma("sync", lambda e, s=s_seq, lb=lb, dq=dq: e.dma_start(
                        out=x2ic[:, :],
                        in_=sp_x2i.ap()[s * 512 + lb * 128: s * 512 + (lb + 1) * 128,
                                        dq * 512:(dq + 1) * 512]), qS)
                    em.w("vector", qS)
                    em.o("vector", lambda e, lb=lb, dq=dq: e.tensor_add(
                        comb[:, lb * DM + dq * 512: lb * DM + (dq + 1) * 512],
                        x2ic[:, :], ps[lb][:, :]), inc=sV)
                    consume(ps[lb], sV)
            em.w("sync", cc, cc_mrs)
            for lb in range(4):
                em.w("sync", sV)
                em.dma("sync", lambda e, s=s_seq, lb=lb: e.dma_start(
                    out=mrs_sb[:, :].rearrange("p (b n) -> p b n", b=NDB),
                    in_=z["mrs_out"][s].ap().rearrange("(b p) n -> p b n", p=128)
                    [:, :, lb * 128:(lb + 1) * 128]), qS)
                em.w("tensor", qS)
                em.w("tensor", sC)
                for db in range(NDB):
                    mm_guard(ptbf)
                    em.o("tensor", lambda e, db=db: e.transpose(
                        ptbf[:, :], mrs_sb[:, db * 128:(db + 1) * 128], ident_b[:, :]),
                        inc=sT)
                    em.w("scalar", sT)
                    em.o("scalar", lambda e, db=db: e.copy(
                        rt_sb[:, db * 128:(db + 1) * 128], ptbf[:, :]), inc=sC)
                    consume(ptbf, sC)
                if "shr" in dbg_outs:
                    em.w("sync", sV)
                    em.dma("sync", lambda e, s=s_seq, lb=lb: e.dma_start(
                        out=dbg_outs["shr"][s * 512 + lb * 128: s * 512 + (lb + 1) * 128, :],
                        in_=comb[:, lb * DM:(lb + 1) * DM]), qS)
                em.w("vector", sC)
                em.o("vector", lambda e, lb=lb: e.tensor_add(
                    ffn_t[:, :], comb[:, lb * DM:(lb + 1) * DM], rt_sb[:, :]), inc=sV)
                if "routed" in dbg_outs:
                    em.w("sync", sV)
                    em.dma("sync", lambda e, s=s_seq, lb=lb: e.dma_start(
                        out=dbg_outs["routed"][s * 512 + lb * 128: s * 512 + (lb + 1) * 128, :],
                        in_=ffn_t[:, :]), qS)
                ln_stats(ffn_t, DM, 34, EPS_LN, sqe2)
                em.o("vector", lambda e: e.tensor_scalar(
                    monl2[:, :], ffn_t[:, :], st[:, 36:37], st[:, 39:40],
                    ALU.subtract, ALU.mult), inc=sV)
                em.w("sync", sV)
                em.dma("sync", lambda e, s=s_seq, lb=lb: e.dma_start(
                    out=x1e[:, :],
                    in_=z["sp_x1"].ap()[s * 512 + lb * 128: s * 512 + (lb + 1) * 128, :]), qS)
                em.w("vector", qS)
                em.o("vector", lambda e: e.tensor_add(sqe2[:, :], monl2[:, :], x1e[:, :]),
                     inc=sV)
                em.w("sync", sV)
                em.dma("sync", lambda e, s=s_seq, lb=lb: e.dma_start(
                    out=z["out_own"][s * 512 + lb * 128: s * 512 + (lb + 1) * 128, :],
                    in_=sqe2[:, :]), qS)


def make_core_inputs(inputs, c):
    T = lambda a: np.ascontiguousarray(a, dtype=np.float32)
    x = inputs["L0_attn_x"]
    d = {}
    d["x_own"] = T(np.concatenate(
        [x[0, 512 * c:512 * (c + 1)], x[1, 512 * c:512 * (c + 1)]], axis=0))
    d["A_slc"] = T(inputs["L0_attn_A_log"][256 * c:256 * (c + 1)])
    d["Dcol"] = T(np.asarray(inputs["L0_attn_D_param"]).reshape(NDB, 128).T)
    d["nwrow"] = T(np.asarray(inputs["L0_attn_norm_w"])[None, :])
    d["nbrow"] = T(np.asarray(inputs["L0_attn_norm_b"])[None, :])
    d["dtbrow"] = T(np.asarray(inputs["L0_attn_dt_proj_b"])[None, :])
    d["l1row"] = T(np.asarray(inputs["L0_ffn_ln1_w"])[None, :])
    d["l2row"] = T(np.asarray(inputs["L0_ffn_ln2_w"])[None, :])
    d["xpwT"] = T(np.asarray(inputs["L0_attn_x_proj_w"]).T)
    d["dtwT"] = T(np.asarray(inputs["L0_attn_dt_proj_w"]).T)
    d["opwT"] = T(np.asarray(inputs["L0_attn_out_proj_w"]).T)
    d["wq_own"] = T(np.asarray(inputs["L0_ffn_wq"])[:, 256 * c:256 * (c + 1)])
    d["wk_own"] = T(np.asarray(inputs["L0_ffn_wk"])[:, 256 * c:256 * (c + 1)])
    d["wv_own"] = T(np.asarray(inputs["L0_ffn_wv"])[:, 256 * c:256 * (c + 1)])
    d["wo_own"] = T(np.asarray(inputs["L0_ffn_wo"])[256 * c:256 * (c + 1), :])
    d["gwT"] = T(np.asarray(inputs["L0_ffn_gate_w"]).T)
    eg = np.asarray(inputs["L0_ffn_eg_w"])
    eu = np.asarray(inputs["L0_ffn_eu_w"])
    ed = np.asarray(inputs["L0_ffn_ed_w"])
    d["egT"] = T(np.stack([eg[2 * c].T, eg[2 * c + 1].T]))
    d["euT"] = T(np.stack([eu[2 * c].T, eu[2 * c + 1].T]))
    d["edT"] = T(np.stack([ed[2 * c].T, ed[2 * c + 1].T]))
    d["sgT"] = T(np.asarray(inputs["L0_ffn_sg_w"]).T)
    d["suT"] = T(np.asarray(inputs["L0_ffn_su_w"]).T)
    d["sdT"] = T(np.asarray(inputs["L0_ffn_sd_w"]).T)
    return d


def kernel(**inputs):
    nc, _ = build()
    in_maps = [make_core_inputs(inputs, c) for c in range(NCORE)]
    res = run_bass_kernel_spmd(nc, in_maps, core_ids=list(range(NCORE)))
    out = np.zeros((NSEQ, SEQ, DM), np.float32)
    for c in range(NCORE):
        o = res.results[c]["out_own"]
        out[0, 512 * c:512 * (c + 1)] = o[0:OWN]
        out[1, 512 * c:512 * (c + 1)] = o[OWN:OWNT]
    return out


def run_debug(inputs, phases, dbg):
    nc, dbg_outs = build(phases=phases, dbg=dbg)
    in_maps = [make_core_inputs(inputs, c) for c in range(NCORE)]
    res = run_bass_kernel_spmd(nc, in_maps, core_ids=list(range(NCORE)))
    return res


# version 02
# speedup vs baseline: 1.1783x; 1.1783x over previous
import math
import numpy as np
import concourse.bass as bass
import concourse.mybir as mybir
from concourse.bass_utils import run_bass_kernel_spmd
from concourse.aot_env import allow_ici_all_to_all
from contextlib import ExitStack

F32 = mybir.dt.float32
BF16 = mybir.dt.bfloat16
FP16 = mybir.dt.float16
AF = mybir.ActivationFunctionType
ALU = mybir.AluOpType
AX = mybir.AxisListType.X

NCORE = 8
SEQ = 4096
NSEQ = 2
DM = 2048
OWN = 512
OWNT = NSEQ * OWN
DSL = 256
NST = 64
RK = 128
HD = 128
NE = 16
EI = 1408
SI = 2816
EPS_LN = 1e-5
EPS_RMS = 1e-6
NDB = DM // 128
NIB = EI // 128
NSB = SI // 128
QSCALE = 1.0 / math.sqrt(HD)

ENGS = ("sync", "gpsimd", "tensor", "vector", "scalar")


class S:
    def __init__(self, nc, name):
        self.sem = nc.alloc_semaphore(name)
        self.n = 0
        self.name = name


class Em:
    def __init__(self):
        self.q = {k: [] for k in ENGS}

    def scope(self, name):
        for e in ENGS:
            self.q[e].append(("scope", name))

    def pop(self):
        for e in ENGS:
            self.q[e].append(("pop", None))

    def w(self, eng, sem, val=None):
        v = sem.n if val is None else val
        if v > 0:
            self.q[eng].append(lambda e, s=sem.sem, v=v: e.wait_ge(s, v))

    def o(self, eng, f, inc=None, inck=1):
        if inc is None:
            self.q[eng].append(f)
        else:
            inc.n += inck
            self.q[eng].append(lambda e, f=f, s=inc.sem, k=inck: f(e).then_inc(s, k))

    def dma(self, eng, f, sem):
        self.o(eng, f, inc=sem, inck=16)

    def barrier(self, bars, engs=ENGS):
        marks = {}
        for e in engs:
            self.o(e, lambda x: x.drain(), inc=bars[e])
            marks[e] = bars[e].n
        for e in engs:
            for e2 in engs:
                if e2 != e:
                    self.w(e, bars[e2], marks[e2])


def build(phases="ABCDE", dbg=()):
    nc = bass.Bass("TRN2", target_bir_lowering=False)

    def inp(name, shape, dtype=F32):
        return nc.dram_tensor(name, shape, dtype, kind="ExternalInput").ap()

    x_own = inp("x_own", [OWNT, DM])
    A_slc = inp("A_slc", [DSL, NST])
    Dcol = inp("Dcol", [128, NDB])
    nwrow = inp("nwrow", [1, DM])
    nbrow = inp("nbrow", [1, DM])
    dtbrow = inp("dtbrow", [1, DM])
    l1row = inp("l1row", [1, DM])
    l2row = inp("l2row", [1, DM])
    xpwT = inp("xpwT", [DM, 2 * NST + RK])
    dtwT = inp("dtwT", [RK, DM])
    opwT = inp("opwT", [DM, DM])
    wq_own = inp("wq_own", [DM, 2 * HD])
    wk_own = inp("wk_own", [DM, 2 * HD])
    wv_own = inp("wv_own", [DM, 2 * HD])
    wo_own = inp("wo_own", [2 * HD, DM])
    gwT = inp("gwT", [DM, NE])
    egT = inp("egT", [2, DM, EI])
    euT = inp("euT", [2, DM, EI])
    edT = inp("edT", [2, EI, DM])
    sgT = inp("sgT", [DM, SI])
    suT = inp("suT", [DM, SI])
    sdT = inp("sdT", [SI, DM])

    out_own = nc.dram_tensor("out_own", [OWNT, DM], F32, kind="ExternalOutput").ap()
    dbg_outs = {}
    for nm, shp in (("lx", [OWNT, DM]), ("xn", [OWNT, DM]), ("dt", [OWNT, DM]),
                    ("y", [NSEQ * DSL, SEQ]), ("x1", [OWNT, DM]),
                    ("hln", [OWNT, DM]), ("attn", [OWNT, DM]),
                    ("cw", [OWNT, NE]), ("x2i", [OWNT, DM]),
                    ("qt", [128, 2 * SEQ]), ("kt", [128, 2 * SEQ]),
                    ("htb", [128, NDB * 512]), ("hti", [128, 512]), ("hbt", [128, DM]), ("stg", [128, DM]), ("gl", [OWNT, NE]), ("shr", [OWNT, DM]), ("hhs", [NSEQ * 128, 2048]),
                    ("vv", [128, 32 * 256]), ("aot", [128, 2 * SEQ]),
                    ("routed", [OWNT, DM]), ("shared", [OWNT, DM])):
        if nm in dbg:
            dbg_outs[nm] = nc.dram_tensor("dbg_" + nm, shp, F32,
                                          kind="ExternalOutput").ap()

    e1_in = nc.dram_tensor("e1_in", [NCORE * 640, 1024], FP16)
    e1_out = nc.dram_tensor("e1_out", [NCORE * 640, 1024], FP16)
    e2_in = [nc.dram_tensor(f"e2_in{s}", [DM, OWN], BF16) for s in range(NSEQ)]
    e2_out = [nc.dram_tensor(f"e2_out{s}", [DM, OWN], BF16) for s in range(NSEQ)]
    ht_in = [nc.dram_tensor(f"ht_in{s}", [DM, OWN], BF16) for s in range(NSEQ)]
    ht_out = [nc.dram_tensor(f"ht_out{s}", [NCORE * DM, OWN], BF16) for s in range(NSEQ)]
    ars_in = [nc.dram_tensor(f"ars_in{s}", [SEQ, DM], BF16) for s in range(NSEQ)]
    ars_out = [nc.dram_tensor(f"ars_out{s}", [OWN, DM], BF16) for s in range(NSEQ)]
    tt_in = [nc.dram_tensor(f"tt_in{s}", [DM, OWN], BF16) for s in range(NSEQ)]
    tt_out = [nc.dram_tensor(f"tt_out{s}", [NCORE * DM, OWN], BF16) for s in range(NSEQ)]
    cw_in = [nc.dram_tensor(f"cw_in{s}", [NE, OWN], F32) for s in range(NSEQ)]
    cw_out = [nc.dram_tensor(f"cw_out{s}", [NE, OWN], F32) for s in range(NSEQ)]
    mrs_in = [nc.dram_tensor(f"mrs_in{s}", [NCORE * DM, OWN], BF16) for s in range(NSEQ)]
    mrs_out = [nc.dram_tensor(f"mrs_out{s}", [DM, OWN], BF16) for s in range(NSEQ)]
    sp_lx = nc.dram_tensor("sp_lx", [OWNT, DM], F32)
    sp_x1 = nc.dram_tensor("sp_x1", [OWNT, DM], F32)
    sp_lx1 = nc.dram_tensor("sp_lx1", [OWNT, DM], F32)
    sp_x2i = nc.dram_tensor("sp_x2i", [OWNT, DM], F32)

    ident_dram = nc.inline_tensor(np.eye(128, dtype=np.float32), "identf")

    qS = S(nc, "qS")
    qG = S(nc, "qG")
    cc = S(nc, "cc")
    sT = S(nc, "sT")
    sV = S(nc, "sV")
    sC = S(nc, "sC")
    sSy = S(nc, "sSy")
    sGp = S(nc, "sGp")
    BAR = {"sync": sSy, "gpsimd": sGp, "tensor": sT, "vector": sV, "scalar": sC}

    em = Em()

    stk = ExitStack()
    ident_f = stk.enter_context(nc.sbuf_tensor("ident_f", [128, 128], F32))
    ident_b = stk.enter_context(nc.sbuf_tensor("ident_b", [128, 128], BF16))
    ident_h = stk.enter_context(nc.sbuf_tensor("ident_h", [128, 128], FP16))
    Dcol_sb = stk.enter_context(nc.sbuf_tensor("Dcol_sb", [128, NDB], F32))
    A_sb = stk.enter_context(nc.sbuf_tensor("A_sb", [128, 2 * NST], F32))
    st = stk.enter_context(nc.sbuf_tensor("stats", [128, 48], F32))
    kcol = stk.enter_context(nc.sbuf_tensor("kcol", [128, 4], F32))
    ps = [stk.enter_context(nc.psum_tensor(f"ps{i}", [128, 512], F32)) for i in range(4)]
    ptA = stk.enter_context(nc.psum_tensor("ptA", [128, 128], F32))
    ptB = stk.enter_context(nc.psum_tensor("ptB", [128, 128], F32))
    pt16 = stk.enter_context(nc.psum_tensor("pt16", [128, 128], FP16))
    ptbf = stk.enter_context(nc.psum_tensor("ptbf", [128, 128], BF16))

    ps_guard = {}

    def mm_guard(pst):
        g = ps_guard.get(id(pst))
        if g:
            em.w("tensor", g[0], g[1])

    def consume(pst, sem):
        ps_guard[id(pst)] = (sem, sem.n)

    em.dma("sync", lambda e: e.dma_start(out=ident_f[:, :], in_=ident_dram.ap()), qS)
    em.dma("sync", lambda e: e.dma_start(out=Dcol_sb[:, :], in_=Dcol), qS)
    em.dma("sync", lambda e: e.dma_start(
        out=A_sb[:, :].rearrange("p (b n) -> p b n", b=2),
        in_=A_slc.rearrange("(b p) n -> p b n", p=128)), qS)
    em.w("vector", qS)
    em.o("vector", lambda e: e.tensor_copy(ident_b[:, :], ident_f[:, :]))
    em.o("vector", lambda e: e.memset(kcol[:, 0:1], EPS_LN))
    em.o("vector", lambda e: e.memset(kcol[:, 1:2], EPS_RMS))
    em.o("vector", lambda e: e.memset(kcol[:, 2:3], QSCALE))
    em.o("vector", lambda e: e.tensor_copy(ident_h[:, :], ident_f[:, :]), inc=sV)
    em.w("scalar", sV)
    em.w("scalar", qS)
    em.o("scalar", lambda e: e.activation(A_sb[:, :], A_sb[:, :], AF.Exp))
    em.o("scalar", lambda e: e.activation(A_sb[:, :], A_sb[:, :], AF.Copy, scale=-1.0),
         inc=sC)

    def tv(f):
        em.w("vector", sV)
        em.o("vector", f, inc=sV)

    def ln_stats(xt, F, c0, eps, scratch):
        em.w("scalar", sV)
        em.w("vector", sC)
        em.o("vector", lambda e: e.reduce_sum(st[:, c0:c0 + 1], xt[:, 0:F], axis=AX),
             inc=sV)
        em.o("scalar", lambda e: e.activation(
            scratch[:, 0:F], xt[:, 0:F], AF.Square, accum_out=st[:, c0 + 1:c0 + 2]),
            inc=sC)
        em.w("vector", sC)
        tv(lambda e: e.tensor_scalar(
            st[:, c0 + 2:c0 + 3], st[:, c0:c0 + 1], 1.0 / F, None, ALU.mult))
        tv(lambda e: e.tensor_mul(
            st[:, c0 + 3:c0 + 4], st[:, c0 + 2:c0 + 3], st[:, c0 + 2:c0 + 3]))
        tv(lambda e: e.tensor_scalar(
            st[:, c0 + 4:c0 + 5], st[:, c0 + 1:c0 + 2], 1.0 / F, None, ALU.mult))
        tv(lambda e: e.tensor_sub(
            st[:, c0 + 5:c0 + 6], st[:, c0 + 4:c0 + 5], st[:, c0 + 3:c0 + 4]))
        em.w("scalar", sV)
        em.o("scalar", lambda e: e.activation(
            st[:, c0 + 4:c0 + 5], st[:, c0 + 5:c0 + 6], AF.Sqrt, bias=kcol[:, 0:1]),
            inc=sC)
        em.w("vector", sC)
        tv(lambda e: e.reciprocal(st[:, c0 + 5:c0 + 6], st[:, c0 + 4:c0 + 5]))
        em.w("vector", sV)

    def rms_stats(xt, F, c0, eps, scratch):
        em.w("scalar", sV)
        em.o("scalar", lambda e: e.activation(
            scratch[:, 0:F], xt[:, 0:F], AF.Square, accum_out=st[:, c0:c0 + 1]),
            inc=sC)
        em.w("vector", sC)
        tv(lambda e: e.tensor_scalar(
            st[:, c0 + 2:c0 + 3], st[:, c0:c0 + 1], 1.0 / F, None, ALU.mult))
        em.w("scalar", sV)
        em.o("scalar", lambda e: e.activation(
            st[:, c0:c0 + 1], st[:, c0 + 2:c0 + 3], AF.Sqrt, bias=kcol[:, 1:2]), inc=sC)
        em.w("vector", sC)
        tv(lambda e: e.reciprocal(st[:, c0 + 1:c0 + 2], st[:, c0:c0 + 1]))
        em.w("vector", sV)

    if "A" in phases:
        em.scope("phA")
        with ExitStack() as pk:
            A = lambda n, s, d: pk.enter_context(nc.sbuf_tensor(n, s, d))
            xa = A("xa", [128, DM], F32)
            lxa = A("lxa", [128, DM], F32)
            sqa = A("sqa", [128, DM], F32)
            ln2a = A("ln2a", [128, DM], F32)
            xna = A("xna", [128, DM], F32)
            xna_b = A("xna_b", [128, DM], BF16)
            xnT_b = A("xnT_b", [128, NDB * 128], BF16)
            nw_rep = A("nw_rep", [128, DM], F32)
            nb_rep = A("nb_rep", [128, DM], F32)
            dtb_rep = A("dtb_rep", [128, DM], F32)
            xpw_b = A("xpw_b", [128, NDB * 256], BF16)
            dtw_b = A("dtw_b", [128, DM], BF16)
            pj_del = A("pj_del", [128, 128], BF16)
            pbc_h = A("pbc_h", [128, 128], FP16)
            dta = A("dta", [128, DM], F32)
            spa = A("spa", [128, DM], F32)
            dtxa = A("dtxa", [128, DM], F32)
            dt_h = A("dt_h", [128, DM], FP16)
            dtx_h = A("dtx_h", [128, DM], FP16)
            stag = A("stag", [128, NCORE * 5 * 128], FP16)

            em.dma("sync", lambda e: e.dma_start(out=nw_rep[:, :], in_=nwrow.broadcast_to([128, DM])), qS)
            em.dma("sync", lambda e: e.dma_start(out=nb_rep[:, :], in_=nbrow.broadcast_to([128, DM])), qS)
            em.dma("sync", lambda e: e.dma_start(out=dtb_rep[:, :], in_=dtbrow.broadcast_to([128, DM])), qS)
            em.dma("gpsimd", lambda e: e.dma_start(
                out=xpw_b[:, :].rearrange("p (b n) -> p b n", b=NDB),
                in_=xpwT.rearrange("(b p) n -> p b n", p=128)), qG)
            em.dma("gpsimd", lambda e: e.dma_start(out=dtw_b[:, :], in_=dtwT), qG)

            for t in range(8):
                em.w("sync", sV)
                em.w("sync", sC)
                em.dma("sync", lambda e, t=t: e.dma_start(
                    out=xa[:, :], in_=x_own[t * 128:(t + 1) * 128, :]), qS)
                em.w("vector", qS)
                em.w("vector", sT)
                em.w("scalar", qS)
                em.w("scalar", sT)
                ln_stats(xa, DM, 0, EPS_LN, sqa)
                em.o("vector", lambda e: e.tensor_scalar(
                    lxa[:, :], xa[:, :], st[:, 2:3], st[:, 5:6],
                    ALU.subtract, ALU.mult), inc=sV)
                em.w("sync", sV)
                em.dma("sync", lambda e, t=t: e.dma_start(
                    out=sp_lx.ap()[t * 128:(t + 1) * 128, :], in_=lxa[:, :]), qS)
                if "lx" in dbg_outs:
                    em.dma("sync", lambda e, t=t: e.dma_start(
                        out=dbg_outs["lx"][t * 128:(t + 1) * 128, :], in_=lxa[:, :]), qS)
                ln_stats(lxa, DM, 6, EPS_LN, sqa)
                em.o("vector", lambda e: e.tensor_scalar(
                    ln2a[:, :], lxa[:, :], st[:, 8:9], st[:, 11:12],
                    ALU.subtract, ALU.mult))
                em.o("vector", lambda e: e.tensor_mul(sqa[:, :], ln2a[:, :], nw_rep[:, :]))
                em.o("vector", lambda e: e.tensor_add(xna[:, :], sqa[:, :], nb_rep[:, :]))
                em.o("vector", lambda e: e.tensor_copy(xna_b[:, :], xna[:, :]), inc=sV)
                if "xn" in dbg_outs:
                    em.w("sync", sV)
                    em.dma("sync", lambda e, t=t: e.dma_start(
                        out=dbg_outs["xn"][t * 128:(t + 1) * 128, :], in_=xna[:, :]), qS)
                em.w("tensor", sV)
                em.w("tensor", qG)
                for db in range(NDB):
                    mm_guard(ptbf)
                    em.o("tensor", lambda e, db=db: e.transpose(
                        ptbf[:, :], xna_b[:, db * 128:(db + 1) * 128], ident_b[:, :]),
                        inc=sT)
                    em.w("scalar", sT)
                    em.o("scalar", lambda e, db=db: e.copy(
                        xnT_b[:, db * 128:(db + 1) * 128], ptbf[:, :]), inc=sC)
                    consume(ptbf, sC)
                em.w("tensor", sC)
                for pr in range(2):
                    pst = ptA if pr == 0 else ptB
                    mm_guard(pst)
                    for db in range(NDB):
                        em.o("tensor", lambda e, pr=pr, db=db, pst=pst: e.matmul(
                            pst[:, :],
                            xpw_b[:, db * 256 + pr * 128: db * 256 + (pr + 1) * 128],
                            xnT_b[:, db * 128:(db + 1) * 128],
                            start=(db == 0), stop=(db == NDB - 1)),
                            inc=(sT if db == NDB - 1 else None))
                    em.w("vector", sT)
                    dst = pj_del if pr == 0 else pbc_h
                    em.o("vector", lambda e, dst=dst, pst=pst: e.tensor_copy(
                        dst[:, :], pst[:, :]), inc=sV)
                    consume(pst, sV)
                em.w("scalar", sV)
                for sh in range(NCORE):
                    em.o("scalar", lambda e, sh=sh: e.copy(
                        stag[:, (sh * 5 + 4) * 128:(sh * 5 + 5) * 128], pbc_h[:, :]),
                        inc=(sC if sh == NCORE - 1 else None))
                em.w("tensor", sV)
                for dch in range(4):
                    pst = ps[dch]
                    mm_guard(pst)
                    em.o("tensor", lambda e, dch=dch, pst=pst: e.matmul(
                        pst[:, :], pj_del[:, :], dtw_b[:, dch * 512:(dch + 1) * 512],
                        start=True, stop=True), inc=sT)
                em.w("vector", sT)
                for dch in range(4):
                    em.o("vector", lambda e, dch=dch: e.tensor_add(
                        dta[:, dch * 512:(dch + 1) * 512], ps[dch][:, :],
                        dtb_rep[:, dch * 512:(dch + 1) * 512]),
                        inc=(sV if dch == 3 else None))
                for dch in range(4):
                    consume(ps[dch], sV)
                em.w("scalar", sV)
                em.o("scalar", lambda e: e.activation(spa[:, :], dta[:, :], AF.Abs))
                em.o("scalar", lambda e: e.activation(spa[:, :], spa[:, :], AF.Exp, scale=-1.0))
                em.o("scalar", lambda e: e.activation(spa[:, :], spa[:, :], AF.Ln, bias=1.0),
                     inc=sC)
                em.w("vector", sC)
                em.o("vector", lambda e: e.tensor_relu(sqa[:, :], dta[:, :]))
                em.o("vector", lambda e: e.tensor_add(dta[:, :], sqa[:, :], spa[:, :]),
                     inc=sV)
                if "dt" in dbg_outs:
                    em.w("sync", sV)
                    em.dma("sync", lambda e, t=t: e.dma_start(
                        out=dbg_outs["dt"][t * 128:(t + 1) * 128, :], in_=dta[:, :]), qS)
                em.o("vector", lambda e: e.tensor_mul(dtxa[:, :], dta[:, :], xna[:, :]))
                em.o("vector", lambda e: e.tensor_copy(dt_h[:, :], dta[:, :]))
                em.o("vector", lambda e: e.tensor_copy(dtx_h[:, :], dtxa[:, :]), inc=sV)
                em.w("tensor", sV)
                em.w("scalar", qS)
                for which, src in ((0, dt_h), (1, dtx_h)):
                    for db in range(NDB):
                        sh, half = db // 2, db % 2
                        piece = which * 2 + half
                        mm_guard(pt16)
                        em.o("tensor", lambda e, src=src, db=db: e.transpose(
                            pt16[:, :], src[:, db * 128:(db + 1) * 128], ident_h[:, :]),
                            inc=sT)
                        em.w("scalar", sT)
                        em.o("scalar", lambda e, sh=sh, piece=piece: e.copy(
                            stag[:, (sh * 5 + piece) * 128:(sh * 5 + piece + 1) * 128],
                            pt16[:, :]), inc=sC)
                        consume(pt16, sC)
                em.w("sync", sC)
                em.w("sync", qS)
                em.dma("sync", lambda e, t=t: e.dma_start(
                    out=e1_in.ap().rearrange("(f p) n -> p f n", p=128)
                    [:, :, t * 128:(t + 1) * 128],
                    in_=stag[:, :].rearrange("p (f n) -> p f n", n=128)), qS)
                em.w("sync", qS)

            em.w("gpsimd", qS)
            with allow_ici_all_to_all():
                em.w("gpsimd", cc)
                em.o("gpsimd", lambda e: e.collective_compute(
                    "AllToAll", ALU.bypass, replica_groups=[list(range(NCORE))],
                    ins=[e1_in.ap()], outs=[e1_out.ap()]), inc=cc)
            em.pop()
            em.barrier(BAR)

    if "B" in phases:
        em.scope("phB")
        cc_e1 = cc.n
        with ExitStack() as pk:
            A = lambda n, s, d: pk.enter_context(nc.sbuf_tensor(n, s, d))
            dtT = A("dtT", [128, 2 * SEQ], FP16)
            dtxT = A("dtxT", [128, 2 * SEQ], FP16)
            Brep = A("Brep", [128, 2 * SEQ], FP16)
            Crep = A("Crep", [128, 2 * SEQ], FP16)
            dAb = A("dAb", [128, 2 * SEQ], F32)
            ub = A("ub", [128, SEQ], FP16)
            hb = A("hb", [128, SEQ], FP16)
            qscan = A("qscan", [128, SEQ], FP16)
            yb = A("yb", [128, 4 * SEQ], FP16)
            ybf = A("ybf", [128, SEQ], BF16)
            zero1 = A("zero1", [128, 1], F32)
            em.o("vector", lambda e: e.memset(zero1[:, :], 0.0), inc=sV)

            e1o = e1_out.ap().rearrange("(r m) n -> r m n", m=640)
            for s_seq in range(NSEQ):
                em.w("sync", cc, cc_e1)
                em.w("sync", sV)
                for k in range(2):
                    em.dma("sync", lambda e, k=k, s=s_seq: e.dma_start(
                        out=dtT[:, k * SEQ:(k + 1) * SEQ].rearrange("p (r n) -> p r n", r=8),
                        in_=e1o[:, k * 128:(k + 1) * 128, s * 512:(s + 1) * 512]
                        .rearrange("r m n -> m r n")), qS)
                    em.dma("sync", lambda e, k=k, s=s_seq: e.dma_start(
                        out=dtxT[:, k * SEQ:(k + 1) * SEQ].rearrange("p (r n) -> p r n", r=8),
                        in_=e1o[:, 256 + k * 128:256 + (k + 1) * 128, s * 512:(s + 1) * 512]
                        .rearrange("r m n -> m r n")), qS)
                for k in range(2):
                    em.o("vector", lambda e, k=k: e.memset(
                        yb[:, (k * 2) * SEQ:(k * 2 + 1) * SEQ], 0.0),
                        inc=(sV if k == 1 else None))
                ld_done = qS.n
                rep_done = {}
                yv_hist = {}
                step_hist = {}
                sv_base = sV.n
                for n in range(NST + 1):
                    if n < NST:
                        par = n % 2
                        if n >= 2:
                            em.w("sync", sV, yv_hist[n - 2])
                        em.dma("sync", lambda e, n=n, s=s_seq, par=par: e.dma_start(
                            out=Brep[:, par * SEQ:(par + 1) * SEQ]
                            .rearrange("p (r n) -> p r n", r=8),
                            in_=e1o[:, 512 + n:513 + n, s * 512:(s + 1) * 512]
                            .rearrange("r m n -> m r n").broadcast_to([128, 8, 512])), qS)
                        em.dma("sync", lambda e, n=n, s=s_seq, par=par: e.dma_start(
                            out=Crep[:, par * SEQ:(par + 1) * SEQ]
                            .rearrange("p (r n) -> p r n", r=8),
                            in_=e1o[0:1, 512 + NST + n:513 + NST + n, s * 512:(s + 1) * 512]
                            .broadcast_to([128, 8, 512])), qS)
                        rep_done[n] = qS.n
                    if n == 0:
                        continue
                    nn = n - 1
                    par = nn % 2
                    for k in range(2):
                        sp = (nn * 2 + k) % 2
                        em.w("scalar", qS, ld_done)
                        em.w("scalar", sV, step_hist.get(nn * 2 + k - 2, sv_base))
                        em.o("scalar", lambda e, k=k, nn=nn, sp=sp: e.activation(
                            dAb[:, sp * SEQ:(sp + 1) * SEQ],
                            dtT[:, k * SEQ:(k + 1) * SEQ], AF.Exp,
                            scale=A_sb[:, k * NST + nn:k * NST + nn + 1]), inc=sC)
                        em.w("vector", sC)
                        em.w("vector", qS, rep_done[nn])
                        em.o("vector", lambda e, k=k, par=par: e.tensor_mul(
                            ub[:, :], dtxT[:, k * SEQ:(k + 1) * SEQ],
                            Brep[:, par * SEQ:(par + 1) * SEQ]))
                        em.o("vector", lambda e, sp=sp: e.tensor_tensor_scan(
                            hb[:, :], dAb[:, sp * SEQ:(sp + 1) * SEQ], ub[:, :],
                            zero1[:, 0:1], ALU.mult, ALU.add), inc=sV)
                        em.w("vector", sV)
                        em.o("vector", lambda e, par=par: e.tensor_mul(
                            qscan[:, :], hb[:, :], Crep[:, par * SEQ:(par + 1) * SEQ]))
                        pp, qq = (nn % 2), ((nn + 1) % 2)
                        em.o("vector", lambda e, k=k, pp=pp, qq=qq: e.tensor_add(
                            yb[:, (k * 2 + qq) * SEQ:(k * 2 + qq + 1) * SEQ],
                            yb[:, (k * 2 + pp) * SEQ:(k * 2 + pp + 1) * SEQ], qscan[:, :]),
                            inc=sV)
                        step_hist[nn * 2 + k] = sV.n
                    yv_hist[nn] = sV.n
                for k in range(2):
                    em.o("vector", lambda e, k=k: e.tensor_copy(
                        ybf[:, :], yb[:, (k * 2) * SEQ:(k * 2 + 1) * SEQ]), inc=sV)
                    if "y" in dbg_outs:
                        em.o("vector", lambda e, k=k: e.tensor_copy(
                            dAb[:, 0:SEQ], yb[:, (k * 2) * SEQ:(k * 2 + 1) * SEQ]), inc=sV)
                        em.w("sync", sV)
                        em.dma("sync", lambda e, k=k, s=s_seq: e.dma_start(
                            out=dbg_outs["y"][s * DSL + k * 128: s * DSL + (k + 1) * 128, :],
                            in_=dAb[:, 0:SEQ]), qS)
                    em.w("sync", sV)
                    em.dma("sync", lambda e, k=k, s=s_seq: e.dma_start(
                        out=e2_in[s].ap().rearrange("(j m) n -> j m n", m=256)
                        [:, k * 128:(k + 1) * 128, :].rearrange("j m n -> m j n"),
                        in_=ybf[:, :].rearrange("p (j n) -> p j n", j=8)), qS)
                    em.w("vector", qS)
                em.w("gpsimd", qS)
                em.w("gpsimd", cc)
                with allow_ici_all_to_all():
                    em.o("gpsimd", lambda e, s=s_seq: e.collective_compute(
                        "AllToAll", ALU.bypass, replica_groups=[list(range(NCORE))],
                        ins=[e2_in[s].ap()], outs=[e2_out[s].ap()]), inc=cc)
            em.pop()
            em.barrier(BAR)

    if "C" in phases:
        em.scope("phC")
        cc_e2 = cc.n
        with ExitStack() as pk:
            A = lambda n, s, d: pk.enter_context(nc.sbuf_tensor(n, s, d))
            ypx_g = A("ypx_g", [128, NDB * 512], BF16)
            lxc = A("lxc", [128, DM], F32)
            ypxT_b = A("ypxT_b", [128, NDB * 512], BF16)
            opw_b = A("opw_b", [128, NDB * 512], BF16)
            mo_all = A("mo_all", [128, 4 * DM], F32)
            xc = A("xc", [128, DM], F32)
            sqc = A("sqc", [128, DM], F32)
            monl = A("monl", [128, DM], F32)
            x1c = A("x1c", [128, DM], F32)

            for s_seq in range(NSEQ):
                em.w("sync", cc, cc_e2)
                em.w("sync", sT)
                em.w("sync", sV)
                em.dma("sync", lambda e, s=s_seq: e.dma_start(
                    out=ypx_g[:, :].rearrange("p (b n) -> p b n", b=NDB),
                    in_=e2_out[s].ap().rearrange("(b p) n -> p b n", p=128)), qS)
                for lb in range(4):
                    em.w("sync", sT)
                    em.dma("sync", lambda e, s=s_seq, lb=lb: e.dma_start(
                        out=lxc[:, :],
                        in_=sp_lx.ap()[s * 512 + lb * 128: s * 512 + (lb + 1) * 128, :]), qS)
                    em.w("tensor", qS)
                    em.w("vector", qS)
                    for db in range(NDB):
                        pst = ptA if db % 2 == 0 else ptB
                        mm_guard(pst)
                        em.o("tensor", lambda e, db=db, pst=pst: e.transpose(
                            pst[:, :], lxc[:, db * 128:(db + 1) * 128], ident_f[:, :]),
                            inc=sT)
                        em.w("vector", sT)
                        em.o("vector", lambda e, db=db, lb=lb, pst=pst: e.scalar_tensor_tensor(
                            ypxT_b[:, db * 512 + lb * 128: db * 512 + (lb + 1) * 128],
                            pst[:, :], Dcol_sb[:, db:db + 1],
                            ypx_g[:, db * 512 + lb * 128: db * 512 + (lb + 1) * 128],
                            ALU.mult, ALU.add), inc=sV)
                        consume(pst, sV)
                for dch in range(4):
                    em.w("gpsimd", sT)
                    em.dma("gpsimd", lambda e, dch=dch: e.dma_start(
                        out=opw_b[:, :].rearrange("p (b n) -> p b n", b=NDB),
                        in_=opwT[:, dch * 512:(dch + 1) * 512]
                        .rearrange("(b p) n -> p b n", p=128)), qG)
                    em.w("tensor", qG)
                    em.w("tensor", sV)
                    for lb in range(4):
                        pst = ps[lb]
                        mm_guard(pst)
                        for db in range(NDB):
                            em.o("tensor", lambda e, db=db, lb=lb, pst=pst: e.matmul(
                                pst[:, :],
                                ypxT_b[:, db * 512 + lb * 128: db * 512 + (lb + 1) * 128],
                                opw_b[:, db * 512:(db + 1) * 512],
                                start=(db == 0), stop=(db == NDB - 1)),
                                inc=(sT if db == NDB - 1 else None))
                    em.w("scalar", sT)
                    for lb in range(4):
                        em.o("scalar", lambda e, lb=lb, dch=dch: e.copy(
                            mo_all[:, lb * DM + dch * 512: lb * DM + (dch + 1) * 512],
                            ps[lb][:, :]), inc=(sC if lb == 3 else None))
                    for lb in range(4):
                        consume(ps[lb], sC)
                for lb in range(4):
                    em.w("sync", sV)
                    em.dma("sync", lambda e, s=s_seq, lb=lb: e.dma_start(
                        out=xc[:, :],
                        in_=x_own[s * 512 + lb * 128: s * 512 + (lb + 1) * 128, :]), qS)
                    em.w("vector", qS)
                    em.w("vector", sC)
                    mot = mo_all[:, lb * DM:(lb + 1) * DM]
                    ln_stats(mot, DM, 12, EPS_LN, sqc)
                    em.o("vector", lambda e, mot=mot: e.tensor_scalar(
                        monl[:, :], mot[:, :], st[:, 14:15], st[:, 17:18],
                        ALU.subtract, ALU.mult))
                    em.o("vector", lambda e: e.tensor_add(x1c[:, :], xc[:, :], monl[:, :]),
                         inc=sV)
                    em.w("sync", sV)
                    em.dma("sync", lambda e, s=s_seq, lb=lb: e.dma_start(
                        out=sp_x1.ap()[s * 512 + lb * 128: s * 512 + (lb + 1) * 128, :],
                        in_=x1c[:, :]), qS)
                    if "x1" in dbg_outs:
                        em.dma("sync", lambda e, s=s_seq, lb=lb: e.dma_start(
                            out=dbg_outs["x1"][s * 512 + lb * 128: s * 512 + (lb + 1) * 128, :],
                            in_=x1c[:, :]), qS)
            em.pop()
            em.barrier(BAR)

    if "D" in phases:
        for s_seq in range(NSEQ):
            with ExitStack() as sk:
                with ExitStack() as pk:
                    A = lambda n, s, d: pk.enter_context(nc.sbuf_tensor(n + str(s_seq), s, d))
                    xd = A("xd", [128, DM], F32)
                    sqd = A("sqd", [128, DM], F32)
                    hb_t = A("hb_t", [128, DM], BF16)
                    l1_rep = A("l1_rep", [128, DM], F32)
                    stg = A("stg", [128, NDB * 128], BF16)
                    wq_b = A("wq_b", [128, NDB * 256], BF16)
                    wk_b = A("wk_b", [128, NDB * 256], BF16)
                    wv_b = A("wv_b", [128, NDB * 256], BF16)
                    wo_b = A("wo_b", [128, 2 * DM], BF16)
                    hTb = A("hTb", [128, NDB * 512], BF16)
                    qT_sb = A("qT_sb", [128, 2 * SEQ], BF16)
                    kT_sb = A("kT_sb", [128, 2 * SEQ], BF16)
                    vtmp = A("vtmp", [128, 512], BF16)
                    v_sb = A("v_sb", [128, 32 * 256], BF16)
                    sc_sb = A("sc_sb", [128, SEQ], F32)
                    p2_sb = A("p2_sb", [128, SEQ], BF16)
                    p2b_sb = A("p2b_sb", [128, SEQ], BF16)
                    pT_sb = A("pT_sb", [128, 128], BF16)
                    aoT_sb = A("aoT_sb", [128, 2 * SEQ], BF16)
                    arstg = A("arstg", [128, DM], BF16)
                    ars_sb = A("ars_sb", [128, DM], BF16)
                    mask_sb = A("mask_sb", [128, 4 * 512], BF16)
                    lx1t = A("lx1t", [128, DM], F32)
                    x2it = A("x2it", [128, DM], F32)

                    import ml_dtypes as _mld
                    mrow = np.zeros((4, 128, 512), np.float32)
                    for qo in range(4):
                        for i in range(128):
                            mrow[qo, i, qo * 128 + i + 1:] = -1e9
                    mask_dram = nc.inline_tensor(
                        mrow.transpose(1, 0, 2).reshape(128, 4 * 512)
                        .astype(_mld.bfloat16), f"mask{s_seq}")
                    em.dma("sync", lambda e: e.dma_start(
                        out=mask_sb[:, :], in_=mask_dram.ap()), qS)
                    em.dma("sync", lambda e: e.dma_start(
                        out=l1_rep[:, :], in_=l1row.broadcast_to([128, DM])), qS)
                    em.dma("gpsimd", lambda e: e.dma_start(
                        out=wq_b[:, :].rearrange("p (b n) -> p b n", b=NDB),
                        in_=wq_own.rearrange("(b p) n -> p b n", p=128)), qG)
                    em.dma("gpsimd", lambda e: e.dma_start(
                        out=wk_b[:, :].rearrange("p (b n) -> p b n", b=NDB),
                        in_=wk_own.rearrange("(b p) n -> p b n", p=128)), qG)
                    em.dma("gpsimd", lambda e: e.dma_start(
                        out=wv_b[:, :].rearrange("p (b n) -> p b n", b=NDB),
                        in_=wv_own.rearrange("(b p) n -> p b n", p=128)), qG)
                    em.dma("gpsimd", lambda e: e.dma_start(
                        out=wo_b[:, :].rearrange("p (b n) -> p b n", b=2),
                        in_=wo_own.rearrange("(b p) n -> p b n", p=128)), qG)

                    em.scope("D_h" + str(s_seq))
                    for lb in range(4):
                        em.w("sync", sV)
                        em.w("sync", sC)
                        em.dma("sync", lambda e, s=s_seq, lb=lb: e.dma_start(
                            out=xd[:, :],
                            in_=sp_x1.ap()[s * 512 + lb * 128: s * 512 + (lb + 1) * 128, :]), qS)
                        em.w("vector", qS)
                        em.w("vector", sT)
                        em.w("scalar", qS)
                        ln_stats(xd, DM, 18, EPS_LN, sqd)
                        em.o("vector", lambda e: e.tensor_scalar(
                            lx1t[:, :], xd[:, :], st[:, 20:21], st[:, 23:24],
                            ALU.subtract, ALU.mult), inc=sV)
                        em.w("sync", sV)
                        em.dma("sync", lambda e, s=s_seq, lb=lb: e.dma_start(
                            out=sp_lx1.ap()[s * 512 + lb * 128: s * 512 + (lb + 1) * 128, :],
                            in_=lx1t[:, :]), qS)
                        rms_stats(lx1t, DM, 24, EPS_RMS, sqd)
                        em.o("vector", lambda e: e.tensor_scalar(
                            sqd[:, :], lx1t[:, :], st[:, 25:26], None, ALU.mult))
                        em.o("vector", lambda e: e.tensor_mul(xd[:, :], sqd[:, :], l1_rep[:, :]))
                        em.o("vector", lambda e: e.tensor_copy(hb_t[:, :], xd[:, :]), inc=sV)
                        if "hln" in dbg_outs:
                            em.w("sync", sV)
                            em.dma("sync", lambda e, s=s_seq, lb=lb: e.dma_start(
                                out=dbg_outs["hln"][s * 512 + lb * 128: s * 512 + (lb + 1) * 128, :],
                                in_=xd[:, :]), qS)
                        em.w("tensor", sV)
                        em.w("scalar", qS)
                        if s_seq == 0 and lb == 0 and "hbt" in dbg_outs:
                            em.w("vector", sV)
                            em.o("vector", lambda e: e.tensor_copy(sqd[:, :], hb_t[:, :]),
                                 inc=sV)
                            em.w("sync", sV)
                            em.dma("sync", lambda e: e.dma_start(
                                out=dbg_outs["hbt"], in_=sqd[:, :]), qS)
                            em.w("vector", qS)
                        for db in range(NDB):
                            mm_guard(ptbf)
                            em.o("tensor", lambda e, db=db: e.transpose(
                                ptbf[:, :], hb_t[:, db * 128:(db + 1) * 128], ident_b[:, :]),
                                inc=sT)
                            em.w("scalar", sT)
                            em.o("scalar", lambda e, db=db: e.copy(
                                stg[:, db * 128:(db + 1) * 128], ptbf[:, :]), inc=sC)
                            consume(ptbf, sC)
                        if s_seq == 0 and lb == 0 and "stg" in dbg_outs:
                            em.w("vector", sC)
                            em.o("vector", lambda e: e.tensor_copy(sqd[:, :], stg[:, :]),
                                 inc=sV)
                            em.w("sync", sV)
                            em.dma("sync", lambda e: e.dma_start(
                                out=dbg_outs["stg"], in_=sqd[:, :]), qS)
                            em.w("vector", qS)
                            em.w("scalar", qS)
                        em.w("sync", sC)
                        em.w("sync", qS)
                        em.dma("sync", lambda e, s=s_seq, lb=lb: e.dma_start(
                            out=ht_in[s].ap().rearrange("(b p) n -> p b n", p=128)
                            [:, :, lb * 128:(lb + 1) * 128],
                            in_=stg[:, :].rearrange("p (b n) -> p b n", b=NDB)), qS)
                        em.w("sync", qS)
                    if s_seq == 0 and "hti" in dbg_outs:
                        em.w("sync", qS)
                        em.dma("sync", lambda e: e.dma_start(
                            out=hTb[:, 0:512].rearrange("p n -> p n"),
                            in_=ht_in[0].ap()[0:128, :]), qS)
                        em.w("sync", qS)
                        for j in range(1):
                            em.w("vector", qS)
                            em.o("vector", lambda e: e.tensor_copy(
                                sqd[:, 0:512], hTb[:, 0:512]), inc=sV)
                            em.w("sync", sV)
                            em.dma("sync", lambda e: e.dma_start(
                                out=dbg_outs["hti"], in_=sqd[:, 0:512]), qS)
                            em.w("vector", qS)
                    em.w("gpsimd", qS)
                    em.w("gpsimd", cc)
                    em.o("gpsimd", lambda e, s=s_seq: e.collective_compute(
                        "AllGather", ALU.bypass, replica_groups=[list(range(NCORE))],
                        ins=[ht_in[s].ap()], outs=[ht_out[s].ap()]), inc=cc)
                    cc_ht = cc.n

                    em.pop()
                    em.scope("D_qkv" + str(s_seq))
                    hto = ht_out[s_seq].ap().rearrange("(r b p) n -> r p b n", p=128, b=NDB)
                    for r in range(NCORE):
                        em.w("sync", cc, cc_ht)
                        em.w("sync", sT)
                        em.dma("sync", lambda e, r=r, hto=hto: e.dma_start(
                            out=hTb[:, :].rearrange("p (b n) -> p b n", b=NDB),
                            in_=hto[r]), qS)
                        if r == 0 and s_seq == 0 and "htb" in dbg_outs:
                            for j in range(4):
                                em.w("vector", qS)
                                em.o("vector", lambda e, j=j: e.tensor_copy(
                                    sqd[:, :], hTb[:, j * 2048:(j + 1) * 2048]), inc=sV)
                                em.w("sync", sV)
                                em.dma("sync", lambda e, j=j: e.dma_start(
                                    out=dbg_outs["htb"][:, j * 2048:(j + 1) * 2048],
                                    in_=sqd[:, :]), qS)
                                em.w("vector", qS)
                        em.w("tensor", qS)
                        em.w("tensor", qG)
                        for ti, (wt, dst, scale) in enumerate(
                                ((wq_b, qT_sb, QSCALE), (wk_b, kT_sb, 1.0), (wv_b, None, 1.0))):
                            for h2 in range(2):
                                pst = ps[(ti * 2 + h2) % 4]
                                mm_guard(pst)
                                for db in range(NDB):
                                    em.o("tensor", lambda e, db=db, h2=h2, wt=wt, pst=pst: e.matmul(
                                        pst[:, :],
                                        wt[:, db * 256 + h2 * 128: db * 256 + (h2 + 1) * 128],
                                        hTb[:, db * 512:(db + 1) * 512],
                                        start=(db == 0), stop=(db == NDB - 1)),
                                        inc=(sT if db == NDB - 1 else None))
                                em.w("scalar", sT)
                                if ti < 2:
                                    em.o("scalar", lambda e, h2=h2, r=r, dst=dst, scale=scale, pst=pst:
                                         e.activation(
                                             dst[:, h2 * SEQ + r * 512: h2 * SEQ + (r + 1) * 512],
                                             pst[:, :], AF.Copy,
                                             scale=(kcol[:, 2:3] if scale != 1.0 else 1.0)), inc=sC)
                                    consume(pst, sC)
                                else:
                                    em.w("scalar", sT)
                                    em.w("scalar", sT)
                                    em.o("scalar", lambda e, pst=pst: e.copy(
                                        vtmp[:, :], pst[:, :]), inc=sC)
                                    consume(pst, sC)
                                    em.w("tensor", sC)
                                    for j in range(4):
                                        mm_guard(ptbf)
                                        em.o("tensor", lambda e, j=j: e.transpose(
                                            ptbf[:, :], vtmp[:, j * 128:(j + 1) * 128],
                                            ident_b[:, :]), inc=sT)
                                        em.w("scalar", sT)
                                        em.o("scalar", lambda e, r=r, j=j, h2=h2: e.copy(
                                            v_sb[:, (r * 4 + j) * 256 + h2 * 128:
                                                 (r * 4 + j) * 256 + (h2 + 1) * 128],
                                            ptbf[:, :]), inc=sC)
                                        consume(ptbf, sC)

                    em.pop()
                    em.scope("D_attn" + str(s_seq))
                    em.w("tensor", sC)
                    for h2 in range(2):
                        for qb in range(32):
                            kvlen = (qb + 1) * 128
                            nch = (kvlen + 511) // 512
                            em.w("tensor", sV)
                            for ch in range(nch):
                                cl = min(512, kvlen - ch * 512)
                                pst = ps[ch % 2]
                                mm_guard(pst)
                                em.o("tensor", lambda e, h2=h2, qb=qb, ch=ch, cl=cl, pst=pst:
                                     e.matmul(
                                         pst[:, 0:cl],
                                         qT_sb[:, h2 * SEQ + qb * 128:h2 * SEQ + (qb + 1) * 128],
                                         kT_sb[:, h2 * SEQ + ch * 512: h2 * SEQ + ch * 512 + cl],
                                         start=True, stop=True), inc=sT)
                                em.w("vector", sT)
                                is_diag = (ch == qb // 4)
                                if is_diag:
                                    qo = qb % 4
                                    em.o("vector", lambda e, ch=ch, cl=cl, qo=qo, pst=pst:
                                         e.tensor_add(
                                             sc_sb[:, ch * 512:ch * 512 + cl], pst[:, 0:cl],
                                             mask_sb[:, qo * 512:qo * 512 + cl]), inc=sV)
                                else:
                                    em.o("vector", lambda e, ch=ch, cl=cl, pst=pst:
                                         e.tensor_copy(
                                             sc_sb[:, ch * 512:ch * 512 + cl], pst[:, 0:cl]),
                                         inc=sV)
                                consume(pst, sV)
                            em.o("vector", lambda e, kvlen=kvlen: e.reduce_max(
                                st[:, 44:45], sc_sb[:, 0:kvlen], axis=AX, negate=True),
                                inc=sV)
                            em.w("scalar", sV)
                            em.o("scalar", lambda e, kvlen=kvlen: e.activation(
                                p2_sb[:, 0:kvlen], sc_sb[:, 0:kvlen], AF.Exp,
                                bias=st[:, 44:45], accum_out=st[:, 45:46]), inc=sC)
                            em.w("vector", sC)
                            em.o("vector", lambda e: e.reciprocal(st[:, 46:47], st[:, 45:46]),
                                 inc=sV)
                            em.w("vector", sV)
                            em.o("vector", lambda e, kvlen=kvlen: e.tensor_scalar(
                                p2b_sb[:, 0:kvlen], p2_sb[:, 0:kvlen], st[:, 46:47], None,
                                ALU.mult), inc=sV)
                            em.w("tensor", sV)
                            pao = ps[2]
                            mm_guard(pao)
                            for kb in range(qb + 1):
                                mm_guard(ptbf)
                                em.o("tensor", lambda e, kb=kb: e.transpose(
                                    ptbf[:, :], p2b_sb[:, kb * 128:(kb + 1) * 128],
                                    ident_b[:, :]), inc=sT)
                                em.w("scalar", sT)
                                em.o("scalar", lambda e: e.copy(pT_sb[:, :], ptbf[:, :]),
                                     inc=sC)
                                consume(ptbf, sC)
                                em.w("tensor", sC)
                                em.o("tensor", lambda e, kb=kb, h2=h2, qb=qb, pao=pao: e.matmul(
                                    pao[:, 0:128],
                                    v_sb[:, kb * 256 + h2 * 128: kb * 256 + (h2 + 1) * 128],
                                    pT_sb[:, :],
                                    start=(kb == 0), stop=(kb == qb)),
                                    inc=(sT if kb == qb else None))
                            em.w("scalar", sT)
                            em.o("scalar", lambda e, h2=h2, qb=qb, pao=pao: e.copy(
                                aoT_sb[:, h2 * SEQ + qb * 128: h2 * SEQ + (qb + 1) * 128],
                                pao[:, 0:128]), inc=sC)
                            consume(pao, sC)

                    if s_seq == 0:
                        for nm, buf in (("qt", qT_sb), ("kt", kT_sb), ("vv", v_sb),
                                        ("aot", aoT_sb)):
                            if nm in dbg_outs:
                                for j in range(4):
                                    em.w("vector", sC)
                                    em.o("vector", lambda e, buf=buf, j=j: e.tensor_copy(
                                        sqd[:, :], buf[:, j * 2048:(j + 1) * 2048]),
                                        inc=sV)
                                    em.w("sync", sV)
                                    em.dma("sync", lambda e, nm=nm, j=j: e.dma_start(
                                        out=dbg_outs[nm][:, j * 2048:(j + 1) * 2048],
                                        in_=sqd[:, :]), qS)
                                    em.w("vector", qS)
                    em.pop()
                    em.scope("D_wo" + str(s_seq))
                    em.w("tensor", sC)
                    for qb in range(32):
                        for dch in range(4):
                            pst = ps[dch % 2]
                            mm_guard(pst)
                            for h2 in range(2):
                                em.o("tensor", lambda e, qb=qb, dch=dch, h2=h2, pst=pst:
                                     e.matmul(
                                         pst[:, :],
                                         aoT_sb[:, h2 * SEQ + qb * 128: h2 * SEQ + (qb + 1) * 128],
                                         wo_b[:, h2 * DM + dch * 512: h2 * DM + (dch + 1) * 512],
                                         start=(h2 == 0), stop=(h2 == 1)),
                                     inc=(sT if h2 == 1 else None))
                            em.w("scalar", sT)
                            em.w("scalar", qS)
                            em.o("scalar", lambda e, dch=dch, pst=pst: e.copy(
                                arstg[:, dch * 512:(dch + 1) * 512], pst[:, :]),
                                inc=sC)
                            consume(pst, sC)
                        em.w("sync", sC)
                        em.dma("sync", lambda e, s=s_seq, qb=qb: e.dma_start(
                            out=ars_in[s].ap()[qb * 128:(qb + 1) * 128, :],
                            in_=arstg[:, :]), qS)
                    em.w("gpsimd", qS)
                    em.w("gpsimd", cc)
                    em.o("gpsimd", lambda e, s=s_seq: e.collective_compute(
                        "ReduceScatter", ALU.add, replica_groups=[list(range(NCORE))],
                        ins=[ars_in[s].ap()], outs=[ars_out[s].ap()]), inc=cc)
                    cc_ars = cc.n
                    em.pop()
                    em.scope("D_x2i" + str(s_seq))
                    for lb in range(4):
                        em.w("sync", cc, cc_ars)
                        em.w("sync", sV)
                        em.dma("sync", lambda e, s=s_seq, lb=lb: e.dma_start(
                            out=ars_sb[:, :],
                            in_=ars_out[s].ap()[lb * 128:(lb + 1) * 128, :]), qS)
                        em.dma("sync", lambda e, s=s_seq, lb=lb: e.dma_start(
                            out=lx1t[:, :],
                            in_=sp_lx1.ap()[s * 512 + lb * 128: s * 512 + (lb + 1) * 128, :]), qS)
                        em.w("vector", qS)
                        em.o("vector", lambda e, lb=lb: e.tensor_add(
                            x2it[:, :], lx1t[:, :], ars_sb[:, :]), inc=sV)
                        em.w("sync", sV)
                        em.dma("sync", lambda e, s=s_seq, lb=lb: e.dma_start(
                            out=sp_x2i.ap()[s * 512 + lb * 128: s * 512 + (lb + 1) * 128, :],
                            in_=x2it[:, :]), qS)
                        if "x2i" in dbg_outs:
                            em.dma("sync", lambda e, s=s_seq, lb=lb: e.dma_start(
                                out=dbg_outs["x2i"][s * 512 + lb * 128: s * 512 + (lb + 1) * 128, :],
                                in_=x2it[:, :]), qS)
                    em.pop()
                    em.barrier(BAR)
                if "E" in phases:
                    _phase_E(nc, em, BAR, s_seq, sp_x2i, dict(
                        l2row=l2row, gwT=gwT, egT=egT, euT=euT, edT=edT,
                        sgT=sgT, suT=suT, sdT=sdT, kcol=kcol,
                        tt_in=tt_in, tt_out=tt_out, cw_in=cw_in, cw_out=cw_out,
                        mrs_in=mrs_in, mrs_out=mrs_out, sp_x1=sp_x1,
                        out_own=out_own, dbg_outs=dbg_outs,
                        ident_f=ident_f, ident_b=ident_b, st=st,
                        ps=ps, ptA=ptA, ptB=ptB, ptbf=ptbf,
                        qS=qS, qG=qG, cc=cc, sT=sT, sV=sV, sC=sC,
                        mm_guard=mm_guard, consume=consume,
                        ln_stats=ln_stats, rms_stats=rms_stats))
                em.barrier(BAR)

    em.w("sync", qG)
    em.w("sync", cc)
    em.w("sync", qS)

    def _replay(eng, ops):
        stack = []
        for item in ops:
            if callable(item):
                item(eng)
            elif item[0] == "scope":
                cm = nc.named_scope(item[1])
                cm.__enter__()
                stack.append(cm)
            else:
                stack.pop().__exit__(None, None, None)
        while stack:
            stack.pop().__exit__(None, None, None)
    with allow_ici_all_to_all():
        with nc.Block() as blk:
            for name in ENGS:
                getattr(blk, name)(lambda eng, ops=tuple(em.q[name]): _replay(eng, ops))
    stk.close()
    return nc, dbg_outs


def _phase_E(nc, em, BAR, s_seq, sp_x2i, z):
    qS, qG, cc, sT, sV, sC = z["qS"], z["qG"], z["cc"], z["sT"], z["sV"], z["sC"]
    mm_guard, consume = z["mm_guard"], z["consume"]
    ps, ptA, ptB, ptbf = z["ps"], z["ptA"], z["ptB"], z["ptbf"]
    st = z["st"]
    ident_f, ident_b = z["ident_f"], z["ident_b"]
    dbg_outs = z["dbg_outs"]
    ln_stats, rms_stats = z["ln_stats"], z["rms_stats"]

    with ExitStack() as pk:
        A = lambda n, s, d: pk.enter_context(nc.sbuf_tensor(n + "E" + str(s_seq), s, d))
        l2_rep = A("l2_rep", [128, DM], F32)
        x2e = A("x2e", [128, DM], F32)
        te = A("te", [128, DM], F32)
        tb = A("tb", [128, DM], BF16)
        sqe = A("sqe", [128, DM], F32)
        tT_b = A("tT_b", [128, NDB * 512], BF16)
        gw_b = A("gw_b", [128, NDB * NE], BF16)
        glog = A("glog", [128, NE], F32)
        mx8 = A("mx8", [128, 8], F32)
        ind = A("ind", [128, NE], F32)
        wsel = A("wsel", [128, NE], F32)
        cwt = A("cwt", [128, NE], F32)
        cwT_sb = A("cwT_sb", [128, OWN], F32)

        em.dma("sync", lambda e: e.dma_start(
            out=l2_rep[:, :], in_=z["l2row"].broadcast_to([128, DM])), qS)
        em.dma("gpsimd", lambda e: e.dma_start(
            out=gw_b[:, :].rearrange("p (b n) -> p b n", b=NDB),
            in_=z["gwT"].rearrange("(b p) n -> p b n", p=128)), qG)

        for lb in range(4):
            em.w("sync", sV)
            em.w("sync", sC)
            em.dma("sync", lambda e, s=s_seq, lb=lb: e.dma_start(
                out=x2e[:, :],
                in_=sp_x2i.ap()[s * 512 + lb * 128: s * 512 + (lb + 1) * 128, :]), qS)
            em.w("vector", qS)
            em.w("scalar", qS)
            em.w("vector", sT)
            rms_stats(x2e, DM, 28, EPS_RMS, sqe)
            em.o("vector", lambda e: e.tensor_scalar(
                sqe[:, :], x2e[:, :], st[:, 29:30], None, ALU.mult))
            em.o("vector", lambda e: e.tensor_mul(te[:, :], sqe[:, :], l2_rep[:, :]))
            em.o("vector", lambda e: e.tensor_copy(tb[:, :], te[:, :]), inc=sV)
            em.w("tensor", sV)
            for db in range(NDB):
                mm_guard(ptbf)
                em.o("tensor", lambda e, db=db: e.transpose(
                    ptbf[:, :], tb[:, db * 128:(db + 1) * 128], ident_b[:, :]), inc=sT)
                em.w("scalar", sT)
                em.o("scalar", lambda e, db=db, lb=lb: e.copy(
                    tT_b[:, db * 512 + lb * 128: db * 512 + (lb + 1) * 128],
                    ptbf[:, :]), inc=sC)
                consume(ptbf, sC)
            em.w("tensor", sC)
            em.w("tensor", qG)
            mm_guard(ptA)
            for db in range(NDB):
                em.o("tensor", lambda e, db=db, lb=lb: e.matmul(
                    ptA[:, 0:NE],
                    tT_b[:, db * 512 + lb * 128: db * 512 + (lb + 1) * 128],
                    gw_b[:, db * NE:(db + 1) * NE],
                    start=(db == 0), stop=(db == NDB - 1)),
                    inc=(sT if db == NDB - 1 else None))
            em.w("vector", sT)
            em.o("vector", lambda e: e.tensor_copy(glog[:, :], ptA[:, 0:NE]), inc=sV)
            consume(ptA, sV)
            em.w("vector", sV)
            if "gl" in dbg_outs:
                em.w("sync", sV)
                em.dma("sync", lambda e, s=s_seq, lb=lb: e.dma_start(
                    out=dbg_outs["gl"][s * 512 + lb * 128: s * 512 + (lb + 1) * 128, :],
                    in_=glog[:, :]), qS)
            em.o("vector", lambda e: e.max(mx8[:, :], glog[:, :]), inc=sV)
            em.w("vector", sV)
            em.o("vector", lambda e: e.tensor_scalar(
                st[:, 40:41], mx8[:, 0:1], -1.0, None, ALU.mult), inc=sV)
            em.w("vector", sV)
            em.o("vector", lambda e: e.tensor_scalar(
                ind[:, :], glog[:, :], mx8[:, 5:6], None, ALU.is_ge), inc=sV)
            em.w("scalar", sV)
            em.o("scalar", lambda e: e.activation(
                wsel[:, :], glog[:, :], AF.Exp, bias=st[:, 40:41]), inc=sC)
            em.w("vector", sC)
            em.o("vector", lambda e: e.tensor_mul(glog[:, :], wsel[:, :], ind[:, :]),
                 inc=sV)
            em.w("vector", sV)
            em.o("vector", lambda e: e.reduce_sum(st[:, 41:42], glog[:, :], axis=AX),
                 inc=sV)
            em.w("vector", sV)
            em.o("vector", lambda e: e.reciprocal(st[:, 42:43], st[:, 41:42]), inc=sV)
            em.w("vector", sV)
            em.o("vector", lambda e: e.tensor_scalar(
                cwt[:, :], glog[:, :], st[:, 42:43], None, ALU.mult), inc=sV)
            if "cw" in dbg_outs:
                em.w("sync", sV)
                em.dma("sync", lambda e, s=s_seq, lb=lb: e.dma_start(
                    out=dbg_outs["cw"][s * 512 + lb * 128: s * 512 + (lb + 1) * 128, :],
                    in_=cwt[:, :]), qS)
            em.w("tensor", sV)
            mm_guard(ptB)
            em.o("tensor", lambda e: e.transpose(
                ptB[0:NE, :], cwt[:, :], ident_f[:, :]), inc=sT)
            em.w("vector", sT)
            em.o("vector", lambda e, lb=lb: e.tensor_copy(
                cwT_sb[0:NE, lb * 128:(lb + 1) * 128], ptB[0:NE, 0:128]), inc=sV)
            consume(ptB, sV)
        em.w("sync", sC)
        em.w("sync", sV)
        em.dma("sync", lambda e, s=s_seq: e.dma_start(
            out=z["tt_in"][s].ap().rearrange("(b p) n -> p b n", p=128),
            in_=tT_b[:, :].rearrange("p (b n) -> p b n", b=NDB)), qS)
        em.dma("sync", lambda e, s=s_seq: e.dma_start(
            out=z["cw_in"][s].ap(), in_=cwT_sb[0:NE, :]), qS)
        em.w("gpsimd", qS)
        em.w("gpsimd", cc)
        em.o("gpsimd", lambda e, s=s_seq: e.collective_compute(
            "AllGather", ALU.bypass, replica_groups=[list(range(NCORE))],
            ins=[z["tt_in"][s].ap()], outs=[z["tt_out"][s].ap()]), inc=cc)
        with allow_ici_all_to_all():
            em.w("gpsimd", cc)
            em.o("gpsimd", lambda e, s=s_seq: e.collective_compute(
                "AllToAll", ALU.bypass, replica_groups=[list(range(NCORE))],
                ins=[z["cw_in"][s].ap()], outs=[z["cw_out"][s].ap()]), inc=cc)
        cc_moe = cc.n

        with ExitStack() as rk:
            B = lambda n, s, d: rk.enter_context(nc.sbuf_tensor(n + "R" + str(s_seq), s, d))
            tT_all = B("tT_all", [128, NDB * 1024], BF16)
            hh_w = B("hh_w", [128, 2 * NIB * 1024], BF16)
            eg_b = B("eg_b", [128, NDB * 128], BF16)
            eu_b = B("eu_b", [128, NDB * 128], BF16)
            ed_b = B("ed_b", [128, 2 * EI], BF16)
            cwrep = B("cwrep", [128, 1024], F32)
            sil_b = B("sil_b", [128, 512], BF16)
            uw_b = B("uw_b", [128, 512], BF16)
            mstg = B("mstg", [128, 1024], BF16)

            tto = z["tt_out"][s_seq].ap().rearrange("(r b p) n -> p b r n", p=128, b=NDB)
            cwo = z["cw_out"][s_seq].ap().rearrange("(r e) n -> e r n", e=2)
            for qt in range(4):
                em.w("sync", cc, cc_moe)
                em.w("sync", sT)
                for rr in range(2):
                    em.dma("sync", lambda e, qt=qt, rr=rr: e.dma_start(
                        out=tT_all[:, :].rearrange("p (b m) -> p b m", b=NDB)
                        [:, :, rr * 512:(rr + 1) * 512],
                        in_=tto[:, :, 2 * qt + rr, :]), qS)
                for e2 in range(2):
                    em.w("sync", sV)
                    em.dma("sync", lambda e, e2=e2, qt=qt: e.dma_start(
                        out=cwrep[:, :].rearrange("p (r n) -> p r n", r=2),
                        in_=cwo[e2:e2 + 1, 2 * qt:2 * qt + 2, :]
                        .broadcast_to([128, 2, 512])), qS)
                    for ib in range(NIB):
                        em.w("gpsimd", sT)
                        em.dma("gpsimd", lambda e, e2=e2, ib=ib: e.dma_start(
                            out=eg_b[:, :].rearrange("p (b n) -> p b n", b=NDB),
                            in_=z["egT"][e2:e2 + 1, :, ib * 128:(ib + 1) * 128]
                            .rearrange("o (b p) n -> p b (o n)", p=128)), qG)
                        em.dma("gpsimd", lambda e, e2=e2, ib=ib: e.dma_start(
                            out=eu_b[:, :].rearrange("p (b n) -> p b n", b=NDB),
                            in_=z["euT"][e2:e2 + 1, :, ib * 128:(ib + 1) * 128]
                            .rearrange("o (b p) n -> p b (o n)", p=128)), qG)
                        em.w("tensor", qG)
                        em.w("tensor", qS)
                        for lch in range(2):
                            pg, pu = ps[0], ps[1]
                            mm_guard(pg)
                            for db in range(NDB):
                                em.o("tensor", lambda e, db=db, lch=lch, pg=pg: e.matmul(
                                    pg[:, :], eg_b[:, db * 128:(db + 1) * 128],
                                    tT_all[:, db * 1024 + lch * 512: db * 1024 + (lch + 1) * 512],
                                    start=(db == 0), stop=(db == NDB - 1)),
                                    inc=(sT if db == NDB - 1 else None))
                            mm_guard(pu)
                            for db in range(NDB):
                                em.o("tensor", lambda e, db=db, lch=lch, pu=pu: e.matmul(
                                    pu[:, :], eu_b[:, db * 128:(db + 1) * 128],
                                    tT_all[:, db * 1024 + lch * 512: db * 1024 + (lch + 1) * 512],
                                    start=(db == 0), stop=(db == NDB - 1)),
                                    inc=(sT if db == NDB - 1 else None))
                            em.w("scalar", sT)
                            em.w("scalar", sV)
                            em.o("scalar", lambda e, pg=pg: e.activation(
                                sil_b[:, :], pg[:, :], AF.Silu), inc=sC)
                            consume(pg, sC)
                            em.w("vector", sT)
                            em.w("vector", sC)
                            em.o("vector", lambda e, lch=lch, pu=pu: e.scalar_tensor_tensor(
                                uw_b[:, :], pu[:, :], 1.0,
                                cwrep[:, lch * 512:(lch + 1) * 512],
                                ALU.mult, ALU.mult), inc=sV)
                            consume(pu, sV)
                            em.o("vector", lambda e, e2=e2, ib=ib, lch=lch: e.tensor_mul(
                                hh_w[:, (e2 * NIB + ib) * 1024 + lch * 512:
                                     (e2 * NIB + ib) * 1024 + (lch + 1) * 512],
                                uw_b[:, :], sil_b[:, :]), inc=sV)
                for dob in range(NDB):
                    em.w("gpsimd", sT)
                    for ee in range(2):
                        em.dma("gpsimd", lambda e, dob=dob, ee=ee: e.dma_start(
                            out=ed_b[:, ee * EI:(ee + 1) * EI]
                            .rearrange("p (b n) -> p b n", b=NIB),
                            in_=z["edT"][ee:ee + 1, :, dob * 128:(dob + 1) * 128]
                            .rearrange("o (b p) n -> p b (o n)", p=128)), qG)
                    em.w("tensor", qG)
                    em.w("tensor", sV)
                    for lch in range(2):
                        pst = ps[2 + lch]
                        mm_guard(pst)
                        for e2 in range(2):
                            for ib in range(NIB):
                                em.o("tensor", lambda e, e2=e2, ib=ib, lch=lch, pst=pst:
                                     e.matmul(
                                         pst[:, :],
                                         ed_b[:, e2 * EI + ib * 128: e2 * EI + (ib + 1) * 128],
                                         hh_w[:, (e2 * NIB + ib) * 1024 + lch * 512:
                                              (e2 * NIB + ib) * 1024 + (lch + 1) * 512],
                                         start=(e2 == 0 and ib == 0),
                                         stop=(e2 == 1 and ib == NIB - 1)),
                                     inc=(sT if (e2 == 1 and ib == NIB - 1) else None))
                        em.w("vector", sT)
                        em.w("vector", qS)
                        em.o("vector", lambda e, lch=lch, pst=pst: e.tensor_copy(
                            mstg[:, lch * 512:(lch + 1) * 512], pst[:, :]), inc=sV)
                        consume(pst, sV)
                    em.w("sync", sV)
                    em.dma("sync", lambda e, s=s_seq, dob=dob, qt=qt: e.dma_start(
                        out=z["mrs_in"][s].ap().rearrange("(j m) n -> j m n", m=DM)
                        [2 * qt:2 * qt + 2, dob * 128:(dob + 1) * 128, :]
                        .rearrange("j m n -> m j n"),
                        in_=mstg[:, :].rearrange("p (j n) -> p j n", j=2)), qS)
            em.w("gpsimd", qS)
            em.w("gpsimd", cc)
            em.o("gpsimd", lambda e, s=s_seq: e.collective_compute(
                "ReduceScatter", ALU.add, replica_groups=[list(range(NCORE))],
                ins=[z["mrs_in"][s].ap()], outs=[z["mrs_out"][s].ap()]), inc=cc)
        cc_mrs = cc.n

        with ExitStack() as hk:
            C = lambda n, s, d: hk.enter_context(nc.sbuf_tensor(n + "S" + str(s_seq), s, d))
            sg_b = C("sg_b", [128, NDB * 128], BF16)
            su_b = C("su_b", [128, NDB * 128], BF16)
            sd_b = C("sd_b", [128, DM], BF16)
            hhs = C("hhs", [128, NSB * 512], BF16)
            ssil = C("ssil", [128, 512], BF16)
            comb = C("comb", [128, 4 * DM], F32)
            x2ic = C("x2ic", [128, 512], F32)
            rt_sb = C("rt_sb", [128, DM], BF16)
            mrs_sb = C("mrs_sb", [128, DM], BF16)
            ffn_t = C("ffn_t", [128, DM], F32)
            monl2 = C("monl2", [128, DM], F32)
            x1e = C("x1e", [128, DM], F32)
            sqe2 = C("sqe2", [128, DM], F32)

            for sib in range(NSB):
                em.w("gpsimd", sT)
                em.dma("gpsimd", lambda e, sib=sib: e.dma_start(
                    out=sg_b[:, :].rearrange("p (b n) -> p b n", b=NDB),
                    in_=z["sgT"][:, sib * 128:(sib + 1) * 128]
                    .rearrange("(b p) n -> p b n", p=128)), qG)
                em.dma("gpsimd", lambda e, sib=sib: e.dma_start(
                    out=su_b[:, :].rearrange("p (b n) -> p b n", b=NDB),
                    in_=z["suT"][:, sib * 128:(sib + 1) * 128]
                    .rearrange("(b p) n -> p b n", p=128)), qG)
                em.w("tensor", qG)
                pg, pu = ps[0], ps[1]
                mm_guard(pg)
                for db in range(NDB):
                    em.o("tensor", lambda e, db=db, pg=pg: e.matmul(
                        pg[:, :], sg_b[:, db * 128:(db + 1) * 128],
                        tT_b[:, db * 512:(db + 1) * 512],
                        start=(db == 0), stop=(db == NDB - 1)),
                        inc=(sT if db == NDB - 1 else None))
                mm_guard(pu)
                for db in range(NDB):
                    em.o("tensor", lambda e, db=db, pu=pu: e.matmul(
                        pu[:, :], su_b[:, db * 128:(db + 1) * 128],
                        tT_b[:, db * 512:(db + 1) * 512],
                        start=(db == 0), stop=(db == NDB - 1)),
                        inc=(sT if db == NDB - 1 else None))
                em.w("scalar", sT)
                em.w("scalar", sV)
                em.o("scalar", lambda e, pg=pg: e.activation(
                    ssil[:, :], pg[:, :], AF.Silu), inc=sC)
                consume(pg, sC)
                em.w("vector", sT)
                em.w("vector", sC)
                em.o("vector", lambda e, sib=sib, pu=pu: e.scalar_tensor_tensor(
                    hhs[:, sib * 512:(sib + 1) * 512], pu[:, :], 1.0,
                    ssil[:, :], ALU.mult, ALU.mult), inc=sV)
                consume(pu, sV)
            if "hhs" in dbg_outs:
                for j in range(4):
                    em.w("vector", sV)
                    em.o("vector", lambda e, j=j: e.tensor_copy(
                        sqe2[:, 0:512], hhs[:, j * 512:(j + 1) * 512]), inc=sV)
                    em.w("sync", sV)
                    em.dma("sync", lambda e, s=s_seq, j=j: e.dma_start(
                        out=dbg_outs["hhs"][s * 128:(s + 1) * 128, j * 512:(j + 1) * 512],
                        in_=sqe2[:, 0:512]), qS)
                    em.w("vector", qS)
            for dq in range(4):
                for sib in range(NSB):
                    em.w("gpsimd", sT)
                    em.dma("gpsimd", lambda e, sib=sib, dq=dq: e.dma_start(
                        out=sd_b[:, 0:512],
                        in_=z["sdT"][sib * 128:(sib + 1) * 128, dq * 512:(dq + 1) * 512]), qG)
                    em.w("tensor", qG)
                    em.w("tensor", sV)
                    for lb in range(4):
                        pst = ps[lb]
                        if sib == 0:
                            mm_guard(pst)
                        em.o("tensor", lambda e, sib=sib, lb=lb, pst=pst: e.matmul(
                            pst[:, :], hhs[:, sib * 512 + lb * 128: sib * 512 + (lb + 1) * 128],
                            sd_b[:, 0:512],
                            start=(sib == 0), stop=(sib == NSB - 1)),
                            inc=(sT if lb == 3 else None))
                em.w("vector", sT)
                for lb in range(4):
                    em.w("sync", sV)
                    em.dma("sync", lambda e, s=s_seq, lb=lb, dq=dq: e.dma_start(
                        out=x2ic[:, :],
                        in_=sp_x2i.ap()[s * 512 + lb * 128: s * 512 + (lb + 1) * 128,
                                        dq * 512:(dq + 1) * 512]), qS)
                    em.w("vector", qS)
                    em.o("vector", lambda e, lb=lb, dq=dq: e.tensor_add(
                        comb[:, lb * DM + dq * 512: lb * DM + (dq + 1) * 512],
                        x2ic[:, :], ps[lb][:, :]), inc=sV)
                    consume(ps[lb], sV)
            em.w("sync", cc, cc_mrs)
            for lb in range(4):
                em.w("sync", sV)
                em.dma("sync", lambda e, s=s_seq, lb=lb: e.dma_start(
                    out=mrs_sb[:, :].rearrange("p (b n) -> p b n", b=NDB),
                    in_=z["mrs_out"][s].ap().rearrange("(b p) n -> p b n", p=128)
                    [:, :, lb * 128:(lb + 1) * 128]), qS)
                em.w("tensor", qS)
                em.w("tensor", sC)
                for db in range(NDB):
                    mm_guard(ptbf)
                    em.o("tensor", lambda e, db=db: e.transpose(
                        ptbf[:, :], mrs_sb[:, db * 128:(db + 1) * 128], ident_b[:, :]),
                        inc=sT)
                    em.w("scalar", sT)
                    em.o("scalar", lambda e, db=db: e.copy(
                        rt_sb[:, db * 128:(db + 1) * 128], ptbf[:, :]), inc=sC)
                    consume(ptbf, sC)
                if "shr" in dbg_outs:
                    em.w("sync", sV)
                    em.dma("sync", lambda e, s=s_seq, lb=lb: e.dma_start(
                        out=dbg_outs["shr"][s * 512 + lb * 128: s * 512 + (lb + 1) * 128, :],
                        in_=comb[:, lb * DM:(lb + 1) * DM]), qS)
                em.w("vector", sC)
                em.o("vector", lambda e, lb=lb: e.tensor_add(
                    ffn_t[:, :], comb[:, lb * DM:(lb + 1) * DM], rt_sb[:, :]), inc=sV)
                if "routed" in dbg_outs:
                    em.w("sync", sV)
                    em.dma("sync", lambda e, s=s_seq, lb=lb: e.dma_start(
                        out=dbg_outs["routed"][s * 512 + lb * 128: s * 512 + (lb + 1) * 128, :],
                        in_=ffn_t[:, :]), qS)
                ln_stats(ffn_t, DM, 34, EPS_LN, sqe2)
                em.o("vector", lambda e: e.tensor_scalar(
                    monl2[:, :], ffn_t[:, :], st[:, 36:37], st[:, 39:40],
                    ALU.subtract, ALU.mult), inc=sV)
                em.w("sync", sV)
                em.dma("sync", lambda e, s=s_seq, lb=lb: e.dma_start(
                    out=x1e[:, :],
                    in_=z["sp_x1"].ap()[s * 512 + lb * 128: s * 512 + (lb + 1) * 128, :]), qS)
                em.w("vector", qS)
                em.o("vector", lambda e: e.tensor_add(sqe2[:, :], monl2[:, :], x1e[:, :]),
                     inc=sV)
                em.w("sync", sV)
                em.dma("sync", lambda e, s=s_seq, lb=lb: e.dma_start(
                    out=z["out_own"][s * 512 + lb * 128: s * 512 + (lb + 1) * 128, :],
                    in_=sqe2[:, :]), qS)


def make_core_inputs(inputs, c):
    T = lambda a: np.ascontiguousarray(a, dtype=np.float32)
    x = inputs["L0_attn_x"]
    d = {}
    d["x_own"] = T(np.concatenate(
        [x[0, 512 * c:512 * (c + 1)], x[1, 512 * c:512 * (c + 1)]], axis=0))
    d["A_slc"] = T(inputs["L0_attn_A_log"][256 * c:256 * (c + 1)])
    d["Dcol"] = T(np.asarray(inputs["L0_attn_D_param"]).reshape(NDB, 128).T)
    d["nwrow"] = T(np.asarray(inputs["L0_attn_norm_w"])[None, :])
    d["nbrow"] = T(np.asarray(inputs["L0_attn_norm_b"])[None, :])
    d["dtbrow"] = T(np.asarray(inputs["L0_attn_dt_proj_b"])[None, :])
    d["l1row"] = T(np.asarray(inputs["L0_ffn_ln1_w"])[None, :])
    d["l2row"] = T(np.asarray(inputs["L0_ffn_ln2_w"])[None, :])
    d["xpwT"] = T(np.asarray(inputs["L0_attn_x_proj_w"]).T)
    d["dtwT"] = T(np.asarray(inputs["L0_attn_dt_proj_w"]).T)
    d["opwT"] = T(np.asarray(inputs["L0_attn_out_proj_w"]).T)
    d["wq_own"] = T(np.asarray(inputs["L0_ffn_wq"])[:, 256 * c:256 * (c + 1)])
    d["wk_own"] = T(np.asarray(inputs["L0_ffn_wk"])[:, 256 * c:256 * (c + 1)])
    d["wv_own"] = T(np.asarray(inputs["L0_ffn_wv"])[:, 256 * c:256 * (c + 1)])
    d["wo_own"] = T(np.asarray(inputs["L0_ffn_wo"])[256 * c:256 * (c + 1), :])
    d["gwT"] = T(np.asarray(inputs["L0_ffn_gate_w"]).T)
    eg = np.asarray(inputs["L0_ffn_eg_w"])
    eu = np.asarray(inputs["L0_ffn_eu_w"])
    ed = np.asarray(inputs["L0_ffn_ed_w"])
    d["egT"] = T(np.stack([eg[2 * c].T, eg[2 * c + 1].T]))
    d["euT"] = T(np.stack([eu[2 * c].T, eu[2 * c + 1].T]))
    d["edT"] = T(np.stack([ed[2 * c].T, ed[2 * c + 1].T]))
    d["sgT"] = T(np.asarray(inputs["L0_ffn_sg_w"]).T)
    d["suT"] = T(np.asarray(inputs["L0_ffn_su_w"]).T)
    d["sdT"] = T(np.asarray(inputs["L0_ffn_sd_w"]).T)
    return d


def kernel(**inputs):
    nc, _ = build()
    in_maps = [make_core_inputs(inputs, c) for c in range(NCORE)]
    res = run_bass_kernel_spmd(nc, in_maps, core_ids=list(range(NCORE)))
    out = np.zeros((NSEQ, SEQ, DM), np.float32)
    for c in range(NCORE):
        o = res.results[c]["out_own"]
        out[0, 512 * c:512 * (c + 1)] = o[0:OWN]
        out[1, 512 * c:512 * (c + 1)] = o[OWN:OWNT]
    return out


def run_debug(inputs, phases, dbg):
    nc, dbg_outs = build(phases=phases, dbg=dbg)
    in_maps = [make_core_inputs(inputs, c) for c in range(NCORE)]
    res = run_bass_kernel_spmd(nc, in_maps, core_ids=list(range(NCORE)))
    return res


# version 03
# speedup vs baseline: 1.3878x; 1.1778x over previous
import math
import numpy as np
import concourse.bass as bass
import concourse.mybir as mybir
from concourse.bass_utils import run_bass_kernel_spmd
from concourse.aot_env import allow_ici_all_to_all
from contextlib import ExitStack

F32 = mybir.dt.float32
BF16 = mybir.dt.bfloat16
FP16 = mybir.dt.float16
AF = mybir.ActivationFunctionType
ALU = mybir.AluOpType
AX = mybir.AxisListType.X

NCORE = 8
SEQ = 4096
NSEQ = 2
DM = 2048
OWN = 512
OWNT = NSEQ * OWN
DSL = 256
NST = 64
RK = 128
HD = 128
NE = 16
EI = 1408
SI = 2816
EPS_LN = 1e-5
EPS_RMS = 1e-6
NDB = DM // 128
NIB = EI // 128
NSB = SI // 128
QSCALE = 1.0 / math.sqrt(HD)

ENGS = ("sync", "gpsimd", "tensor", "vector", "scalar")


class S:
    def __init__(self, nc, name):
        self.sem = nc.alloc_semaphore(name)
        self.n = 0
        self.name = name


class Em:
    def __init__(self):
        self.q = {k: [] for k in ENGS}

    def scope(self, name):
        for e in ENGS:
            self.q[e].append(("scope", name))

    def pop(self):
        for e in ENGS:
            self.q[e].append(("pop", None))

    def w(self, eng, sem, val=None):
        v = sem.n if val is None else val
        if v > 0:
            self.q[eng].append(lambda e, s=sem.sem, v=v: e.wait_ge(s, v))

    def o(self, eng, f, inc=None, inck=1):
        if inc is None:
            self.q[eng].append(f)
        else:
            inc.n += inck
            self.q[eng].append(lambda e, f=f, s=inc.sem, k=inck: f(e).then_inc(s, k))

    def dma(self, eng, f, sem):
        self.o(eng, f, inc=sem, inck=16)

    def barrier(self, bars, engs=ENGS):
        marks = {}
        for e in engs:
            self.o(e, lambda x: x.drain(), inc=bars[e])
            marks[e] = bars[e].n
        for e in engs:
            for e2 in engs:
                if e2 != e:
                    self.w(e, bars[e2], marks[e2])


def build(phases="ABCDE", dbg=()):
    nc = bass.Bass("TRN2", target_bir_lowering=False)

    def inp(name, shape, dtype=F32):
        return nc.dram_tensor(name, shape, dtype, kind="ExternalInput").ap()

    x_own = inp("x_own", [OWNT, DM])
    A_slc = inp("A_slc", [DSL, NST])
    Dcol = inp("Dcol", [128, NDB])
    nwrow = inp("nwrow", [1, DM])
    nbrow = inp("nbrow", [1, DM])
    dtbrow = inp("dtbrow", [1, DM])
    l1row = inp("l1row", [1, DM])
    l2row = inp("l2row", [1, DM])
    xpwT = inp("xpwT", [DM, 2 * NST + RK])
    dtwT = inp("dtwT", [RK, DM])
    opwT = inp("opwT", [DM, DM])
    wq_own = inp("wq_own", [DM, 2 * HD])
    wk_own = inp("wk_own", [DM, 2 * HD])
    wv_own = inp("wv_own", [DM, 2 * HD])
    wo_own = inp("wo_own", [2 * HD, DM])
    gwT = inp("gwT", [DM, NE])
    egT = inp("egT", [2, DM, EI])
    euT = inp("euT", [2, DM, EI])
    edT = inp("edT", [2, EI, DM])
    sgT = inp("sgT", [DM, SI])
    suT = inp("suT", [DM, SI])
    sdT = inp("sdT", [SI, DM])

    out_own = nc.dram_tensor("out_own", [OWNT, DM], F32, kind="ExternalOutput").ap()
    dbg_outs = {}
    for nm, shp in (("lx", [OWNT, DM]), ("xn", [OWNT, DM]), ("dt", [OWNT, DM]),
                    ("y", [NSEQ * DSL, SEQ]), ("x1", [OWNT, DM]),
                    ("hln", [OWNT, DM]), ("attn", [OWNT, DM]),
                    ("cw", [OWNT, NE]), ("x2i", [OWNT, DM]),
                    ("qt", [128, 2 * SEQ]), ("kt", [128, 2 * SEQ]),
                    ("htb", [128, NDB * 512]), ("hti", [128, 512]), ("hbt", [128, DM]), ("stg", [128, DM]), ("gl", [OWNT, NE]), ("shr", [OWNT, DM]), ("hhs", [NSEQ * 128, 2048]),
                    ("vv", [128, 32 * 256]), ("aot", [128, 2 * SEQ]),
                    ("routed", [OWNT, DM]), ("shared", [OWNT, DM])):
        if nm in dbg:
            dbg_outs[nm] = nc.dram_tensor("dbg_" + nm, shp, F32,
                                          kind="ExternalOutput").ap()

    e1_in = nc.dram_tensor("e1_in", [NCORE * 640, 1024], FP16)
    e1_out = nc.dram_tensor("e1_out", [NCORE * 640, 1024], FP16)
    e2_in = [nc.dram_tensor(f"e2_in{s}", [DM, OWN], BF16) for s in range(NSEQ)]
    e2_out = [nc.dram_tensor(f"e2_out{s}", [DM, OWN], BF16) for s in range(NSEQ)]
    ht_in = [nc.dram_tensor(f"ht_in{s}", [DM, OWN], BF16) for s in range(NSEQ)]
    ht_out = [nc.dram_tensor(f"ht_out{s}", [NCORE * DM, OWN], BF16) for s in range(NSEQ)]
    ars_in = [nc.dram_tensor(f"ars_in{s}", [SEQ, DM], BF16) for s in range(NSEQ)]
    ars_out = [nc.dram_tensor(f"ars_out{s}", [OWN, DM], BF16) for s in range(NSEQ)]
    tt_in = [nc.dram_tensor(f"tt_in{s}", [DM, OWN], BF16) for s in range(NSEQ)]
    tt_out = [nc.dram_tensor(f"tt_out{s}", [NCORE * DM, OWN], BF16) for s in range(NSEQ)]
    cw_in = [nc.dram_tensor(f"cw_in{s}", [NE, OWN], F32) for s in range(NSEQ)]
    cw_out = [nc.dram_tensor(f"cw_out{s}", [NE, OWN], F32) for s in range(NSEQ)]
    mrs_in = [nc.dram_tensor(f"mrs_in{s}", [NCORE * DM, OWN], BF16) for s in range(NSEQ)]
    mrs_out = [nc.dram_tensor(f"mrs_out{s}", [DM, OWN], BF16) for s in range(NSEQ)]
    sp_lx = nc.dram_tensor("sp_lx", [OWNT, DM], F32)
    sp_x1 = nc.dram_tensor("sp_x1", [OWNT, DM], F32)
    sp_lx1 = nc.dram_tensor("sp_lx1", [OWNT, DM], F32)
    sp_x2i = nc.dram_tensor("sp_x2i", [OWNT, DM], F32)

    ident_dram = nc.inline_tensor(np.eye(128, dtype=np.float32), "identf")

    qS = S(nc, "qS")
    qG = S(nc, "qG")
    cc = S(nc, "cc")
    sT = S(nc, "sT")
    sV = S(nc, "sV")
    sC = S(nc, "sC")
    sSy = S(nc, "sSy")
    sGp = S(nc, "sGp")
    BAR = {"sync": sSy, "gpsimd": sGp, "tensor": sT, "vector": sV, "scalar": sC}

    em = Em()

    stk = ExitStack()
    ident_f = stk.enter_context(nc.sbuf_tensor("ident_f", [128, 128], F32))
    ident_b = stk.enter_context(nc.sbuf_tensor("ident_b", [128, 128], BF16))
    ident_h = stk.enter_context(nc.sbuf_tensor("ident_h", [128, 128], FP16))
    Dcol_sb = stk.enter_context(nc.sbuf_tensor("Dcol_sb", [128, NDB], F32))
    A_sb = stk.enter_context(nc.sbuf_tensor("A_sb", [128, 2 * NST], F32))
    st = stk.enter_context(nc.sbuf_tensor("stats", [128, 48], F32))
    kcol = stk.enter_context(nc.sbuf_tensor("kcol", [128, 4], F32))
    ps = [stk.enter_context(nc.psum_tensor(f"ps{i}", [128, 512], F32)) for i in range(4)]
    ptA = stk.enter_context(nc.psum_tensor("ptA", [128, 128], F32))
    ptB = stk.enter_context(nc.psum_tensor("ptB", [128, 128], F32))
    pt16 = stk.enter_context(nc.psum_tensor("pt16", [128, 128], FP16))
    ptbf = stk.enter_context(nc.psum_tensor("ptbf", [128, 1024], BF16))

    ps_guard = {}

    def mm_guard(pst):
        g = ps_guard.get(id(pst))
        if g:
            em.w("tensor", g[0], g[1])

    def consume(pst, sem):
        ps_guard[id(pst)] = (sem, sem.n)

    em.dma("sync", lambda e: e.dma_start(out=ident_f[:, :], in_=ident_dram.ap()), qS)
    em.dma("sync", lambda e: e.dma_start(out=Dcol_sb[:, :], in_=Dcol), qS)
    em.dma("sync", lambda e: e.dma_start(
        out=A_sb[:, :].rearrange("p (b n) -> p b n", b=2),
        in_=A_slc.rearrange("(b p) n -> p b n", p=128)), qS)
    em.w("vector", qS)
    em.o("vector", lambda e: e.tensor_copy(ident_b[:, :], ident_f[:, :]))
    em.o("vector", lambda e: e.memset(kcol[:, 0:1], EPS_LN))
    em.o("vector", lambda e: e.memset(kcol[:, 1:2], EPS_RMS))
    em.o("vector", lambda e: e.memset(kcol[:, 2:3], QSCALE))
    em.o("vector", lambda e: e.tensor_copy(ident_h[:, :], ident_f[:, :]), inc=sV)
    em.w("scalar", sV)
    em.w("scalar", qS)
    em.o("scalar", lambda e: e.activation(A_sb[:, :], A_sb[:, :], AF.Exp))
    em.o("scalar", lambda e: e.activation(A_sb[:, :], A_sb[:, :], AF.Copy, scale=-1.0),
         inc=sC)

    def tv(f):
        em.w("vector", sV)
        em.o("vector", f, inc=sV)

    def ln_stats(xt, F, c0, eps, scratch):
        em.w("scalar", sV)
        em.w("vector", sC)
        em.o("vector", lambda e: e.reduce_sum(st[:, c0:c0 + 1], xt[:, 0:F], axis=AX),
             inc=sV)
        em.o("scalar", lambda e: e.activation(
            scratch[:, 0:F], xt[:, 0:F], AF.Square, accum_out=st[:, c0 + 1:c0 + 2]),
            inc=sC)
        em.w("vector", sC)
        tv(lambda e: e.tensor_scalar(
            st[:, c0 + 2:c0 + 3], st[:, c0:c0 + 1], 1.0 / F, None, ALU.mult))
        tv(lambda e: e.tensor_mul(
            st[:, c0 + 3:c0 + 4], st[:, c0 + 2:c0 + 3], st[:, c0 + 2:c0 + 3]))
        tv(lambda e: e.tensor_scalar(
            st[:, c0 + 4:c0 + 5], st[:, c0 + 1:c0 + 2], 1.0 / F, None, ALU.mult))
        tv(lambda e: e.tensor_sub(
            st[:, c0 + 5:c0 + 6], st[:, c0 + 4:c0 + 5], st[:, c0 + 3:c0 + 4]))
        em.w("scalar", sV)
        em.o("scalar", lambda e: e.activation(
            st[:, c0 + 4:c0 + 5], st[:, c0 + 5:c0 + 6], AF.Sqrt, bias=kcol[:, 0:1]),
            inc=sC)
        em.w("vector", sC)
        tv(lambda e: e.reciprocal(st[:, c0 + 5:c0 + 6], st[:, c0 + 4:c0 + 5]))
        em.w("vector", sV)

    def rms_stats(xt, F, c0, eps, scratch):
        em.w("scalar", sV)
        em.o("scalar", lambda e: e.activation(
            scratch[:, 0:F], xt[:, 0:F], AF.Square, accum_out=st[:, c0:c0 + 1]),
            inc=sC)
        em.w("vector", sC)
        tv(lambda e: e.tensor_scalar(
            st[:, c0 + 2:c0 + 3], st[:, c0:c0 + 1], 1.0 / F, None, ALU.mult))
        em.w("scalar", sV)
        em.o("scalar", lambda e: e.activation(
            st[:, c0:c0 + 1], st[:, c0 + 2:c0 + 3], AF.Sqrt, bias=kcol[:, 1:2]), inc=sC)
        em.w("vector", sC)
        tv(lambda e: e.reciprocal(st[:, c0 + 1:c0 + 2], st[:, c0:c0 + 1]))
        em.w("vector", sV)

    if "A" in phases:
        em.scope("phA")
        with ExitStack() as pk:
            A = lambda n, s, d: pk.enter_context(nc.sbuf_tensor(n, s, d))
            xa = A("xa", [128, DM], F32)
            lxa = A("lxa", [128, DM], F32)
            sqa = A("sqa", [128, DM], F32)
            ln2a = A("ln2a", [128, DM], F32)
            xna = A("xna", [128, DM], F32)
            xna_b = A("xna_b", [128, DM], BF16)
            xnT_b = A("xnT_b", [128, NDB * 128], BF16)
            nw_rep = A("nw_rep", [128, DM], F32)
            nb_rep = A("nb_rep", [128, DM], F32)
            dtb_rep = A("dtb_rep", [128, DM], F32)
            xpw_b = A("xpw_b", [128, NDB * 256], BF16)
            dtw_b = A("dtw_b", [128, DM], BF16)
            pj_del = A("pj_del", [128, 128], BF16)
            pbc_h = A("pbc_h", [128, 128], FP16)
            dta = A("dta", [128, DM], F32)
            spa = A("spa", [128, DM], F32)
            dtxa = A("dtxa", [128, DM], F32)
            dt_h = A("dt_h", [128, DM], FP16)
            dtx_h = A("dtx_h", [128, DM], FP16)
            stag = A("stag", [128, NCORE * 5 * 128], FP16)

            em.dma("sync", lambda e: e.dma_start(out=nw_rep[:, :], in_=nwrow.broadcast_to([128, DM])), qS)
            em.dma("sync", lambda e: e.dma_start(out=nb_rep[:, :], in_=nbrow.broadcast_to([128, DM])), qS)
            em.dma("sync", lambda e: e.dma_start(out=dtb_rep[:, :], in_=dtbrow.broadcast_to([128, DM])), qS)
            em.dma("gpsimd", lambda e: e.dma_start(
                out=xpw_b[:, :].rearrange("p (b n) -> p b n", b=NDB),
                in_=xpwT.rearrange("(b p) n -> p b n", p=128)), qG)
            em.dma("gpsimd", lambda e: e.dma_start(out=dtw_b[:, :], in_=dtwT), qG)

            for t in range(8):
                em.w("sync", sV)
                em.w("sync", sC)
                em.dma("sync", lambda e, t=t: e.dma_start(
                    out=xa[:, :], in_=x_own[t * 128:(t + 1) * 128, :]), qS)
                em.w("vector", qS)
                em.w("vector", sT)
                em.w("scalar", qS)
                em.w("scalar", sT)
                ln_stats(xa, DM, 0, EPS_LN, sqa)
                em.o("vector", lambda e: e.tensor_scalar(
                    lxa[:, :], xa[:, :], st[:, 2:3], st[:, 5:6],
                    ALU.subtract, ALU.mult), inc=sV)
                em.w("sync", sV)
                em.dma("sync", lambda e, t=t: e.dma_start(
                    out=sp_lx.ap()[t * 128:(t + 1) * 128, :], in_=lxa[:, :]), qS)
                if "lx" in dbg_outs:
                    em.dma("sync", lambda e, t=t: e.dma_start(
                        out=dbg_outs["lx"][t * 128:(t + 1) * 128, :], in_=lxa[:, :]), qS)
                ln_stats(lxa, DM, 6, EPS_LN, sqa)
                em.o("vector", lambda e: e.tensor_scalar(
                    ln2a[:, :], lxa[:, :], st[:, 8:9], st[:, 11:12],
                    ALU.subtract, ALU.mult))
                em.o("vector", lambda e: e.tensor_mul(sqa[:, :], ln2a[:, :], nw_rep[:, :]))
                em.o("vector", lambda e: e.tensor_add(xna[:, :], sqa[:, :], nb_rep[:, :]))
                em.o("vector", lambda e: e.tensor_copy(xna_b[:, :], xna[:, :]), inc=sV)
                if "xn" in dbg_outs:
                    em.w("sync", sV)
                    em.dma("sync", lambda e, t=t: e.dma_start(
                        out=dbg_outs["xn"][t * 128:(t + 1) * 128, :], in_=xna[:, :]), qS)
                em.w("tensor", sV)
                em.w("tensor", qG)
                for db in range(NDB):
                    mm_guard(ptbf)
                    em.o("tensor", lambda e, db=db: e.transpose(
                        ptbf[:, 0:128], xna_b[:, db * 128:(db + 1) * 128], ident_b[:, :]),
                        inc=sT)
                    em.w("scalar", sT)
                    em.o("scalar", lambda e, db=db: e.copy(
                        xnT_b[:, db * 128:(db + 1) * 128], ptbf[:, 0:128]), inc=sC)
                    consume(ptbf, sC)
                em.w("tensor", sC)
                for pr in range(2):
                    pst = ptA if pr == 0 else ptB
                    mm_guard(pst)
                    for db in range(NDB):
                        em.o("tensor", lambda e, pr=pr, db=db, pst=pst: e.matmul(
                            pst[:, :],
                            xpw_b[:, db * 256 + pr * 128: db * 256 + (pr + 1) * 128],
                            xnT_b[:, db * 128:(db + 1) * 128],
                            start=(db == 0), stop=(db == NDB - 1)),
                            inc=(sT if db == NDB - 1 else None))
                    em.w("vector", sT)
                    dst = pj_del if pr == 0 else pbc_h
                    em.o("vector", lambda e, dst=dst, pst=pst: e.tensor_copy(
                        dst[:, :], pst[:, :]), inc=sV)
                    consume(pst, sV)
                em.w("scalar", sV)
                for sh in range(NCORE):
                    em.o("scalar", lambda e, sh=sh: e.copy(
                        stag[:, (sh * 5 + 4) * 128:(sh * 5 + 5) * 128], pbc_h[:, :]),
                        inc=(sC if sh == NCORE - 1 else None))
                em.w("tensor", sV)
                for dch in range(4):
                    pst = ps[dch]
                    mm_guard(pst)
                    em.o("tensor", lambda e, dch=dch, pst=pst: e.matmul(
                        pst[:, :], pj_del[:, :], dtw_b[:, dch * 512:(dch + 1) * 512],
                        start=True, stop=True), inc=sT)
                em.w("vector", sT)
                for dch in range(4):
                    em.o("vector", lambda e, dch=dch: e.tensor_add(
                        dta[:, dch * 512:(dch + 1) * 512], ps[dch][:, :],
                        dtb_rep[:, dch * 512:(dch + 1) * 512]),
                        inc=(sV if dch == 3 else None))
                for dch in range(4):
                    consume(ps[dch], sV)
                em.w("scalar", sV)
                em.o("scalar", lambda e: e.activation(spa[:, :], dta[:, :], AF.Abs))
                em.o("scalar", lambda e: e.activation(spa[:, :], spa[:, :], AF.Exp, scale=-1.0))
                em.o("scalar", lambda e: e.activation(spa[:, :], spa[:, :], AF.Ln, bias=1.0),
                     inc=sC)
                em.w("vector", sC)
                em.o("vector", lambda e: e.tensor_relu(sqa[:, :], dta[:, :]))
                em.o("vector", lambda e: e.tensor_add(dta[:, :], sqa[:, :], spa[:, :]),
                     inc=sV)
                if "dt" in dbg_outs:
                    em.w("sync", sV)
                    em.dma("sync", lambda e, t=t: e.dma_start(
                        out=dbg_outs["dt"][t * 128:(t + 1) * 128, :], in_=dta[:, :]), qS)
                em.o("vector", lambda e: e.tensor_mul(dtxa[:, :], dta[:, :], xna[:, :]))
                em.o("vector", lambda e: e.tensor_copy(dt_h[:, :], dta[:, :]))
                em.o("vector", lambda e: e.tensor_copy(dtx_h[:, :], dtxa[:, :]), inc=sV)
                em.w("tensor", sV)
                em.w("scalar", qS)
                for which, src in ((0, dt_h), (1, dtx_h)):
                    for db in range(NDB):
                        sh, half = db // 2, db % 2
                        piece = which * 2 + half
                        mm_guard(pt16)
                        em.o("tensor", lambda e, src=src, db=db: e.transpose(
                            pt16[:, :], src[:, db * 128:(db + 1) * 128], ident_h[:, :]),
                            inc=sT)
                        em.w("scalar", sT)
                        em.o("scalar", lambda e, sh=sh, piece=piece: e.copy(
                            stag[:, (sh * 5 + piece) * 128:(sh * 5 + piece + 1) * 128],
                            pt16[:, :]), inc=sC)
                        consume(pt16, sC)
                em.w("sync", sC)
                em.w("sync", qS)
                em.dma("sync", lambda e, t=t: e.dma_start(
                    out=e1_in.ap().rearrange("(f p) n -> p f n", p=128)
                    [:, :, t * 128:(t + 1) * 128],
                    in_=stag[:, :].rearrange("p (f n) -> p f n", n=128)), qS)
                em.w("sync", qS)

            em.w("gpsimd", qS)
            with allow_ici_all_to_all():
                em.w("gpsimd", cc)
                em.o("gpsimd", lambda e: e.collective_compute(
                    "AllToAll", ALU.bypass, replica_groups=[list(range(NCORE))],
                    ins=[e1_in.ap()], outs=[e1_out.ap()]), inc=cc)
            em.pop()
            em.barrier(BAR)

    if "B" in phases:
        em.scope("phB")
        cc_e1 = cc.n
        with ExitStack() as pk:
            A = lambda n, s, d: pk.enter_context(nc.sbuf_tensor(n, s, d))
            dtT = A("dtT", [128, 2 * SEQ], FP16)
            dtxT = A("dtxT", [128, 2 * SEQ], FP16)
            Brep = A("Brep", [128, 2 * SEQ], FP16)
            Crep = A("Crep", [128, 2 * SEQ], FP16)
            dAb = A("dAb", [128, 2 * SEQ], F32)
            ub = A("ub", [128, SEQ], FP16)
            hb = A("hb", [128, SEQ], FP16)
            qscan = A("qscan", [128, SEQ], FP16)
            yb = A("yb", [128, 4 * SEQ], FP16)
            ybf = A("ybf", [128, SEQ], BF16)
            zero1 = A("zero1", [128, 1], F32)
            em.o("vector", lambda e: e.memset(zero1[:, :], 0.0), inc=sV)

            e1o = e1_out.ap().rearrange("(r m) n -> r m n", m=640)
            for s_seq in range(NSEQ):
                em.w("sync", cc, cc_e1)
                em.w("sync", sV)
                for k in range(2):
                    em.dma("sync", lambda e, k=k, s=s_seq: e.dma_start(
                        out=dtT[:, k * SEQ:(k + 1) * SEQ].rearrange("p (r n) -> p r n", r=8),
                        in_=e1o[:, k * 128:(k + 1) * 128, s * 512:(s + 1) * 512]
                        .rearrange("r m n -> m r n")), qS)
                    em.dma("sync", lambda e, k=k, s=s_seq: e.dma_start(
                        out=dtxT[:, k * SEQ:(k + 1) * SEQ].rearrange("p (r n) -> p r n", r=8),
                        in_=e1o[:, 256 + k * 128:256 + (k + 1) * 128, s * 512:(s + 1) * 512]
                        .rearrange("r m n -> m r n")), qS)
                for k in range(2):
                    em.o("vector", lambda e, k=k: e.memset(
                        yb[:, (k * 2) * SEQ:(k * 2 + 1) * SEQ], 0.0),
                        inc=(sV if k == 1 else None))
                ld_done = qS.n
                rep_done = {}
                yv_hist = {}
                step_hist = {}
                sv_base = sV.n
                for n in range(NST + 1):
                    if n < NST:
                        par = n % 2
                        if n >= 2:
                            em.w("sync", sV, yv_hist[n - 2])
                        em.dma("sync", lambda e, n=n, s=s_seq, par=par: e.dma_start(
                            out=Brep[:, par * SEQ:(par + 1) * SEQ]
                            .rearrange("p (r n) -> p r n", r=8),
                            in_=e1o[:, 512 + n:513 + n, s * 512:(s + 1) * 512]
                            .rearrange("r m n -> m r n").broadcast_to([128, 8, 512])), qS)
                        em.dma("sync", lambda e, n=n, s=s_seq, par=par: e.dma_start(
                            out=Crep[:, par * SEQ:(par + 1) * SEQ]
                            .rearrange("p (r n) -> p r n", r=8),
                            in_=e1o[0:1, 512 + NST + n:513 + NST + n, s * 512:(s + 1) * 512]
                            .broadcast_to([128, 8, 512])), qS)
                        rep_done[n] = qS.n
                    if n == 0:
                        continue
                    nn = n - 1
                    par = nn % 2
                    for k in range(2):
                        sp = (nn * 2 + k) % 2
                        em.w("scalar", qS, ld_done)
                        em.w("scalar", sV, step_hist.get(nn * 2 + k - 2, sv_base))
                        em.o("scalar", lambda e, k=k, nn=nn, sp=sp: e.activation(
                            dAb[:, sp * SEQ:(sp + 1) * SEQ],
                            dtT[:, k * SEQ:(k + 1) * SEQ], AF.Exp,
                            scale=A_sb[:, k * NST + nn:k * NST + nn + 1]), inc=sC)
                        em.w("vector", sC)
                        em.w("vector", qS, rep_done[nn])
                        em.o("vector", lambda e, k=k, par=par: e.tensor_mul(
                            ub[:, :], dtxT[:, k * SEQ:(k + 1) * SEQ],
                            Brep[:, par * SEQ:(par + 1) * SEQ]))
                        em.o("vector", lambda e, sp=sp: e.tensor_tensor_scan(
                            hb[:, :], dAb[:, sp * SEQ:(sp + 1) * SEQ], ub[:, :],
                            zero1[:, 0:1], ALU.mult, ALU.add), inc=sV)
                        em.w("vector", sV)
                        em.o("vector", lambda e, par=par: e.tensor_mul(
                            qscan[:, :], hb[:, :], Crep[:, par * SEQ:(par + 1) * SEQ]))
                        pp, qq = (nn % 2), ((nn + 1) % 2)
                        em.o("vector", lambda e, k=k, pp=pp, qq=qq: e.tensor_add(
                            yb[:, (k * 2 + qq) * SEQ:(k * 2 + qq + 1) * SEQ],
                            yb[:, (k * 2 + pp) * SEQ:(k * 2 + pp + 1) * SEQ], qscan[:, :]),
                            inc=sV)
                        step_hist[nn * 2 + k] = sV.n
                    yv_hist[nn] = sV.n
                for k in range(2):
                    em.o("vector", lambda e, k=k: e.tensor_copy(
                        ybf[:, :], yb[:, (k * 2) * SEQ:(k * 2 + 1) * SEQ]), inc=sV)
                    if "y" in dbg_outs:
                        em.o("vector", lambda e, k=k: e.tensor_copy(
                            dAb[:, 0:SEQ], yb[:, (k * 2) * SEQ:(k * 2 + 1) * SEQ]), inc=sV)
                        em.w("sync", sV)
                        em.dma("sync", lambda e, k=k, s=s_seq: e.dma_start(
                            out=dbg_outs["y"][s * DSL + k * 128: s * DSL + (k + 1) * 128, :],
                            in_=dAb[:, 0:SEQ]), qS)
                    em.w("sync", sV)
                    em.dma("sync", lambda e, k=k, s=s_seq: e.dma_start(
                        out=e2_in[s].ap().rearrange("(j m) n -> j m n", m=256)
                        [:, k * 128:(k + 1) * 128, :].rearrange("j m n -> m j n"),
                        in_=ybf[:, :].rearrange("p (j n) -> p j n", j=8)), qS)
                    em.w("vector", qS)
                em.w("gpsimd", qS)
                em.w("gpsimd", cc)
                with allow_ici_all_to_all():
                    em.o("gpsimd", lambda e, s=s_seq: e.collective_compute(
                        "AllToAll", ALU.bypass, replica_groups=[list(range(NCORE))],
                        ins=[e2_in[s].ap()], outs=[e2_out[s].ap()]), inc=cc)
            em.pop()
            em.barrier(BAR)

    if "C" in phases:
        em.scope("phC")
        cc_e2 = cc.n
        with ExitStack() as pk:
            A = lambda n, s, d: pk.enter_context(nc.sbuf_tensor(n, s, d))
            ypx_g = A("ypx_g", [128, NDB * 512], BF16)
            lxc = A("lxc", [128, DM], F32)
            ypxT_b = A("ypxT_b", [128, NDB * 512], BF16)
            opw_b = A("opw_b", [128, NDB * 512], BF16)
            mo_all = A("mo_all", [128, 4 * DM], F32)
            xc = A("xc", [128, DM], F32)
            sqc = A("sqc", [128, DM], F32)
            monl = A("monl", [128, DM], F32)
            x1c = A("x1c", [128, DM], F32)

            for s_seq in range(NSEQ):
                em.w("sync", cc, cc_e2)
                em.w("sync", sT)
                em.w("sync", sV)
                em.dma("sync", lambda e, s=s_seq: e.dma_start(
                    out=ypx_g[:, :].rearrange("p (b n) -> p b n", b=NDB),
                    in_=e2_out[s].ap().rearrange("(b p) n -> p b n", p=128)), qS)
                for lb in range(4):
                    em.w("sync", sT)
                    em.dma("sync", lambda e, s=s_seq, lb=lb: e.dma_start(
                        out=lxc[:, :],
                        in_=sp_lx.ap()[s * 512 + lb * 128: s * 512 + (lb + 1) * 128, :]), qS)
                    em.w("tensor", qS)
                    em.w("vector", qS)
                    for db in range(NDB):
                        pst = ptA if db % 2 == 0 else ptB
                        mm_guard(pst)
                        em.o("tensor", lambda e, db=db, pst=pst: e.transpose(
                            pst[:, :], lxc[:, db * 128:(db + 1) * 128], ident_f[:, :]),
                            inc=sT)
                        em.w("vector", sT)
                        em.o("vector", lambda e, db=db, lb=lb, pst=pst: e.scalar_tensor_tensor(
                            ypxT_b[:, db * 512 + lb * 128: db * 512 + (lb + 1) * 128],
                            pst[:, :], Dcol_sb[:, db:db + 1],
                            ypx_g[:, db * 512 + lb * 128: db * 512 + (lb + 1) * 128],
                            ALU.mult, ALU.add), inc=sV)
                        consume(pst, sV)
                for dch in range(4):
                    em.w("gpsimd", sT)
                    em.dma("gpsimd", lambda e, dch=dch: e.dma_start(
                        out=opw_b[:, :].rearrange("p (b n) -> p b n", b=NDB),
                        in_=opwT[:, dch * 512:(dch + 1) * 512]
                        .rearrange("(b p) n -> p b n", p=128)), qG)
                    em.w("tensor", qG)
                    em.w("tensor", sV)
                    for lb in range(4):
                        pst = ps[lb]
                        mm_guard(pst)
                        for db in range(NDB):
                            em.o("tensor", lambda e, db=db, lb=lb, pst=pst: e.matmul(
                                pst[:, :],
                                ypxT_b[:, db * 512 + lb * 128: db * 512 + (lb + 1) * 128],
                                opw_b[:, db * 512:(db + 1) * 512],
                                start=(db == 0), stop=(db == NDB - 1)),
                                inc=(sT if db == NDB - 1 else None))
                    em.w("scalar", sT)
                    for lb in range(4):
                        em.o("scalar", lambda e, lb=lb, dch=dch: e.copy(
                            mo_all[:, lb * DM + dch * 512: lb * DM + (dch + 1) * 512],
                            ps[lb][:, :]), inc=(sC if lb == 3 else None))
                    for lb in range(4):
                        consume(ps[lb], sC)
                for lb in range(4):
                    em.w("sync", sV)
                    em.dma("sync", lambda e, s=s_seq, lb=lb: e.dma_start(
                        out=xc[:, :],
                        in_=x_own[s * 512 + lb * 128: s * 512 + (lb + 1) * 128, :]), qS)
                    em.w("vector", qS)
                    em.w("vector", sC)
                    mot = mo_all[:, lb * DM:(lb + 1) * DM]
                    ln_stats(mot, DM, 12, EPS_LN, sqc)
                    em.o("vector", lambda e, mot=mot: e.tensor_scalar(
                        monl[:, :], mot[:, :], st[:, 14:15], st[:, 17:18],
                        ALU.subtract, ALU.mult))
                    em.o("vector", lambda e: e.tensor_add(x1c[:, :], xc[:, :], monl[:, :]),
                         inc=sV)
                    em.w("sync", sV)
                    em.dma("sync", lambda e, s=s_seq, lb=lb: e.dma_start(
                        out=sp_x1.ap()[s * 512 + lb * 128: s * 512 + (lb + 1) * 128, :],
                        in_=x1c[:, :]), qS)
                    if "x1" in dbg_outs:
                        em.dma("sync", lambda e, s=s_seq, lb=lb: e.dma_start(
                            out=dbg_outs["x1"][s * 512 + lb * 128: s * 512 + (lb + 1) * 128, :],
                            in_=x1c[:, :]), qS)
            em.pop()
            em.barrier(BAR)

    if "D" in phases:
        for s_seq in range(NSEQ):
            with ExitStack() as sk:
                with ExitStack() as pk:
                    A = lambda n, s, d: pk.enter_context(nc.sbuf_tensor(n + str(s_seq), s, d))
                    xd = A("xd", [128, DM], F32)
                    sqd = A("sqd", [128, DM], F32)
                    hb_t = A("hb_t", [128, DM], BF16)
                    l1_rep = A("l1_rep", [128, DM], F32)
                    stg = A("stg", [128, NDB * 128], BF16)
                    wq_b = A("wq_b", [128, NDB * 256], BF16)
                    wk_b = A("wk_b", [128, NDB * 256], BF16)
                    wv_b = A("wv_b", [128, NDB * 256], BF16)
                    wo_b = A("wo_b", [128, 2 * DM], BF16)
                    hTb = A("hTb", [128, NDB * 512], BF16)
                    qT_sb = A("qT_sb", [128, 2 * SEQ], BF16)
                    kT_sb = A("kT_sb", [128, 2 * SEQ], BF16)
                    vtmp = A("vtmp", [128, 512], BF16)
                    v_sb = A("v_sb", [128, 32 * 256], BF16)
                    sc_sb = A("sc_sb", [128, SEQ], F32)
                    p2_sb = A("p2_sb", [128, SEQ], BF16)
                    p2b_sb = A("p2b_sb", [128, SEQ], BF16)
                    pT_sb = A("pT_sb", [128, 512], BF16)
                    aoT_sb = A("aoT_sb", [128, 2 * SEQ], BF16)
                    arstg = A("arstg", [128, DM], BF16)
                    ars_sb = A("ars_sb", [128, DM], BF16)
                    mask_sb = A("mask_sb", [128, 4 * 512], BF16)
                    lx1t = A("lx1t", [128, DM], F32)
                    x2it = A("x2it", [128, DM], F32)

                    import ml_dtypes as _mld
                    mrow = np.zeros((4, 128, 512), np.float32)
                    for qo in range(4):
                        for i in range(128):
                            mrow[qo, i, qo * 128 + i + 1:] = -1e9
                    mask_dram = nc.inline_tensor(
                        mrow.transpose(1, 0, 2).reshape(128, 4 * 512)
                        .astype(_mld.bfloat16), f"mask{s_seq}")
                    em.dma("sync", lambda e: e.dma_start(
                        out=mask_sb[:, :], in_=mask_dram.ap()), qS)
                    em.dma("sync", lambda e: e.dma_start(
                        out=l1_rep[:, :], in_=l1row.broadcast_to([128, DM])), qS)
                    em.dma("gpsimd", lambda e: e.dma_start(
                        out=wq_b[:, :].rearrange("p (b n) -> p b n", b=NDB),
                        in_=wq_own.rearrange("(b p) n -> p b n", p=128)), qG)
                    em.dma("gpsimd", lambda e: e.dma_start(
                        out=wk_b[:, :].rearrange("p (b n) -> p b n", b=NDB),
                        in_=wk_own.rearrange("(b p) n -> p b n", p=128)), qG)
                    em.dma("gpsimd", lambda e: e.dma_start(
                        out=wv_b[:, :].rearrange("p (b n) -> p b n", b=NDB),
                        in_=wv_own.rearrange("(b p) n -> p b n", p=128)), qG)
                    em.dma("gpsimd", lambda e: e.dma_start(
                        out=wo_b[:, :].rearrange("p (b n) -> p b n", b=2),
                        in_=wo_own.rearrange("(b p) n -> p b n", p=128)), qG)

                    em.scope("D_h" + str(s_seq))
                    for lb in range(4):
                        em.w("sync", sV)
                        em.w("sync", sC)
                        em.dma("sync", lambda e, s=s_seq, lb=lb: e.dma_start(
                            out=xd[:, :],
                            in_=sp_x1.ap()[s * 512 + lb * 128: s * 512 + (lb + 1) * 128, :]), qS)
                        em.w("vector", qS)
                        em.w("vector", sT)
                        em.w("scalar", qS)
                        ln_stats(xd, DM, 18, EPS_LN, sqd)
                        em.o("vector", lambda e: e.tensor_scalar(
                            lx1t[:, :], xd[:, :], st[:, 20:21], st[:, 23:24],
                            ALU.subtract, ALU.mult), inc=sV)
                        em.w("sync", sV)
                        em.dma("sync", lambda e, s=s_seq, lb=lb: e.dma_start(
                            out=sp_lx1.ap()[s * 512 + lb * 128: s * 512 + (lb + 1) * 128, :],
                            in_=lx1t[:, :]), qS)
                        rms_stats(lx1t, DM, 24, EPS_RMS, sqd)
                        em.o("vector", lambda e: e.tensor_scalar(
                            sqd[:, :], lx1t[:, :], st[:, 25:26], None, ALU.mult))
                        em.o("vector", lambda e: e.tensor_mul(xd[:, :], sqd[:, :], l1_rep[:, :]))
                        em.o("vector", lambda e: e.tensor_copy(hb_t[:, :], xd[:, :]), inc=sV)
                        if "hln" in dbg_outs:
                            em.w("sync", sV)
                            em.dma("sync", lambda e, s=s_seq, lb=lb: e.dma_start(
                                out=dbg_outs["hln"][s * 512 + lb * 128: s * 512 + (lb + 1) * 128, :],
                                in_=xd[:, :]), qS)
                        em.w("tensor", sV)
                        em.w("scalar", qS)
                        if s_seq == 0 and lb == 0 and "hbt" in dbg_outs:
                            em.w("vector", sV)
                            em.o("vector", lambda e: e.tensor_copy(sqd[:, :], hb_t[:, :]),
                                 inc=sV)
                            em.w("sync", sV)
                            em.dma("sync", lambda e: e.dma_start(
                                out=dbg_outs["hbt"], in_=sqd[:, :]), qS)
                            em.w("vector", qS)
                        for db in range(NDB):
                            mm_guard(ptbf)
                            em.o("tensor", lambda e, db=db: e.transpose(
                                ptbf[:, 0:128], hb_t[:, db * 128:(db + 1) * 128], ident_b[:, :]),
                                inc=sT)
                            em.w("scalar", sT)
                            em.o("scalar", lambda e, db=db: e.copy(
                                stg[:, db * 128:(db + 1) * 128], ptbf[:, 0:128]), inc=sC)
                            consume(ptbf, sC)
                        if s_seq == 0 and lb == 0 and "stg" in dbg_outs:
                            em.w("vector", sC)
                            em.o("vector", lambda e: e.tensor_copy(sqd[:, :], stg[:, :]),
                                 inc=sV)
                            em.w("sync", sV)
                            em.dma("sync", lambda e: e.dma_start(
                                out=dbg_outs["stg"], in_=sqd[:, :]), qS)
                            em.w("vector", qS)
                            em.w("scalar", qS)
                        em.w("sync", sC)
                        em.w("sync", qS)
                        em.dma("sync", lambda e, s=s_seq, lb=lb: e.dma_start(
                            out=ht_in[s].ap().rearrange("(b p) n -> p b n", p=128)
                            [:, :, lb * 128:(lb + 1) * 128],
                            in_=stg[:, :].rearrange("p (b n) -> p b n", b=NDB)), qS)
                        em.w("sync", qS)
                    if s_seq == 0 and "hti" in dbg_outs:
                        em.w("sync", qS)
                        em.dma("sync", lambda e: e.dma_start(
                            out=hTb[:, 0:512].rearrange("p n -> p n"),
                            in_=ht_in[0].ap()[0:128, :]), qS)
                        em.w("sync", qS)
                        for j in range(1):
                            em.w("vector", qS)
                            em.o("vector", lambda e: e.tensor_copy(
                                sqd[:, 0:512], hTb[:, 0:512]), inc=sV)
                            em.w("sync", sV)
                            em.dma("sync", lambda e: e.dma_start(
                                out=dbg_outs["hti"], in_=sqd[:, 0:512]), qS)
                            em.w("vector", qS)
                    em.w("gpsimd", qS)
                    em.w("gpsimd", cc)
                    em.o("gpsimd", lambda e, s=s_seq: e.collective_compute(
                        "AllGather", ALU.bypass, replica_groups=[list(range(NCORE))],
                        ins=[ht_in[s].ap()], outs=[ht_out[s].ap()]), inc=cc)
                    cc_ht = cc.n

                    em.pop()
                    em.scope("D_qkv" + str(s_seq))
                    hto = ht_out[s_seq].ap().rearrange("(r b p) n -> r p b n", p=128, b=NDB)
                    for r in range(NCORE):
                        em.w("sync", cc, cc_ht)
                        em.w("sync", sT)
                        em.dma("sync", lambda e, r=r, hto=hto: e.dma_start(
                            out=hTb[:, :].rearrange("p (b n) -> p b n", b=NDB),
                            in_=hto[r]), qS)
                        if r == 0 and s_seq == 0 and "htb" in dbg_outs:
                            for j in range(4):
                                em.w("vector", qS)
                                em.o("vector", lambda e, j=j: e.tensor_copy(
                                    sqd[:, :], hTb[:, j * 2048:(j + 1) * 2048]), inc=sV)
                                em.w("sync", sV)
                                em.dma("sync", lambda e, j=j: e.dma_start(
                                    out=dbg_outs["htb"][:, j * 2048:(j + 1) * 2048],
                                    in_=sqd[:, :]), qS)
                                em.w("vector", qS)
                        em.w("tensor", qS)
                        em.w("tensor", qG)
                        for ti, (wt, dst, scale) in enumerate(
                                ((wq_b, qT_sb, QSCALE), (wk_b, kT_sb, 1.0), (wv_b, None, 1.0))):
                            for h2 in range(2):
                                pst = ps[(ti * 2 + h2) % 4]
                                mm_guard(pst)
                                for db in range(NDB):
                                    em.o("tensor", lambda e, db=db, h2=h2, wt=wt, pst=pst: e.matmul(
                                        pst[:, :],
                                        wt[:, db * 256 + h2 * 128: db * 256 + (h2 + 1) * 128],
                                        hTb[:, db * 512:(db + 1) * 512],
                                        start=(db == 0), stop=(db == NDB - 1)),
                                        inc=(sT if db == NDB - 1 else None))
                                em.w("scalar", sT)
                                if ti < 2:
                                    em.o("scalar", lambda e, h2=h2, r=r, dst=dst, scale=scale, pst=pst:
                                         e.activation(
                                             dst[:, h2 * SEQ + r * 512: h2 * SEQ + (r + 1) * 512],
                                             pst[:, :], AF.Copy,
                                             scale=(kcol[:, 2:3] if scale != 1.0 else 1.0)), inc=sC)
                                    consume(pst, sC)
                                else:
                                    em.w("scalar", sT)
                                    em.w("scalar", sT)
                                    em.o("scalar", lambda e, pst=pst: e.copy(
                                        vtmp[:, :], pst[:, :]), inc=sC)
                                    consume(pst, sC)
                                    em.w("tensor", sC)
                                    for j in range(4):
                                        mm_guard(ptbf)
                                        em.o("tensor", lambda e, j=j: e.transpose(
                                            ptbf[:, 0:128], vtmp[:, j * 128:(j + 1) * 128],
                                            ident_b[:, :]), inc=sT)
                                        em.w("scalar", sT)
                                        em.o("scalar", lambda e, r=r, j=j, h2=h2: e.copy(
                                            v_sb[:, (r * 4 + j) * 256 + h2 * 128:
                                                 (r * 4 + j) * 256 + (h2 + 1) * 128],
                                            ptbf[:, 0:128]), inc=sC)
                                        consume(ptbf, sC)

                    em.pop()
                    em.scope("D_attn" + str(s_seq))
                    em.w("tensor", sC)
                    for h2 in range(2):
                        for qb in range(32):
                            kvlen = (qb + 1) * 128
                            nch = (kvlen + 511) // 512
                            em.w("tensor", sV)
                            for ch in range(nch):
                                cl = min(512, kvlen - ch * 512)
                                pst = ps[ch % 2]
                                mm_guard(pst)
                                em.o("tensor", lambda e, h2=h2, qb=qb, ch=ch, cl=cl, pst=pst:
                                     e.matmul(
                                         pst[:, 0:cl],
                                         qT_sb[:, h2 * SEQ + qb * 128:h2 * SEQ + (qb + 1) * 128],
                                         kT_sb[:, h2 * SEQ + ch * 512: h2 * SEQ + ch * 512 + cl],
                                         start=True, stop=True), inc=sT)
                                em.w("vector", sT)
                                is_diag = (ch == qb // 4)
                                if is_diag:
                                    qo = qb % 4
                                    em.o("vector", lambda e, ch=ch, cl=cl, qo=qo, pst=pst:
                                         e.tensor_add(
                                             sc_sb[:, ch * 512:ch * 512 + cl], pst[:, 0:cl],
                                             mask_sb[:, qo * 512:qo * 512 + cl]), inc=sV)
                                else:
                                    em.o("vector", lambda e, ch=ch, cl=cl, pst=pst:
                                         e.tensor_copy(
                                             sc_sb[:, ch * 512:ch * 512 + cl], pst[:, 0:cl]),
                                         inc=sV)
                                consume(pst, sV)
                            em.o("vector", lambda e, kvlen=kvlen: e.reduce_max(
                                st[:, 44:45], sc_sb[:, 0:kvlen], axis=AX, negate=True),
                                inc=sV)
                            em.w("scalar", sV)
                            em.o("scalar", lambda e, kvlen=kvlen: e.activation(
                                p2_sb[:, 0:kvlen], sc_sb[:, 0:kvlen], AF.Exp,
                                bias=st[:, 44:45], accum_out=st[:, 45:46]), inc=sC)
                            em.w("vector", sC)
                            em.o("vector", lambda e: e.reciprocal(st[:, 46:47], st[:, 45:46]),
                                 inc=sV)
                            em.w("vector", sV)
                            em.o("vector", lambda e, kvlen=kvlen: e.tensor_scalar(
                                p2b_sb[:, 0:kvlen], p2_sb[:, 0:kvlen], st[:, 46:47], None,
                                ALU.mult), inc=sV)
                            em.w("tensor", sV)
                            pao = ps[2]
                            mm_guard(pao)
                            nkb = qb + 1
                            for g0 in range(0, nkb, 4):
                                gn = min(4, nkb - g0)
                                mm_guard(ptbf)
                                for j in range(gn):
                                    em.o("tensor", lambda e, g0=g0, j=j: e.transpose(
                                        ptbf[:, j * 128:(j + 1) * 128],
                                        p2b_sb[:, (g0 + j) * 128:(g0 + j + 1) * 128],
                                        ident_b[:, :]),
                                        inc=(sT if j == gn - 1 else None))
                                em.w("scalar", sT)
                                em.o("scalar", lambda e, gn=gn: e.copy(
                                    pT_sb[:, 0:gn * 128], ptbf[:, 0:gn * 128]), inc=sC)
                                consume(ptbf, sC)
                                em.w("tensor", sC)
                                for j in range(gn):
                                    kb = g0 + j
                                    em.o("tensor", lambda e, kb=kb, j=j, h2=h2, qb=qb, pao=pao:
                                         e.matmul(
                                             pao[:, 0:128],
                                             v_sb[:, kb * 256 + h2 * 128: kb * 256 + (h2 + 1) * 128],
                                             pT_sb[:, j * 128:(j + 1) * 128],
                                             start=(kb == 0), stop=(kb == qb)),
                                         inc=(sT if kb == qb else None))
                            em.w("scalar", sT)
                            em.o("scalar", lambda e, h2=h2, qb=qb, pao=pao: e.copy(
                                aoT_sb[:, h2 * SEQ + qb * 128: h2 * SEQ + (qb + 1) * 128],
                                pao[:, 0:128]), inc=sC)
                            consume(pao, sC)

                    if s_seq == 0:
                        for nm, buf in (("qt", qT_sb), ("kt", kT_sb), ("vv", v_sb),
                                        ("aot", aoT_sb)):
                            if nm in dbg_outs:
                                for j in range(4):
                                    em.w("vector", sC)
                                    em.o("vector", lambda e, buf=buf, j=j: e.tensor_copy(
                                        sqd[:, :], buf[:, j * 2048:(j + 1) * 2048]),
                                        inc=sV)
                                    em.w("sync", sV)
                                    em.dma("sync", lambda e, nm=nm, j=j: e.dma_start(
                                        out=dbg_outs[nm][:, j * 2048:(j + 1) * 2048],
                                        in_=sqd[:, :]), qS)
                                    em.w("vector", qS)
                    em.pop()
                    em.scope("D_wo" + str(s_seq))
                    em.w("tensor", sC)
                    for qb in range(32):
                        for dch in range(4):
                            pst = ps[dch % 2]
                            mm_guard(pst)
                            for h2 in range(2):
                                em.o("tensor", lambda e, qb=qb, dch=dch, h2=h2, pst=pst:
                                     e.matmul(
                                         pst[:, :],
                                         aoT_sb[:, h2 * SEQ + qb * 128: h2 * SEQ + (qb + 1) * 128],
                                         wo_b[:, h2 * DM + dch * 512: h2 * DM + (dch + 1) * 512],
                                         start=(h2 == 0), stop=(h2 == 1)),
                                     inc=(sT if h2 == 1 else None))
                            em.w("scalar", sT)
                            em.w("scalar", qS)
                            em.o("scalar", lambda e, dch=dch, pst=pst: e.copy(
                                arstg[:, dch * 512:(dch + 1) * 512], pst[:, :]),
                                inc=sC)
                            consume(pst, sC)
                        em.w("sync", sC)
                        em.dma("sync", lambda e, s=s_seq, qb=qb: e.dma_start(
                            out=ars_in[s].ap()[qb * 128:(qb + 1) * 128, :],
                            in_=arstg[:, :]), qS)
                    em.w("gpsimd", qS)
                    em.w("gpsimd", cc)
                    em.o("gpsimd", lambda e, s=s_seq: e.collective_compute(
                        "ReduceScatter", ALU.add, replica_groups=[list(range(NCORE))],
                        ins=[ars_in[s].ap()], outs=[ars_out[s].ap()]), inc=cc)
                    cc_ars = cc.n
                    em.pop()
                    em.scope("D_x2i" + str(s_seq))
                    for lb in range(4):
                        em.w("sync", cc, cc_ars)
                        em.w("sync", sV)
                        em.dma("sync", lambda e, s=s_seq, lb=lb: e.dma_start(
                            out=ars_sb[:, :],
                            in_=ars_out[s].ap()[lb * 128:(lb + 1) * 128, :]), qS)
                        em.dma("sync", lambda e, s=s_seq, lb=lb: e.dma_start(
                            out=lx1t[:, :],
                            in_=sp_lx1.ap()[s * 512 + lb * 128: s * 512 + (lb + 1) * 128, :]), qS)
                        em.w("vector", qS)
                        em.o("vector", lambda e, lb=lb: e.tensor_add(
                            x2it[:, :], lx1t[:, :], ars_sb[:, :]), inc=sV)
                        em.w("sync", sV)
                        em.dma("sync", lambda e, s=s_seq, lb=lb: e.dma_start(
                            out=sp_x2i.ap()[s * 512 + lb * 128: s * 512 + (lb + 1) * 128, :],
                            in_=x2it[:, :]), qS)
                        if "x2i" in dbg_outs:
                            em.dma("sync", lambda e, s=s_seq, lb=lb: e.dma_start(
                                out=dbg_outs["x2i"][s * 512 + lb * 128: s * 512 + (lb + 1) * 128, :],
                                in_=x2it[:, :]), qS)
                    em.pop()
                    em.barrier(BAR)
                if "E" in phases:
                    _phase_E(nc, em, BAR, s_seq, sp_x2i, dict(
                        l2row=l2row, gwT=gwT, egT=egT, euT=euT, edT=edT,
                        sgT=sgT, suT=suT, sdT=sdT, kcol=kcol,
                        tt_in=tt_in, tt_out=tt_out, cw_in=cw_in, cw_out=cw_out,
                        mrs_in=mrs_in, mrs_out=mrs_out, sp_x1=sp_x1,
                        out_own=out_own, dbg_outs=dbg_outs,
                        ident_f=ident_f, ident_b=ident_b, st=st,
                        ps=ps, ptA=ptA, ptB=ptB, ptbf=ptbf,
                        qS=qS, qG=qG, cc=cc, sT=sT, sV=sV, sC=sC,
                        mm_guard=mm_guard, consume=consume,
                        ln_stats=ln_stats, rms_stats=rms_stats))
                em.barrier(BAR)

    em.w("sync", qG)
    em.w("sync", cc)
    em.w("sync", qS)

    def _replay(eng, ops):
        stack = []
        for item in ops:
            if callable(item):
                item(eng)
            elif item[0] == "scope":
                cm = nc.named_scope(item[1])
                cm.__enter__()
                stack.append(cm)
            else:
                stack.pop().__exit__(None, None, None)
        while stack:
            stack.pop().__exit__(None, None, None)
    with allow_ici_all_to_all():
        with nc.Block() as blk:
            for name in ENGS:
                getattr(blk, name)(lambda eng, ops=tuple(em.q[name]): _replay(eng, ops))
    stk.close()
    return nc, dbg_outs


def _phase_E(nc, em, BAR, s_seq, sp_x2i, z):
    qS, qG, cc, sT, sV, sC = z["qS"], z["qG"], z["cc"], z["sT"], z["sV"], z["sC"]
    mm_guard, consume = z["mm_guard"], z["consume"]
    ps, ptA, ptB, ptbf = z["ps"], z["ptA"], z["ptB"], z["ptbf"]
    st = z["st"]
    ident_f, ident_b = z["ident_f"], z["ident_b"]
    dbg_outs = z["dbg_outs"]
    ln_stats, rms_stats = z["ln_stats"], z["rms_stats"]

    with ExitStack() as pk:
        A = lambda n, s, d: pk.enter_context(nc.sbuf_tensor(n + "E" + str(s_seq), s, d))
        l2_rep = A("l2_rep", [128, DM], F32)
        x2e = A("x2e", [128, DM], F32)
        te = A("te", [128, DM], F32)
        tb = A("tb", [128, DM], BF16)
        sqe = A("sqe", [128, DM], F32)
        tT_b = A("tT_b", [128, NDB * 512], BF16)
        gw_b = A("gw_b", [128, NDB * NE], BF16)
        glog = A("glog", [128, NE], F32)
        mx8 = A("mx8", [128, 8], F32)
        ind = A("ind", [128, NE], F32)
        wsel = A("wsel", [128, NE], F32)
        cwt = A("cwt", [128, NE], F32)
        cwT_sb = A("cwT_sb", [128, OWN], F32)

        em.dma("sync", lambda e: e.dma_start(
            out=l2_rep[:, :], in_=z["l2row"].broadcast_to([128, DM])), qS)
        em.dma("gpsimd", lambda e: e.dma_start(
            out=gw_b[:, :].rearrange("p (b n) -> p b n", b=NDB),
            in_=z["gwT"].rearrange("(b p) n -> p b n", p=128)), qG)

        for lb in range(4):
            em.w("sync", sV)
            em.w("sync", sC)
            em.dma("sync", lambda e, s=s_seq, lb=lb: e.dma_start(
                out=x2e[:, :],
                in_=sp_x2i.ap()[s * 512 + lb * 128: s * 512 + (lb + 1) * 128, :]), qS)
            em.w("vector", qS)
            em.w("scalar", qS)
            em.w("vector", sT)
            rms_stats(x2e, DM, 28, EPS_RMS, sqe)
            em.o("vector", lambda e: e.tensor_scalar(
                sqe[:, :], x2e[:, :], st[:, 29:30], None, ALU.mult))
            em.o("vector", lambda e: e.tensor_mul(te[:, :], sqe[:, :], l2_rep[:, :]))
            em.o("vector", lambda e: e.tensor_copy(tb[:, :], te[:, :]), inc=sV)
            em.w("tensor", sV)
            for db in range(NDB):
                mm_guard(ptbf)
                em.o("tensor", lambda e, db=db: e.transpose(
                    ptbf[:, 0:128], tb[:, db * 128:(db + 1) * 128], ident_b[:, :]), inc=sT)
                em.w("scalar", sT)
                em.o("scalar", lambda e, db=db, lb=lb: e.copy(
                    tT_b[:, db * 512 + lb * 128: db * 512 + (lb + 1) * 128],
                    ptbf[:, 0:128]), inc=sC)
                consume(ptbf, sC)
            em.w("tensor", sC)
            em.w("tensor", qG)
            mm_guard(ptA)
            for db in range(NDB):
                em.o("tensor", lambda e, db=db, lb=lb: e.matmul(
                    ptA[:, 0:NE],
                    tT_b[:, db * 512 + lb * 128: db * 512 + (lb + 1) * 128],
                    gw_b[:, db * NE:(db + 1) * NE],
                    start=(db == 0), stop=(db == NDB - 1)),
                    inc=(sT if db == NDB - 1 else None))
            em.w("vector", sT)
            em.o("vector", lambda e: e.tensor_copy(glog[:, :], ptA[:, 0:NE]), inc=sV)
            consume(ptA, sV)
            em.w("vector", sV)
            if "gl" in dbg_outs:
                em.w("sync", sV)
                em.dma("sync", lambda e, s=s_seq, lb=lb: e.dma_start(
                    out=dbg_outs["gl"][s * 512 + lb * 128: s * 512 + (lb + 1) * 128, :],
                    in_=glog[:, :]), qS)
            em.o("vector", lambda e: e.max(mx8[:, :], glog[:, :]), inc=sV)
            em.w("vector", sV)
            em.o("vector", lambda e: e.tensor_scalar(
                st[:, 40:41], mx8[:, 0:1], -1.0, None, ALU.mult), inc=sV)
            em.w("vector", sV)
            em.o("vector", lambda e: e.tensor_scalar(
                ind[:, :], glog[:, :], mx8[:, 5:6], None, ALU.is_ge), inc=sV)
            em.w("scalar", sV)
            em.o("scalar", lambda e: e.activation(
                wsel[:, :], glog[:, :], AF.Exp, bias=st[:, 40:41]), inc=sC)
            em.w("vector", sC)
            em.o("vector", lambda e: e.tensor_mul(glog[:, :], wsel[:, :], ind[:, :]),
                 inc=sV)
            em.w("vector", sV)
            em.o("vector", lambda e: e.reduce_sum(st[:, 41:42], glog[:, :], axis=AX),
                 inc=sV)
            em.w("vector", sV)
            em.o("vector", lambda e: e.reciprocal(st[:, 42:43], st[:, 41:42]), inc=sV)
            em.w("vector", sV)
            em.o("vector", lambda e: e.tensor_scalar(
                cwt[:, :], glog[:, :], st[:, 42:43], None, ALU.mult), inc=sV)
            if "cw" in dbg_outs:
                em.w("sync", sV)
                em.dma("sync", lambda e, s=s_seq, lb=lb: e.dma_start(
                    out=dbg_outs["cw"][s * 512 + lb * 128: s * 512 + (lb + 1) * 128, :],
                    in_=cwt[:, :]), qS)
            em.w("tensor", sV)
            mm_guard(ptB)
            em.o("tensor", lambda e: e.transpose(
                ptB[0:NE, :], cwt[:, :], ident_f[:, :]), inc=sT)
            em.w("vector", sT)
            em.o("vector", lambda e, lb=lb: e.tensor_copy(
                cwT_sb[0:NE, lb * 128:(lb + 1) * 128], ptB[0:NE, 0:128]), inc=sV)
            consume(ptB, sV)
        em.w("sync", sC)
        em.w("sync", sV)
        em.dma("sync", lambda e, s=s_seq: e.dma_start(
            out=z["tt_in"][s].ap().rearrange("(b p) n -> p b n", p=128),
            in_=tT_b[:, :].rearrange("p (b n) -> p b n", b=NDB)), qS)
        em.dma("sync", lambda e, s=s_seq: e.dma_start(
            out=z["cw_in"][s].ap(), in_=cwT_sb[0:NE, :]), qS)
        em.w("gpsimd", qS)
        em.w("gpsimd", cc)
        em.o("gpsimd", lambda e, s=s_seq: e.collective_compute(
            "AllGather", ALU.bypass, replica_groups=[list(range(NCORE))],
            ins=[z["tt_in"][s].ap()], outs=[z["tt_out"][s].ap()]), inc=cc)
        with allow_ici_all_to_all():
            em.w("gpsimd", cc)
            em.o("gpsimd", lambda e, s=s_seq: e.collective_compute(
                "AllToAll", ALU.bypass, replica_groups=[list(range(NCORE))],
                ins=[z["cw_in"][s].ap()], outs=[z["cw_out"][s].ap()]), inc=cc)
        cc_moe = cc.n

        with ExitStack() as rk:
            B = lambda n, s, d: rk.enter_context(nc.sbuf_tensor(n + "R" + str(s_seq), s, d))
            tT_all = B("tT_all", [128, NDB * 1024], BF16)
            hh_w = B("hh_w", [128, 2 * NIB * 1024], BF16)
            eg_b = B("eg_b", [128, 2 * NDB * 128], BF16)
            eu_b = B("eu_b", [128, 2 * NDB * 128], BF16)
            ed_b = B("ed_b", [128, 2 * EI], BF16)
            cwrep = B("cwrep", [128, 1024], F32)
            sil_b = B("sil_b", [128, 1024], BF16)
            uw_b = B("uw_b", [128, 1024], BF16)
            mstg = B("mstg", [128, 1024], BF16)

            tto = z["tt_out"][s_seq].ap().rearrange("(r b p) n -> p b r n", p=128, b=NDB)
            cwo = z["cw_out"][s_seq].ap().rearrange("(r e) n -> e r n", e=2)
            for qt in range(4):
                em.w("sync", cc, cc_moe)
                em.w("sync", sT)
                for rr in range(2):
                    em.dma("sync", lambda e, qt=qt, rr=rr: e.dma_start(
                        out=tT_all[:, :].rearrange("p (b m) -> p b m", b=NDB)
                        [:, :, rr * 512:(rr + 1) * 512],
                        in_=tto[:, :, 2 * qt + rr, :]), qS)
                for e2 in range(2):
                    em.w("sync", sV)
                    em.dma("sync", lambda e, e2=e2, qt=qt: e.dma_start(
                        out=cwrep[:, :].rearrange("p (r n) -> p r n", r=2),
                        in_=cwo[e2:e2 + 1, 2 * qt:2 * qt + 2, :]
                        .broadcast_to([128, 2, 512])), qS)
                    cw_ld = qS.n
                    ld_hist = {}
                    mm_hist = {}
                    st_base = sT.n
                    for ib in range(NIB + 1):
                        if ib < NIB:
                            par = ib % 2
                            em.w("gpsimd", sT, mm_hist.get(ib - 2, st_base))
                            em.dma("gpsimd", lambda e, e2=e2, ib=ib, par=par: e.dma_start(
                                out=eg_b[:, par * 2048:(par + 1) * 2048]
                                .rearrange("p (b n) -> p b n", b=NDB),
                                in_=z["egT"][e2:e2 + 1, :, ib * 128:(ib + 1) * 128]
                                .rearrange("o (b p) n -> p b (o n)", p=128)), qG)
                            em.dma("gpsimd", lambda e, e2=e2, ib=ib, par=par: e.dma_start(
                                out=eu_b[:, par * 2048:(par + 1) * 2048]
                                .rearrange("p (b n) -> p b n", b=NDB),
                                in_=z["euT"][e2:e2 + 1, :, ib * 128:(ib + 1) * 128]
                                .rearrange("o (b p) n -> p b (o n)", p=128)), qG)
                            ld_hist[ib] = qG.n
                        if ib == 0:
                            continue
                        jb = ib - 1
                        par = jb % 2
                        em.w("tensor", qG, ld_hist[jb])
                        em.w("tensor", qS)
                        for lch in range(2):
                            pg, pu = ps[lch * 2], ps[lch * 2 + 1]
                            mm_guard(pg)
                            mm_guard(pu)
                            for db in range(NDB):
                                em.o("tensor", lambda e, db=db, lch=lch, pg=pg, par=par: e.matmul(
                                    pg[:, :],
                                    eg_b[:, par * 2048 + db * 128: par * 2048 + (db + 1) * 128],
                                    tT_all[:, db * 1024 + lch * 512: db * 1024 + (lch + 1) * 512],
                                    start=(db == 0), stop=(db == NDB - 1)),
                                    inc=(sT if db == NDB - 1 else None))
                            for db in range(NDB):
                                em.o("tensor", lambda e, db=db, lch=lch, pu=pu, par=par: e.matmul(
                                    pu[:, :],
                                    eu_b[:, par * 2048 + db * 128: par * 2048 + (db + 1) * 128],
                                    tT_all[:, db * 1024 + lch * 512: db * 1024 + (lch + 1) * 512],
                                    start=(db == 0), stop=(db == NDB - 1)),
                                    inc=(sT if db == NDB - 1 else None))
                        mm_hist[jb] = sT.n
                        for lch in range(2):
                            pg, pu = ps[lch * 2], ps[lch * 2 + 1]
                            em.w("scalar", sT, mm_hist[jb] - 2 * (1 - lch))
                            em.w("scalar", sV)
                            em.o("scalar", lambda e, pg=pg, lch=lch: e.activation(
                                sil_b[:, lch * 512:(lch + 1) * 512], pg[:, :], AF.Silu),
                                inc=sC)
                            consume(pg, sC)
                            em.w("vector", sT, mm_hist[jb] - 2 * (1 - lch))
                            em.w("vector", sC)
                            em.w("vector", qS, cw_ld)
                            em.o("vector", lambda e, lch=lch, pu=pu: e.scalar_tensor_tensor(
                                uw_b[:, lch * 512:(lch + 1) * 512], pu[:, :], 1.0,
                                cwrep[:, lch * 512:(lch + 1) * 512],
                                ALU.mult, ALU.mult), inc=sV)
                            consume(pu, sV)
                            em.o("vector", lambda e, e2=e2, jb=jb, lch=lch: e.tensor_mul(
                                hh_w[:, (e2 * NIB + jb) * 1024 + lch * 512:
                                     (e2 * NIB + jb) * 1024 + (lch + 1) * 512],
                                uw_b[:, lch * 512:(lch + 1) * 512],
                                sil_b[:, lch * 512:(lch + 1) * 512]), inc=sV)
                for dob in range(NDB):
                    em.w("gpsimd", sT)
                    for ee in range(2):
                        em.dma("gpsimd", lambda e, dob=dob, ee=ee: e.dma_start(
                            out=ed_b[:, ee * EI:(ee + 1) * EI]
                            .rearrange("p (b n) -> p b n", b=NIB),
                            in_=z["edT"][ee:ee + 1, :, dob * 128:(dob + 1) * 128]
                            .rearrange("o (b p) n -> p b (o n)", p=128)), qG)
                    em.w("tensor", qG)
                    em.w("tensor", sV)
                    for lch in range(2):
                        pst = ps[2 + lch]
                        mm_guard(pst)
                        for e2 in range(2):
                            for ib in range(NIB):
                                em.o("tensor", lambda e, e2=e2, ib=ib, lch=lch, pst=pst:
                                     e.matmul(
                                         pst[:, :],
                                         ed_b[:, e2 * EI + ib * 128: e2 * EI + (ib + 1) * 128],
                                         hh_w[:, (e2 * NIB + ib) * 1024 + lch * 512:
                                              (e2 * NIB + ib) * 1024 + (lch + 1) * 512],
                                         start=(e2 == 0 and ib == 0),
                                         stop=(e2 == 1 and ib == NIB - 1)),
                                     inc=(sT if (e2 == 1 and ib == NIB - 1) else None))
                        em.w("vector", sT)
                        em.w("vector", qS)
                        em.o("vector", lambda e, lch=lch, pst=pst: e.tensor_copy(
                            mstg[:, lch * 512:(lch + 1) * 512], pst[:, :]), inc=sV)
                        consume(pst, sV)
                    em.w("sync", sV)
                    em.dma("sync", lambda e, s=s_seq, dob=dob, qt=qt: e.dma_start(
                        out=z["mrs_in"][s].ap().rearrange("(j m) n -> j m n", m=DM)
                        [2 * qt:2 * qt + 2, dob * 128:(dob + 1) * 128, :]
                        .rearrange("j m n -> m j n"),
                        in_=mstg[:, :].rearrange("p (j n) -> p j n", j=2)), qS)
            em.w("gpsimd", qS)
            em.w("gpsimd", cc)
            em.o("gpsimd", lambda e, s=s_seq: e.collective_compute(
                "ReduceScatter", ALU.add, replica_groups=[list(range(NCORE))],
                ins=[z["mrs_in"][s].ap()], outs=[z["mrs_out"][s].ap()]), inc=cc)
        cc_mrs = cc.n

        with ExitStack() as hk:
            C = lambda n, s, d: hk.enter_context(nc.sbuf_tensor(n + "S" + str(s_seq), s, d))
            sg_b = C("sg_b", [128, NDB * 128], BF16)
            su_b = C("su_b", [128, NDB * 128], BF16)
            sd_b = C("sd_b", [128, DM], BF16)
            hhs = C("hhs", [128, NSB * 512], BF16)
            ssil = C("ssil", [128, 512], BF16)
            comb = C("comb", [128, 4 * DM], F32)
            x2ic = C("x2ic", [128, 512], F32)
            rt_sb = C("rt_sb", [128, DM], BF16)
            mrs_sb = C("mrs_sb", [128, DM], BF16)
            ffn_t = C("ffn_t", [128, DM], F32)
            monl2 = C("monl2", [128, DM], F32)
            x1e = C("x1e", [128, DM], F32)
            sqe2 = C("sqe2", [128, DM], F32)

            for sib in range(NSB):
                em.w("gpsimd", sT)
                em.dma("gpsimd", lambda e, sib=sib: e.dma_start(
                    out=sg_b[:, :].rearrange("p (b n) -> p b n", b=NDB),
                    in_=z["sgT"][:, sib * 128:(sib + 1) * 128]
                    .rearrange("(b p) n -> p b n", p=128)), qG)
                em.dma("gpsimd", lambda e, sib=sib: e.dma_start(
                    out=su_b[:, :].rearrange("p (b n) -> p b n", b=NDB),
                    in_=z["suT"][:, sib * 128:(sib + 1) * 128]
                    .rearrange("(b p) n -> p b n", p=128)), qG)
                em.w("tensor", qG)
                pg, pu = ps[0], ps[1]
                mm_guard(pg)
                for db in range(NDB):
                    em.o("tensor", lambda e, db=db, pg=pg: e.matmul(
                        pg[:, :], sg_b[:, db * 128:(db + 1) * 128],
                        tT_b[:, db * 512:(db + 1) * 512],
                        start=(db == 0), stop=(db == NDB - 1)),
                        inc=(sT if db == NDB - 1 else None))
                mm_guard(pu)
                for db in range(NDB):
                    em.o("tensor", lambda e, db=db, pu=pu: e.matmul(
                        pu[:, :], su_b[:, db * 128:(db + 1) * 128],
                        tT_b[:, db * 512:(db + 1) * 512],
                        start=(db == 0), stop=(db == NDB - 1)),
                        inc=(sT if db == NDB - 1 else None))
                em.w("scalar", sT)
                em.w("scalar", sV)
                em.o("scalar", lambda e, pg=pg: e.activation(
                    ssil[:, :], pg[:, :], AF.Silu), inc=sC)
                consume(pg, sC)
                em.w("vector", sT)
                em.w("vector", sC)
                em.o("vector", lambda e, sib=sib, pu=pu: e.scalar_tensor_tensor(
                    hhs[:, sib * 512:(sib + 1) * 512], pu[:, :], 1.0,
                    ssil[:, :], ALU.mult, ALU.mult), inc=sV)
                consume(pu, sV)
            if "hhs" in dbg_outs:
                for j in range(4):
                    em.w("vector", sV)
                    em.o("vector", lambda e, j=j: e.tensor_copy(
                        sqe2[:, 0:512], hhs[:, j * 512:(j + 1) * 512]), inc=sV)
                    em.w("sync", sV)
                    em.dma("sync", lambda e, s=s_seq, j=j: e.dma_start(
                        out=dbg_outs["hhs"][s * 128:(s + 1) * 128, j * 512:(j + 1) * 512],
                        in_=sqe2[:, 0:512]), qS)
                    em.w("vector", qS)
            for dq in range(4):
                for sib in range(NSB):
                    em.w("gpsimd", sT)
                    em.dma("gpsimd", lambda e, sib=sib, dq=dq: e.dma_start(
                        out=sd_b[:, 0:512],
                        in_=z["sdT"][sib * 128:(sib + 1) * 128, dq * 512:(dq + 1) * 512]), qG)
                    em.w("tensor", qG)
                    em.w("tensor", sV)
                    for lb in range(4):
                        pst = ps[lb]
                        if sib == 0:
                            mm_guard(pst)
                        em.o("tensor", lambda e, sib=sib, lb=lb, pst=pst: e.matmul(
                            pst[:, :], hhs[:, sib * 512 + lb * 128: sib * 512 + (lb + 1) * 128],
                            sd_b[:, 0:512],
                            start=(sib == 0), stop=(sib == NSB - 1)),
                            inc=(sT if lb == 3 else None))
                em.w("vector", sT)
                for lb in range(4):
                    em.w("sync", sV)
                    em.dma("sync", lambda e, s=s_seq, lb=lb, dq=dq: e.dma_start(
                        out=x2ic[:, :],
                        in_=sp_x2i.ap()[s * 512 + lb * 128: s * 512 + (lb + 1) * 128,
                                        dq * 512:(dq + 1) * 512]), qS)
                    em.w("vector", qS)
                    em.o("vector", lambda e, lb=lb, dq=dq: e.tensor_add(
                        comb[:, lb * DM + dq * 512: lb * DM + (dq + 1) * 512],
                        x2ic[:, :], ps[lb][:, :]), inc=sV)
                    consume(ps[lb], sV)
            em.w("sync", cc, cc_mrs)
            for lb in range(4):
                em.w("sync", sV)
                em.dma("sync", lambda e, s=s_seq, lb=lb: e.dma_start(
                    out=mrs_sb[:, :].rearrange("p (b n) -> p b n", b=NDB),
                    in_=z["mrs_out"][s].ap().rearrange("(b p) n -> p b n", p=128)
                    [:, :, lb * 128:(lb + 1) * 128]), qS)
                em.w("tensor", qS)
                em.w("tensor", sC)
                for db in range(NDB):
                    mm_guard(ptbf)
                    em.o("tensor", lambda e, db=db: e.transpose(
                        ptbf[:, 0:128], mrs_sb[:, db * 128:(db + 1) * 128], ident_b[:, :]),
                        inc=sT)
                    em.w("scalar", sT)
                    em.o("scalar", lambda e, db=db: e.copy(
                        rt_sb[:, db * 128:(db + 1) * 128], ptbf[:, 0:128]), inc=sC)
                    consume(ptbf, sC)
                if "shr" in dbg_outs:
                    em.w("sync", sV)
                    em.dma("sync", lambda e, s=s_seq, lb=lb: e.dma_start(
                        out=dbg_outs["shr"][s * 512 + lb * 128: s * 512 + (lb + 1) * 128, :],
                        in_=comb[:, lb * DM:(lb + 1) * DM]), qS)
                em.w("vector", sC)
                em.o("vector", lambda e, lb=lb: e.tensor_add(
                    ffn_t[:, :], comb[:, lb * DM:(lb + 1) * DM], rt_sb[:, :]), inc=sV)
                if "routed" in dbg_outs:
                    em.w("sync", sV)
                    em.dma("sync", lambda e, s=s_seq, lb=lb: e.dma_start(
                        out=dbg_outs["routed"][s * 512 + lb * 128: s * 512 + (lb + 1) * 128, :],
                        in_=ffn_t[:, :]), qS)
                ln_stats(ffn_t, DM, 34, EPS_LN, sqe2)
                em.o("vector", lambda e: e.tensor_scalar(
                    monl2[:, :], ffn_t[:, :], st[:, 36:37], st[:, 39:40],
                    ALU.subtract, ALU.mult), inc=sV)
                em.w("sync", sV)
                em.dma("sync", lambda e, s=s_seq, lb=lb: e.dma_start(
                    out=x1e[:, :],
                    in_=z["sp_x1"].ap()[s * 512 + lb * 128: s * 512 + (lb + 1) * 128, :]), qS)
                em.w("vector", qS)
                em.o("vector", lambda e: e.tensor_add(sqe2[:, :], monl2[:, :], x1e[:, :]),
                     inc=sV)
                em.w("sync", sV)
                em.dma("sync", lambda e, s=s_seq, lb=lb: e.dma_start(
                    out=z["out_own"][s * 512 + lb * 128: s * 512 + (lb + 1) * 128, :],
                    in_=sqe2[:, :]), qS)


def make_core_inputs(inputs, c):
    T = lambda a: np.ascontiguousarray(a, dtype=np.float32)
    x = inputs["L0_attn_x"]
    d = {}
    d["x_own"] = T(np.concatenate(
        [x[0, 512 * c:512 * (c + 1)], x[1, 512 * c:512 * (c + 1)]], axis=0))
    d["A_slc"] = T(inputs["L0_attn_A_log"][256 * c:256 * (c + 1)])
    d["Dcol"] = T(np.asarray(inputs["L0_attn_D_param"]).reshape(NDB, 128).T)
    d["nwrow"] = T(np.asarray(inputs["L0_attn_norm_w"])[None, :])
    d["nbrow"] = T(np.asarray(inputs["L0_attn_norm_b"])[None, :])
    d["dtbrow"] = T(np.asarray(inputs["L0_attn_dt_proj_b"])[None, :])
    d["l1row"] = T(np.asarray(inputs["L0_ffn_ln1_w"])[None, :])
    d["l2row"] = T(np.asarray(inputs["L0_ffn_ln2_w"])[None, :])
    d["xpwT"] = T(np.asarray(inputs["L0_attn_x_proj_w"]).T)
    d["dtwT"] = T(np.asarray(inputs["L0_attn_dt_proj_w"]).T)
    d["opwT"] = T(np.asarray(inputs["L0_attn_out_proj_w"]).T)
    d["wq_own"] = T(np.asarray(inputs["L0_ffn_wq"])[:, 256 * c:256 * (c + 1)])
    d["wk_own"] = T(np.asarray(inputs["L0_ffn_wk"])[:, 256 * c:256 * (c + 1)])
    d["wv_own"] = T(np.asarray(inputs["L0_ffn_wv"])[:, 256 * c:256 * (c + 1)])
    d["wo_own"] = T(np.asarray(inputs["L0_ffn_wo"])[256 * c:256 * (c + 1), :])
    d["gwT"] = T(np.asarray(inputs["L0_ffn_gate_w"]).T)
    eg = np.asarray(inputs["L0_ffn_eg_w"])
    eu = np.asarray(inputs["L0_ffn_eu_w"])
    ed = np.asarray(inputs["L0_ffn_ed_w"])
    d["egT"] = T(np.stack([eg[2 * c].T, eg[2 * c + 1].T]))
    d["euT"] = T(np.stack([eu[2 * c].T, eu[2 * c + 1].T]))
    d["edT"] = T(np.stack([ed[2 * c].T, ed[2 * c + 1].T]))
    d["sgT"] = T(np.asarray(inputs["L0_ffn_sg_w"]).T)
    d["suT"] = T(np.asarray(inputs["L0_ffn_su_w"]).T)
    d["sdT"] = T(np.asarray(inputs["L0_ffn_sd_w"]).T)
    return d


def kernel(**inputs):
    nc, _ = build()
    in_maps = [make_core_inputs(inputs, c) for c in range(NCORE)]
    res = run_bass_kernel_spmd(nc, in_maps, core_ids=list(range(NCORE)))
    out = np.zeros((NSEQ, SEQ, DM), np.float32)
    for c in range(NCORE):
        o = res.results[c]["out_own"]
        out[0, 512 * c:512 * (c + 1)] = o[0:OWN]
        out[1, 512 * c:512 * (c + 1)] = o[OWN:OWNT]
    return out


def run_debug(inputs, phases, dbg):
    nc, dbg_outs = build(phases=phases, dbg=dbg)
    in_maps = [make_core_inputs(inputs, c) for c in range(NCORE)]
    res = run_bass_kernel_spmd(nc, in_maps, core_ids=list(range(NCORE)))
    return res


# version 04
# speedup vs baseline: 1.4837x; 1.0691x over previous
import math
import numpy as np
import concourse.bass as bass
import concourse.mybir as mybir
from concourse.bass_utils import run_bass_kernel_spmd
from concourse.aot_env import allow_ici_all_to_all
from contextlib import ExitStack

F32 = mybir.dt.float32
BF16 = mybir.dt.bfloat16
FP16 = mybir.dt.float16
AF = mybir.ActivationFunctionType
ALU = mybir.AluOpType
AX = mybir.AxisListType.X

NCORE = 8
SEQ = 4096
NSEQ = 2
DM = 2048
OWN = 512
OWNT = NSEQ * OWN
DSL = 256
NST = 64
RK = 128
HD = 128
NE = 16
EI = 1408
SI = 2816
EPS_LN = 1e-5
EPS_RMS = 1e-6
NDB = DM // 128
NIB = EI // 128
NSB = SI // 128
QSCALE = 1.0 / math.sqrt(HD)

ENGS = ("sync", "gpsimd", "tensor", "vector", "scalar")


class S:
    def __init__(self, nc, name):
        self.sem = nc.alloc_semaphore(name)
        self.n = 0
        self.name = name


class Em:
    def __init__(self):
        self.q = {k: [] for k in ENGS}

    def scope(self, name):
        for e in ENGS:
            self.q[e].append(("scope", name))

    def pop(self):
        for e in ENGS:
            self.q[e].append(("pop", None))

    def w(self, eng, sem, val=None):
        v = sem.n if val is None else val
        if v > 0:
            self.q[eng].append(lambda e, s=sem.sem, v=v: e.wait_ge(s, v))

    def o(self, eng, f, inc=None, inck=1):
        if inc is None:
            self.q[eng].append(f)
        else:
            inc.n += inck
            self.q[eng].append(lambda e, f=f, s=inc.sem, k=inck: f(e).then_inc(s, k))

    def dma(self, eng, f, sem):
        self.o(eng, f, inc=sem, inck=16)

    def barrier(self, bars, engs=ENGS):
        marks = {}
        for e in engs:
            self.o(e, lambda x: x.drain(), inc=bars[e])
            marks[e] = bars[e].n
        for e in engs:
            for e2 in engs:
                if e2 != e:
                    self.w(e, bars[e2], marks[e2])


def build(phases="ABCDE", dbg=()):
    nc = bass.Bass("TRN2", target_bir_lowering=False)

    def inp(name, shape, dtype=F32):
        return nc.dram_tensor(name, shape, dtype, kind="ExternalInput").ap()

    x_own = inp("x_own", [OWNT, DM])
    A_slc = inp("A_slc", [DSL, NST])
    Dcol = inp("Dcol", [128, NDB])
    nwrow = inp("nwrow", [1, DM])
    nbrow = inp("nbrow", [1, DM])
    dtbrow = inp("dtbrow", [1, DM])
    l1row = inp("l1row", [1, DM])
    l2row = inp("l2row", [1, DM])
    xpwT = inp("xpwT", [DM, 2 * NST + RK])
    dtwT = inp("dtwT", [RK, DM])
    opwT = inp("opwT", [DM, DM])
    wq_own = inp("wq_own", [DM, 2 * HD])
    wk_own = inp("wk_own", [DM, 2 * HD])
    wv_own = inp("wv_own", [DM, 2 * HD])
    wo_own = inp("wo_own", [2 * HD, DM])
    gwT = inp("gwT", [DM, NE])
    egT = inp("egT", [2, DM, EI])
    euT = inp("euT", [2, DM, EI])
    edT = inp("edT", [2, EI, DM])
    sgT = inp("sgT", [DM, SI])
    suT = inp("suT", [DM, SI])
    sdT = inp("sdT", [SI, DM])

    out_own = nc.dram_tensor("out_own", [OWNT, DM], F32, kind="ExternalOutput").ap()
    dbg_outs = {}
    for nm, shp in (("lx", [OWNT, DM]), ("xn", [OWNT, DM]), ("dt", [OWNT, DM]),
                    ("y", [NSEQ * DSL, SEQ]), ("x1", [OWNT, DM]),
                    ("hln", [OWNT, DM]), ("attn", [OWNT, DM]),
                    ("cw", [OWNT, NE]), ("x2i", [OWNT, DM]),
                    ("qt", [128, 2 * SEQ]), ("kt", [128, 2 * SEQ]),
                    ("htb", [128, NDB * 512]), ("hti", [128, 512]), ("hbt", [128, DM]), ("stg", [128, DM]), ("gl", [OWNT, NE]), ("shr", [OWNT, DM]), ("hhs", [NSEQ * 128, 2048]),
                    ("vv", [128, 32 * 256]), ("aot", [128, 2 * SEQ]),
                    ("routed", [OWNT, DM]), ("shared", [OWNT, DM])):
        if nm in dbg:
            dbg_outs[nm] = nc.dram_tensor("dbg_" + nm, shp, F32,
                                          kind="ExternalOutput").ap()

    e1_in = nc.dram_tensor("e1_in", [NCORE * 640, 1024], FP16)
    e1_out = nc.dram_tensor("e1_out", [NCORE * 640, 1024], FP16)
    e2_in = [nc.dram_tensor(f"e2_in{s}", [DM, OWN], BF16) for s in range(NSEQ)]
    e2_out = [nc.dram_tensor(f"e2_out{s}", [DM, OWN], BF16) for s in range(NSEQ)]
    ht_in = [nc.dram_tensor(f"ht_in{s}", [DM, OWN], BF16) for s in range(NSEQ)]
    ht_out = [nc.dram_tensor(f"ht_out{s}", [NCORE * DM, OWN], BF16) for s in range(NSEQ)]
    ars_in = [nc.dram_tensor(f"ars_in{s}", [SEQ, DM], BF16) for s in range(NSEQ)]
    ars_out = [nc.dram_tensor(f"ars_out{s}", [OWN, DM], BF16) for s in range(NSEQ)]
    tt_in = [nc.dram_tensor(f"tt_in{s}", [DM, OWN], BF16) for s in range(NSEQ)]
    tt_out = [nc.dram_tensor(f"tt_out{s}", [NCORE * DM, OWN], BF16) for s in range(NSEQ)]
    cw_in = [nc.dram_tensor(f"cw_in{s}", [NE, OWN], F32) for s in range(NSEQ)]
    cw_out = [nc.dram_tensor(f"cw_out{s}", [NE, OWN], F32) for s in range(NSEQ)]
    mrs_in = [nc.dram_tensor(f"mrs_in{s}", [NCORE * DM, OWN], BF16) for s in range(NSEQ)]
    mrs_out = [nc.dram_tensor(f"mrs_out{s}", [DM, OWN], BF16) for s in range(NSEQ)]
    sp_lx = nc.dram_tensor("sp_lx", [OWNT, DM], F32)
    sp_x1 = nc.dram_tensor("sp_x1", [OWNT, DM], F32)
    sp_lx1 = nc.dram_tensor("sp_lx1", [OWNT, DM], F32)
    sp_x2i = nc.dram_tensor("sp_x2i", [OWNT, DM], F32)

    ident_dram = nc.inline_tensor(np.eye(128, dtype=np.float32), "identf")

    qS = S(nc, "qS")
    qG = S(nc, "qG")
    cc = S(nc, "cc")
    sT = S(nc, "sT")
    sV = S(nc, "sV")
    sC = S(nc, "sC")
    sSy = S(nc, "sSy")
    sGp = S(nc, "sGp")
    BAR = {"sync": sSy, "gpsimd": sGp, "tensor": sT, "vector": sV, "scalar": sC}

    em = Em()

    stk = ExitStack()
    ident_f = stk.enter_context(nc.sbuf_tensor("ident_f", [128, 128], F32))
    ident_b = stk.enter_context(nc.sbuf_tensor("ident_b", [128, 128], BF16))
    ident_h = stk.enter_context(nc.sbuf_tensor("ident_h", [128, 128], FP16))
    Dcol_sb = stk.enter_context(nc.sbuf_tensor("Dcol_sb", [128, NDB], F32))
    A_sb = stk.enter_context(nc.sbuf_tensor("A_sb", [128, 2 * NST], F32))
    st = stk.enter_context(nc.sbuf_tensor("stats", [128, 48], F32))
    kcol = stk.enter_context(nc.sbuf_tensor("kcol", [128, 4], F32))
    ps = [stk.enter_context(nc.psum_tensor(f"ps{i}", [128, 512], F32)) for i in range(4)]
    ptA = stk.enter_context(nc.psum_tensor("ptA", [128, 128], F32))
    ptB = stk.enter_context(nc.psum_tensor("ptB", [128, 128], F32))
    pt16 = stk.enter_context(nc.psum_tensor("pt16", [128, 128], FP16))
    ptbf = stk.enter_context(nc.psum_tensor("ptbf", [128, 1024], BF16))

    ps_guard = {}

    def mm_guard(pst):
        g = ps_guard.get(id(pst))
        if g:
            em.w("tensor", g[0], g[1])

    def consume(pst, sem):
        ps_guard[id(pst)] = (sem, sem.n)

    em.dma("sync", lambda e: e.dma_start(out=ident_f[:, :], in_=ident_dram.ap()), qS)
    em.dma("sync", lambda e: e.dma_start(out=Dcol_sb[:, :], in_=Dcol), qS)
    em.dma("sync", lambda e: e.dma_start(
        out=A_sb[:, :].rearrange("p (b n) -> p b n", b=2),
        in_=A_slc.rearrange("(b p) n -> p b n", p=128)), qS)
    em.w("vector", qS)
    em.o("vector", lambda e: e.tensor_copy(ident_b[:, :], ident_f[:, :]))
    em.o("vector", lambda e: e.memset(kcol[:, 0:1], EPS_LN))
    em.o("vector", lambda e: e.memset(kcol[:, 1:2], EPS_RMS))
    em.o("vector", lambda e: e.memset(kcol[:, 2:3], QSCALE))
    em.o("vector", lambda e: e.tensor_copy(ident_h[:, :], ident_f[:, :]), inc=sV)
    em.w("scalar", sV)
    em.w("scalar", qS)
    em.o("scalar", lambda e: e.activation(A_sb[:, :], A_sb[:, :], AF.Exp))
    em.o("scalar", lambda e: e.activation(A_sb[:, :], A_sb[:, :], AF.Copy, scale=-1.0),
         inc=sC)

    def tv(f):
        em.w("vector", sV)
        em.o("vector", f, inc=sV)

    def ln_stats(xt, F, c0, eps, scratch):
        em.w("scalar", sV)
        em.w("vector", sC)
        em.o("vector", lambda e: e.reduce_sum(st[:, c0:c0 + 1], xt[:, 0:F], axis=AX),
             inc=sV)
        em.o("scalar", lambda e: e.activation(
            scratch[:, 0:F], xt[:, 0:F], AF.Square, accum_out=st[:, c0 + 1:c0 + 2]),
            inc=sC)
        em.w("vector", sC)
        tv(lambda e: e.tensor_scalar(
            st[:, c0 + 2:c0 + 3], st[:, c0:c0 + 1], 1.0 / F, None, ALU.mult))
        tv(lambda e: e.tensor_mul(
            st[:, c0 + 3:c0 + 4], st[:, c0 + 2:c0 + 3], st[:, c0 + 2:c0 + 3]))
        tv(lambda e: e.tensor_scalar(
            st[:, c0 + 4:c0 + 5], st[:, c0 + 1:c0 + 2], 1.0 / F, None, ALU.mult))
        tv(lambda e: e.tensor_sub(
            st[:, c0 + 5:c0 + 6], st[:, c0 + 4:c0 + 5], st[:, c0 + 3:c0 + 4]))
        em.w("scalar", sV)
        em.o("scalar", lambda e: e.activation(
            st[:, c0 + 4:c0 + 5], st[:, c0 + 5:c0 + 6], AF.Sqrt, bias=kcol[:, 0:1]),
            inc=sC)
        em.w("vector", sC)
        tv(lambda e: e.reciprocal(st[:, c0 + 5:c0 + 6], st[:, c0 + 4:c0 + 5]))
        em.w("vector", sV)

    def rms_stats(xt, F, c0, eps, scratch):
        em.w("scalar", sV)
        em.o("scalar", lambda e: e.activation(
            scratch[:, 0:F], xt[:, 0:F], AF.Square, accum_out=st[:, c0:c0 + 1]),
            inc=sC)
        em.w("vector", sC)
        tv(lambda e: e.tensor_scalar(
            st[:, c0 + 2:c0 + 3], st[:, c0:c0 + 1], 1.0 / F, None, ALU.mult))
        em.w("scalar", sV)
        em.o("scalar", lambda e: e.activation(
            st[:, c0:c0 + 1], st[:, c0 + 2:c0 + 3], AF.Sqrt, bias=kcol[:, 1:2]), inc=sC)
        em.w("vector", sC)
        tv(lambda e: e.reciprocal(st[:, c0 + 1:c0 + 2], st[:, c0:c0 + 1]))
        em.w("vector", sV)

    if "A" in phases:
        em.scope("phA")
        with ExitStack() as pk:
            A = lambda n, s, d: pk.enter_context(nc.sbuf_tensor(n, s, d))
            xa = A("xa", [128, DM], F32)
            lxa = A("lxa", [128, DM], F32)
            sqa = A("sqa", [128, DM], F32)
            ln2a = A("ln2a", [128, DM], F32)
            xna = A("xna", [128, DM], F32)
            xna_b = A("xna_b", [128, DM], BF16)
            xnT_b = A("xnT_b", [128, NDB * 128], BF16)
            nw_rep = A("nw_rep", [128, DM], F32)
            nb_rep = A("nb_rep", [128, DM], F32)
            dtb_rep = A("dtb_rep", [128, DM], F32)
            xpw_b = A("xpw_b", [128, NDB * 256], BF16)
            dtw_b = A("dtw_b", [128, DM], BF16)
            pj_del = A("pj_del", [128, 128], BF16)
            pbc_h = A("pbc_h", [128, 128], FP16)
            dta = A("dta", [128, DM], F32)
            spa = A("spa", [128, DM], F32)
            dtxa = A("dtxa", [128, DM], F32)
            dt_h = A("dt_h", [128, DM], FP16)
            dtx_h = A("dtx_h", [128, DM], FP16)
            stag = A("stag", [128, NCORE * 5 * 128], FP16)

            em.dma("sync", lambda e: e.dma_start(out=nw_rep[:, :], in_=nwrow.broadcast_to([128, DM])), qS)
            em.dma("sync", lambda e: e.dma_start(out=nb_rep[:, :], in_=nbrow.broadcast_to([128, DM])), qS)
            em.dma("sync", lambda e: e.dma_start(out=dtb_rep[:, :], in_=dtbrow.broadcast_to([128, DM])), qS)
            em.dma("gpsimd", lambda e: e.dma_start(
                out=xpw_b[:, :].rearrange("p (b n) -> p b n", b=NDB),
                in_=xpwT.rearrange("(b p) n -> p b n", p=128)), qG)
            em.dma("gpsimd", lambda e: e.dma_start(out=dtw_b[:, :], in_=dtwT), qG)

            for t in range(8):
                em.w("sync", sV)
                em.w("sync", sC)
                em.dma("sync", lambda e, t=t: e.dma_start(
                    out=xa[:, :], in_=x_own[t * 128:(t + 1) * 128, :]), qS)
                em.w("vector", qS)
                em.w("vector", sT)
                em.w("scalar", qS)
                em.w("scalar", sT)
                ln_stats(xa, DM, 0, EPS_LN, sqa)
                em.o("vector", lambda e: e.tensor_scalar(
                    lxa[:, :], xa[:, :], st[:, 2:3], st[:, 5:6],
                    ALU.subtract, ALU.mult), inc=sV)
                em.w("sync", sV)
                em.dma("sync", lambda e, t=t: e.dma_start(
                    out=sp_lx.ap()[t * 128:(t + 1) * 128, :], in_=lxa[:, :]), qS)
                if "lx" in dbg_outs:
                    em.dma("sync", lambda e, t=t: e.dma_start(
                        out=dbg_outs["lx"][t * 128:(t + 1) * 128, :], in_=lxa[:, :]), qS)
                ln_stats(lxa, DM, 6, EPS_LN, sqa)
                em.o("vector", lambda e: e.tensor_scalar(
                    ln2a[:, :], lxa[:, :], st[:, 8:9], st[:, 11:12],
                    ALU.subtract, ALU.mult))
                em.o("vector", lambda e: e.tensor_mul(sqa[:, :], ln2a[:, :], nw_rep[:, :]))
                em.o("vector", lambda e: e.tensor_add(xna[:, :], sqa[:, :], nb_rep[:, :]))
                em.o("vector", lambda e: e.tensor_copy(xna_b[:, :], xna[:, :]), inc=sV)
                if "xn" in dbg_outs:
                    em.w("sync", sV)
                    em.dma("sync", lambda e, t=t: e.dma_start(
                        out=dbg_outs["xn"][t * 128:(t + 1) * 128, :], in_=xna[:, :]), qS)
                em.w("tensor", sV)
                em.w("tensor", qG)
                for db in range(NDB):
                    mm_guard(ptbf)
                    em.o("tensor", lambda e, db=db: e.transpose(
                        ptbf[:, 0:128], xna_b[:, db * 128:(db + 1) * 128], ident_b[:, :]),
                        inc=sT)
                    em.w("scalar", sT)
                    em.o("scalar", lambda e, db=db: e.copy(
                        xnT_b[:, db * 128:(db + 1) * 128], ptbf[:, 0:128]), inc=sC)
                    consume(ptbf, sC)
                em.w("tensor", sC)
                for pr in range(2):
                    pst = ptA if pr == 0 else ptB
                    mm_guard(pst)
                    for db in range(NDB):
                        em.o("tensor", lambda e, pr=pr, db=db, pst=pst: e.matmul(
                            pst[:, :],
                            xpw_b[:, db * 256 + pr * 128: db * 256 + (pr + 1) * 128],
                            xnT_b[:, db * 128:(db + 1) * 128],
                            start=(db == 0), stop=(db == NDB - 1)),
                            inc=(sT if db == NDB - 1 else None))
                    em.w("vector", sT)
                    dst = pj_del if pr == 0 else pbc_h
                    em.o("vector", lambda e, dst=dst, pst=pst: e.tensor_copy(
                        dst[:, :], pst[:, :]), inc=sV)
                    consume(pst, sV)
                em.w("scalar", sV)
                for sh in range(NCORE):
                    em.o("scalar", lambda e, sh=sh: e.copy(
                        stag[:, (sh * 5 + 4) * 128:(sh * 5 + 5) * 128], pbc_h[:, :]),
                        inc=(sC if sh == NCORE - 1 else None))
                em.w("tensor", sV)
                for dch in range(4):
                    pst = ps[dch]
                    mm_guard(pst)
                    em.o("tensor", lambda e, dch=dch, pst=pst: e.matmul(
                        pst[:, :], pj_del[:, :], dtw_b[:, dch * 512:(dch + 1) * 512],
                        start=True, stop=True), inc=sT)
                em.w("vector", sT)
                for dch in range(4):
                    em.o("vector", lambda e, dch=dch: e.tensor_add(
                        dta[:, dch * 512:(dch + 1) * 512], ps[dch][:, :],
                        dtb_rep[:, dch * 512:(dch + 1) * 512]),
                        inc=(sV if dch == 3 else None))
                for dch in range(4):
                    consume(ps[dch], sV)
                em.w("scalar", sV)
                em.o("scalar", lambda e: e.activation(spa[:, :], dta[:, :], AF.Abs))
                em.o("scalar", lambda e: e.activation(spa[:, :], spa[:, :], AF.Exp, scale=-1.0))
                em.o("scalar", lambda e: e.activation(spa[:, :], spa[:, :], AF.Ln, bias=1.0),
                     inc=sC)
                em.w("vector", sC)
                em.o("vector", lambda e: e.tensor_relu(sqa[:, :], dta[:, :]))
                em.o("vector", lambda e: e.tensor_add(dta[:, :], sqa[:, :], spa[:, :]),
                     inc=sV)
                if "dt" in dbg_outs:
                    em.w("sync", sV)
                    em.dma("sync", lambda e, t=t: e.dma_start(
                        out=dbg_outs["dt"][t * 128:(t + 1) * 128, :], in_=dta[:, :]), qS)
                em.o("vector", lambda e: e.tensor_mul(dtxa[:, :], dta[:, :], xna[:, :]))
                em.o("vector", lambda e: e.tensor_copy(dt_h[:, :], dta[:, :]))
                em.o("vector", lambda e: e.tensor_copy(dtx_h[:, :], dtxa[:, :]), inc=sV)
                em.w("tensor", sV)
                em.w("scalar", qS)
                for which, src in ((0, dt_h), (1, dtx_h)):
                    for db in range(NDB):
                        sh, half = db // 2, db % 2
                        piece = which * 2 + half
                        mm_guard(pt16)
                        em.o("tensor", lambda e, src=src, db=db: e.transpose(
                            pt16[:, :], src[:, db * 128:(db + 1) * 128], ident_h[:, :]),
                            inc=sT)
                        em.w("scalar", sT)
                        em.o("scalar", lambda e, sh=sh, piece=piece: e.copy(
                            stag[:, (sh * 5 + piece) * 128:(sh * 5 + piece + 1) * 128],
                            pt16[:, :]), inc=sC)
                        consume(pt16, sC)
                em.w("sync", sC)
                em.w("sync", qS)
                em.dma("sync", lambda e, t=t: e.dma_start(
                    out=e1_in.ap().rearrange("(f p) n -> p f n", p=128)
                    [:, :, t * 128:(t + 1) * 128],
                    in_=stag[:, :].rearrange("p (f n) -> p f n", n=128)), qS)
                em.w("sync", qS)

            em.w("gpsimd", qS)
            with allow_ici_all_to_all():
                em.w("gpsimd", cc)
                em.o("gpsimd", lambda e: e.collective_compute(
                    "AllToAll", ALU.bypass, replica_groups=[list(range(NCORE))],
                    ins=[e1_in.ap()], outs=[e1_out.ap()]), inc=cc)
            em.pop()
            em.barrier(BAR)

    if "B" in phases:
        em.scope("phB")
        cc_e1 = cc.n
        with ExitStack() as pk:
            A = lambda n, s, d: pk.enter_context(nc.sbuf_tensor(n, s, d))
            dtT = A("dtT", [128, 2 * SEQ], FP16)
            dtxT = A("dtxT", [128, 2 * SEQ], FP16)
            Brep = A("Brep", [128, 2 * SEQ], FP16)
            Crep = A("Crep", [128, 2 * SEQ], FP16)
            dAb = A("dAb", [128, 2 * SEQ], F32)
            ub = A("ub", [128, SEQ], FP16)
            hb = A("hb", [128, SEQ], FP16)
            qscan = A("qscan", [128, SEQ], FP16)
            yb = A("yb", [128, 4 * SEQ], FP16)
            ybf = A("ybf", [128, SEQ], BF16)
            zero1 = A("zero1", [128, 1], F32)
            em.o("vector", lambda e: e.memset(zero1[:, :], 0.0), inc=sV)

            e1o = e1_out.ap().rearrange("(r m) n -> r m n", m=640)
            for s_seq in range(NSEQ):
                em.w("sync", cc, cc_e1)
                em.w("sync", sV)
                for k in range(2):
                    em.dma("sync", lambda e, k=k, s=s_seq: e.dma_start(
                        out=dtT[:, k * SEQ:(k + 1) * SEQ].rearrange("p (r n) -> p r n", r=8),
                        in_=e1o[:, k * 128:(k + 1) * 128, s * 512:(s + 1) * 512]
                        .rearrange("r m n -> m r n")), qS)
                    em.dma("sync", lambda e, k=k, s=s_seq: e.dma_start(
                        out=dtxT[:, k * SEQ:(k + 1) * SEQ].rearrange("p (r n) -> p r n", r=8),
                        in_=e1o[:, 256 + k * 128:256 + (k + 1) * 128, s * 512:(s + 1) * 512]
                        .rearrange("r m n -> m r n")), qS)
                for k in range(2):
                    em.o("vector", lambda e, k=k: e.memset(
                        yb[:, (k * 2) * SEQ:(k * 2 + 1) * SEQ], 0.0),
                        inc=(sV if k == 1 else None))
                ld_done = qS.n
                rep_done = {}
                yv_hist = {}
                step_hist = {}
                sv_base = sV.n
                for n in range(NST + 1):
                    if n < NST:
                        par = n % 2
                        if n >= 2:
                            em.w("sync", sV, yv_hist[n - 2])
                        em.dma("sync", lambda e, n=n, s=s_seq, par=par: e.dma_start(
                            out=Brep[:, par * SEQ:(par + 1) * SEQ]
                            .rearrange("p (r n) -> p r n", r=8),
                            in_=e1o[:, 512 + n:513 + n, s * 512:(s + 1) * 512]
                            .rearrange("r m n -> m r n").broadcast_to([128, 8, 512])), qS)
                        em.dma("sync", lambda e, n=n, s=s_seq, par=par: e.dma_start(
                            out=Crep[:, par * SEQ:(par + 1) * SEQ]
                            .rearrange("p (r n) -> p r n", r=8),
                            in_=e1o[0:1, 512 + NST + n:513 + NST + n, s * 512:(s + 1) * 512]
                            .broadcast_to([128, 8, 512])), qS)
                        rep_done[n] = qS.n
                    if n == 0:
                        continue
                    nn = n - 1
                    par = nn % 2
                    for k in range(2):
                        sp = (nn * 2 + k) % 2
                        em.w("scalar", qS, ld_done)
                        em.w("scalar", sV, step_hist.get(nn * 2 + k - 2, sv_base))
                        em.o("scalar", lambda e, k=k, nn=nn, sp=sp: e.activation(
                            dAb[:, sp * SEQ:(sp + 1) * SEQ],
                            dtT[:, k * SEQ:(k + 1) * SEQ], AF.Exp,
                            scale=A_sb[:, k * NST + nn:k * NST + nn + 1]), inc=sC)
                        em.w("vector", sC)
                        em.w("vector", qS, rep_done[nn])
                        em.o("vector", lambda e, k=k, par=par: e.tensor_mul(
                            ub[:, :], dtxT[:, k * SEQ:(k + 1) * SEQ],
                            Brep[:, par * SEQ:(par + 1) * SEQ]))
                        em.o("vector", lambda e, sp=sp: e.tensor_tensor_scan(
                            hb[:, :], dAb[:, sp * SEQ:(sp + 1) * SEQ], ub[:, :],
                            zero1[:, 0:1], ALU.mult, ALU.add), inc=sV)
                        em.w("vector", sV)
                        em.o("vector", lambda e, par=par: e.tensor_mul(
                            qscan[:, :], hb[:, :], Crep[:, par * SEQ:(par + 1) * SEQ]))
                        pp, qq = (nn % 2), ((nn + 1) % 2)
                        em.o("vector", lambda e, k=k, pp=pp, qq=qq: e.tensor_add(
                            yb[:, (k * 2 + qq) * SEQ:(k * 2 + qq + 1) * SEQ],
                            yb[:, (k * 2 + pp) * SEQ:(k * 2 + pp + 1) * SEQ], qscan[:, :]),
                            inc=sV)
                        step_hist[nn * 2 + k] = sV.n
                    yv_hist[nn] = sV.n
                for k in range(2):
                    em.o("vector", lambda e, k=k: e.tensor_copy(
                        ybf[:, :], yb[:, (k * 2) * SEQ:(k * 2 + 1) * SEQ]), inc=sV)
                    if "y" in dbg_outs:
                        em.o("vector", lambda e, k=k: e.tensor_copy(
                            dAb[:, 0:SEQ], yb[:, (k * 2) * SEQ:(k * 2 + 1) * SEQ]), inc=sV)
                        em.w("sync", sV)
                        em.dma("sync", lambda e, k=k, s=s_seq: e.dma_start(
                            out=dbg_outs["y"][s * DSL + k * 128: s * DSL + (k + 1) * 128, :],
                            in_=dAb[:, 0:SEQ]), qS)
                    em.w("sync", sV)
                    em.dma("sync", lambda e, k=k, s=s_seq: e.dma_start(
                        out=e2_in[s].ap().rearrange("(j m) n -> j m n", m=256)
                        [:, k * 128:(k + 1) * 128, :].rearrange("j m n -> m j n"),
                        in_=ybf[:, :].rearrange("p (j n) -> p j n", j=8)), qS)
                    em.w("vector", qS)
                em.w("gpsimd", qS)
                em.w("gpsimd", cc)
                with allow_ici_all_to_all():
                    em.o("gpsimd", lambda e, s=s_seq: e.collective_compute(
                        "AllToAll", ALU.bypass, replica_groups=[list(range(NCORE))],
                        ins=[e2_in[s].ap()], outs=[e2_out[s].ap()]), inc=cc)
            em.pop()
            em.barrier(BAR)

    if "C" in phases:
        em.scope("phC")
        cc_e2 = cc.n
        with ExitStack() as pk:
            A = lambda n, s, d: pk.enter_context(nc.sbuf_tensor(n, s, d))
            ypx_g = A("ypx_g", [128, NDB * 512], BF16)
            lxc = A("lxc", [128, DM], F32)
            ypxT_b = A("ypxT_b", [128, NDB * 512], BF16)
            opw_b = A("opw_b", [128, NDB * 512], BF16)
            mo_all = A("mo_all", [128, 4 * DM], F32)
            xc = A("xc", [128, DM], F32)
            sqc = A("sqc", [128, DM], F32)
            monl = A("monl", [128, DM], F32)
            x1c = A("x1c", [128, DM], F32)

            for s_seq in range(NSEQ):
                em.w("sync", cc, cc_e2)
                em.w("sync", sT)
                em.w("sync", sV)
                em.dma("sync", lambda e, s=s_seq: e.dma_start(
                    out=ypx_g[:, :].rearrange("p (b n) -> p b n", b=NDB),
                    in_=e2_out[s].ap().rearrange("(b p) n -> p b n", p=128)), qS)
                for lb in range(4):
                    em.w("sync", sT)
                    em.dma("sync", lambda e, s=s_seq, lb=lb: e.dma_start(
                        out=lxc[:, :],
                        in_=sp_lx.ap()[s * 512 + lb * 128: s * 512 + (lb + 1) * 128, :]), qS)
                    em.w("tensor", qS)
                    em.w("vector", qS)
                    for db in range(NDB):
                        pst = ptA if db % 2 == 0 else ptB
                        mm_guard(pst)
                        em.o("tensor", lambda e, db=db, pst=pst: e.transpose(
                            pst[:, :], lxc[:, db * 128:(db + 1) * 128], ident_f[:, :]),
                            inc=sT)
                        em.w("vector", sT)
                        em.o("vector", lambda e, db=db, lb=lb, pst=pst: e.scalar_tensor_tensor(
                            ypxT_b[:, db * 512 + lb * 128: db * 512 + (lb + 1) * 128],
                            pst[:, :], Dcol_sb[:, db:db + 1],
                            ypx_g[:, db * 512 + lb * 128: db * 512 + (lb + 1) * 128],
                            ALU.mult, ALU.add), inc=sV)
                        consume(pst, sV)
                for dch in range(4):
                    em.w("gpsimd", sT)
                    em.dma("gpsimd", lambda e, dch=dch: e.dma_start(
                        out=opw_b[:, :].rearrange("p (b n) -> p b n", b=NDB),
                        in_=opwT[:, dch * 512:(dch + 1) * 512]
                        .rearrange("(b p) n -> p b n", p=128)), qG)
                    em.w("tensor", qG)
                    em.w("tensor", sV)
                    for lb in range(4):
                        pst = ps[lb]
                        mm_guard(pst)
                        for db in range(NDB):
                            em.o("tensor", lambda e, db=db, lb=lb, pst=pst: e.matmul(
                                pst[:, :],
                                ypxT_b[:, db * 512 + lb * 128: db * 512 + (lb + 1) * 128],
                                opw_b[:, db * 512:(db + 1) * 512],
                                start=(db == 0), stop=(db == NDB - 1)),
                                inc=(sT if db == NDB - 1 else None))
                    em.w("scalar", sT)
                    for lb in range(4):
                        em.o("scalar", lambda e, lb=lb, dch=dch: e.copy(
                            mo_all[:, lb * DM + dch * 512: lb * DM + (dch + 1) * 512],
                            ps[lb][:, :]), inc=(sC if lb == 3 else None))
                    for lb in range(4):
                        consume(ps[lb], sC)
                for lb in range(4):
                    em.w("sync", sV)
                    em.dma("sync", lambda e, s=s_seq, lb=lb: e.dma_start(
                        out=xc[:, :],
                        in_=x_own[s * 512 + lb * 128: s * 512 + (lb + 1) * 128, :]), qS)
                    em.w("vector", qS)
                    em.w("vector", sC)
                    mot = mo_all[:, lb * DM:(lb + 1) * DM]
                    ln_stats(mot, DM, 12, EPS_LN, sqc)
                    em.o("vector", lambda e, mot=mot: e.tensor_scalar(
                        monl[:, :], mot[:, :], st[:, 14:15], st[:, 17:18],
                        ALU.subtract, ALU.mult))
                    em.o("vector", lambda e: e.tensor_add(x1c[:, :], xc[:, :], monl[:, :]),
                         inc=sV)
                    em.w("sync", sV)
                    em.dma("sync", lambda e, s=s_seq, lb=lb: e.dma_start(
                        out=sp_x1.ap()[s * 512 + lb * 128: s * 512 + (lb + 1) * 128, :],
                        in_=x1c[:, :]), qS)
                    if "x1" in dbg_outs:
                        em.dma("sync", lambda e, s=s_seq, lb=lb: e.dma_start(
                            out=dbg_outs["x1"][s * 512 + lb * 128: s * 512 + (lb + 1) * 128, :],
                            in_=x1c[:, :]), qS)
            em.pop()
            em.barrier(BAR)

    if "D" in phases:
        for s_seq in range(NSEQ):
            with ExitStack() as sk:
                with ExitStack() as pk:
                    A = lambda n, s, d: pk.enter_context(nc.sbuf_tensor(n + str(s_seq), s, d))
                    xd = A("xd", [128, DM], F32)
                    sqd = A("sqd", [128, DM], F32)
                    hb_t = A("hb_t", [128, DM], BF16)
                    l1_rep = A("l1_rep", [128, DM], F32)
                    stg = A("stg", [128, NDB * 128], BF16)
                    wq_b = A("wq_b", [128, NDB * 256], BF16)
                    wk_b = A("wk_b", [128, NDB * 256], BF16)
                    wv_b = A("wv_b", [128, NDB * 256], BF16)
                    wo_b = A("wo_b", [128, 2 * DM], BF16)
                    hTb = A("hTb", [128, NDB * 512], BF16)
                    qT_sb = A("qT_sb", [128, 2 * SEQ], BF16)
                    kT_sb = A("kT_sb", [128, 2 * SEQ], BF16)
                    vtmp = A("vtmp", [128, 512], BF16)
                    v_sb = A("v_sb", [128, 32 * 256], BF16)
                    sc_sb = A("sc_sb", [128, SEQ], F32)
                    p2_sb = A("p2_sb", [128, SEQ], BF16)
                    p2b_sb = A("p2b_sb", [128, SEQ], BF16)
                    pT_sb = A("pT_sb", [128, 512], BF16)
                    aoT_sb = A("aoT_sb", [128, 2 * SEQ], BF16)
                    arstg = A("arstg", [128, DM], BF16)
                    ars_sb = A("ars_sb", [128, DM], BF16)
                    mask_sb = A("mask_sb", [128, 4 * 512], BF16)
                    lx1t = A("lx1t", [128, DM], F32)
                    x2it = A("x2it", [128, DM], F32)

                    import ml_dtypes as _mld
                    mrow = np.zeros((4, 128, 512), np.float32)
                    for qo in range(4):
                        for i in range(128):
                            mrow[qo, i, qo * 128 + i + 1:] = -1e9
                    mask_dram = nc.inline_tensor(
                        mrow.transpose(1, 0, 2).reshape(128, 4 * 512)
                        .astype(_mld.bfloat16), f"mask{s_seq}")
                    em.dma("sync", lambda e: e.dma_start(
                        out=mask_sb[:, :], in_=mask_dram.ap()), qS)
                    em.dma("sync", lambda e: e.dma_start(
                        out=l1_rep[:, :], in_=l1row.broadcast_to([128, DM])), qS)
                    em.dma("gpsimd", lambda e: e.dma_start(
                        out=wq_b[:, :].rearrange("p (b n) -> p b n", b=NDB),
                        in_=wq_own.rearrange("(b p) n -> p b n", p=128)), qG)
                    em.dma("gpsimd", lambda e: e.dma_start(
                        out=wk_b[:, :].rearrange("p (b n) -> p b n", b=NDB),
                        in_=wk_own.rearrange("(b p) n -> p b n", p=128)), qG)
                    em.dma("gpsimd", lambda e: e.dma_start(
                        out=wv_b[:, :].rearrange("p (b n) -> p b n", b=NDB),
                        in_=wv_own.rearrange("(b p) n -> p b n", p=128)), qG)
                    em.dma("gpsimd", lambda e: e.dma_start(
                        out=wo_b[:, :].rearrange("p (b n) -> p b n", b=2),
                        in_=wo_own.rearrange("(b p) n -> p b n", p=128)), qG)

                    em.scope("D_h" + str(s_seq))
                    for lb in range(4):
                        em.w("sync", sV)
                        em.w("sync", sC)
                        em.dma("sync", lambda e, s=s_seq, lb=lb: e.dma_start(
                            out=xd[:, :],
                            in_=sp_x1.ap()[s * 512 + lb * 128: s * 512 + (lb + 1) * 128, :]), qS)
                        em.w("vector", qS)
                        em.w("vector", sT)
                        em.w("scalar", qS)
                        ln_stats(xd, DM, 18, EPS_LN, sqd)
                        em.o("vector", lambda e: e.tensor_scalar(
                            lx1t[:, :], xd[:, :], st[:, 20:21], st[:, 23:24],
                            ALU.subtract, ALU.mult), inc=sV)
                        em.w("sync", sV)
                        em.dma("sync", lambda e, s=s_seq, lb=lb: e.dma_start(
                            out=sp_lx1.ap()[s * 512 + lb * 128: s * 512 + (lb + 1) * 128, :],
                            in_=lx1t[:, :]), qS)
                        rms_stats(lx1t, DM, 24, EPS_RMS, sqd)
                        em.o("vector", lambda e: e.tensor_scalar(
                            sqd[:, :], lx1t[:, :], st[:, 25:26], None, ALU.mult))
                        em.o("vector", lambda e: e.tensor_mul(xd[:, :], sqd[:, :], l1_rep[:, :]))
                        em.o("vector", lambda e: e.tensor_copy(hb_t[:, :], xd[:, :]), inc=sV)
                        if "hln" in dbg_outs:
                            em.w("sync", sV)
                            em.dma("sync", lambda e, s=s_seq, lb=lb: e.dma_start(
                                out=dbg_outs["hln"][s * 512 + lb * 128: s * 512 + (lb + 1) * 128, :],
                                in_=xd[:, :]), qS)
                        em.w("tensor", sV)
                        em.w("scalar", qS)
                        if s_seq == 0 and lb == 0 and "hbt" in dbg_outs:
                            em.w("vector", sV)
                            em.o("vector", lambda e: e.tensor_copy(sqd[:, :], hb_t[:, :]),
                                 inc=sV)
                            em.w("sync", sV)
                            em.dma("sync", lambda e: e.dma_start(
                                out=dbg_outs["hbt"], in_=sqd[:, :]), qS)
                            em.w("vector", qS)
                        for db in range(NDB):
                            mm_guard(ptbf)
                            em.o("tensor", lambda e, db=db: e.transpose(
                                ptbf[:, 0:128], hb_t[:, db * 128:(db + 1) * 128], ident_b[:, :]),
                                inc=sT)
                            em.w("scalar", sT)
                            em.o("scalar", lambda e, db=db: e.copy(
                                stg[:, db * 128:(db + 1) * 128], ptbf[:, 0:128]), inc=sC)
                            consume(ptbf, sC)
                        if s_seq == 0 and lb == 0 and "stg" in dbg_outs:
                            em.w("vector", sC)
                            em.o("vector", lambda e: e.tensor_copy(sqd[:, :], stg[:, :]),
                                 inc=sV)
                            em.w("sync", sV)
                            em.dma("sync", lambda e: e.dma_start(
                                out=dbg_outs["stg"], in_=sqd[:, :]), qS)
                            em.w("vector", qS)
                            em.w("scalar", qS)
                        em.w("sync", sC)
                        em.w("sync", qS)
                        em.dma("sync", lambda e, s=s_seq, lb=lb: e.dma_start(
                            out=ht_in[s].ap().rearrange("(b p) n -> p b n", p=128)
                            [:, :, lb * 128:(lb + 1) * 128],
                            in_=stg[:, :].rearrange("p (b n) -> p b n", b=NDB)), qS)
                        em.w("sync", qS)
                    if s_seq == 0 and "hti" in dbg_outs:
                        em.w("sync", qS)
                        em.dma("sync", lambda e: e.dma_start(
                            out=hTb[:, 0:512].rearrange("p n -> p n"),
                            in_=ht_in[0].ap()[0:128, :]), qS)
                        em.w("sync", qS)
                        for j in range(1):
                            em.w("vector", qS)
                            em.o("vector", lambda e: e.tensor_copy(
                                sqd[:, 0:512], hTb[:, 0:512]), inc=sV)
                            em.w("sync", sV)
                            em.dma("sync", lambda e: e.dma_start(
                                out=dbg_outs["hti"], in_=sqd[:, 0:512]), qS)
                            em.w("vector", qS)
                    em.w("gpsimd", qS)
                    em.w("gpsimd", cc)
                    em.o("gpsimd", lambda e, s=s_seq: e.collective_compute(
                        "AllGather", ALU.bypass, replica_groups=[list(range(NCORE))],
                        ins=[ht_in[s].ap()], outs=[ht_out[s].ap()]), inc=cc)
                    cc_ht = cc.n

                    em.pop()
                    em.scope("D_qkv" + str(s_seq))
                    hto = ht_out[s_seq].ap().rearrange("(r b p) n -> r p b n", p=128, b=NDB)
                    for r in range(NCORE):
                        em.w("sync", cc, cc_ht)
                        em.w("sync", sT)
                        em.dma("sync", lambda e, r=r, hto=hto: e.dma_start(
                            out=hTb[:, :].rearrange("p (b n) -> p b n", b=NDB),
                            in_=hto[r]), qS)
                        if r == 0 and s_seq == 0 and "htb" in dbg_outs:
                            for j in range(4):
                                em.w("vector", qS)
                                em.o("vector", lambda e, j=j: e.tensor_copy(
                                    sqd[:, :], hTb[:, j * 2048:(j + 1) * 2048]), inc=sV)
                                em.w("sync", sV)
                                em.dma("sync", lambda e, j=j: e.dma_start(
                                    out=dbg_outs["htb"][:, j * 2048:(j + 1) * 2048],
                                    in_=sqd[:, :]), qS)
                                em.w("vector", qS)
                        em.w("tensor", qS)
                        em.w("tensor", qG)
                        for ti, (wt, dst, scale) in enumerate(
                                ((wq_b, qT_sb, QSCALE), (wk_b, kT_sb, 1.0), (wv_b, None, 1.0))):
                            for h2 in range(2):
                                pst = ps[(ti * 2 + h2) % 4]
                                mm_guard(pst)
                                for db in range(NDB):
                                    em.o("tensor", lambda e, db=db, h2=h2, wt=wt, pst=pst: e.matmul(
                                        pst[:, :],
                                        wt[:, db * 256 + h2 * 128: db * 256 + (h2 + 1) * 128],
                                        hTb[:, db * 512:(db + 1) * 512],
                                        start=(db == 0), stop=(db == NDB - 1)),
                                        inc=(sT if db == NDB - 1 else None))
                                em.w("scalar", sT)
                                if ti < 2:
                                    em.o("scalar", lambda e, h2=h2, r=r, dst=dst, scale=scale, pst=pst:
                                         e.activation(
                                             dst[:, h2 * SEQ + r * 512: h2 * SEQ + (r + 1) * 512],
                                             pst[:, :], AF.Copy,
                                             scale=(kcol[:, 2:3] if scale != 1.0 else 1.0)), inc=sC)
                                    consume(pst, sC)
                                else:
                                    em.w("scalar", sT)
                                    em.w("scalar", sT)
                                    em.o("scalar", lambda e, pst=pst: e.copy(
                                        vtmp[:, :], pst[:, :]), inc=sC)
                                    consume(pst, sC)
                                    em.w("tensor", sC)
                                    for j in range(4):
                                        mm_guard(ptbf)
                                        em.o("tensor", lambda e, j=j: e.transpose(
                                            ptbf[:, 0:128], vtmp[:, j * 128:(j + 1) * 128],
                                            ident_b[:, :]), inc=sT)
                                        em.w("scalar", sT)
                                        em.o("scalar", lambda e, r=r, j=j, h2=h2: e.copy(
                                            v_sb[:, (r * 4 + j) * 256 + h2 * 128:
                                                 (r * 4 + j) * 256 + (h2 + 1) * 128],
                                            ptbf[:, 0:128]), inc=sC)
                                        consume(ptbf, sC)

                    em.pop()
                    em.scope("D_attn" + str(s_seq))
                    em.w("tensor", sC)
                    for h2 in range(2):
                        for qb in range(32):
                            kvlen = (qb + 1) * 128
                            nch = (kvlen + 511) // 512
                            em.w("tensor", sV)
                            for ch in range(nch):
                                cl = min(512, kvlen - ch * 512)
                                pst = ps[ch % 2]
                                mm_guard(pst)
                                em.o("tensor", lambda e, h2=h2, qb=qb, ch=ch, cl=cl, pst=pst:
                                     e.matmul(
                                         pst[:, 0:cl],
                                         qT_sb[:, h2 * SEQ + qb * 128:h2 * SEQ + (qb + 1) * 128],
                                         kT_sb[:, h2 * SEQ + ch * 512: h2 * SEQ + ch * 512 + cl],
                                         start=True, stop=True), inc=sT)
                                em.w("vector", sT)
                                is_diag = (ch == qb // 4)
                                if is_diag:
                                    qo = qb % 4
                                    em.o("vector", lambda e, ch=ch, cl=cl, qo=qo, pst=pst:
                                         e.tensor_add(
                                             sc_sb[:, ch * 512:ch * 512 + cl], pst[:, 0:cl],
                                             mask_sb[:, qo * 512:qo * 512 + cl]), inc=sV)
                                else:
                                    em.o("vector", lambda e, ch=ch, cl=cl, pst=pst:
                                         e.tensor_copy(
                                             sc_sb[:, ch * 512:ch * 512 + cl], pst[:, 0:cl]),
                                         inc=sV)
                                consume(pst, sV)
                            em.o("vector", lambda e, kvlen=kvlen: e.reduce_max(
                                st[:, 44:45], sc_sb[:, 0:kvlen], axis=AX, negate=True),
                                inc=sV)
                            em.w("scalar", sV)
                            em.o("scalar", lambda e, kvlen=kvlen: e.activation(
                                p2_sb[:, 0:kvlen], sc_sb[:, 0:kvlen], AF.Exp,
                                bias=st[:, 44:45], accum_out=st[:, 45:46]), inc=sC)
                            em.w("vector", sC)
                            em.o("vector", lambda e: e.reciprocal(st[:, 46:47], st[:, 45:46]),
                                 inc=sV)
                            em.w("vector", sV)
                            em.o("vector", lambda e, kvlen=kvlen: e.tensor_scalar(
                                p2b_sb[:, 0:kvlen], p2_sb[:, 0:kvlen], st[:, 46:47], None,
                                ALU.mult), inc=sV)
                            em.w("tensor", sV)
                            pao = ps[2]
                            mm_guard(pao)
                            nkb = qb + 1
                            for g0 in range(0, nkb, 4):
                                gn = min(4, nkb - g0)
                                mm_guard(ptbf)
                                for j in range(gn):
                                    em.o("tensor", lambda e, g0=g0, j=j: e.transpose(
                                        ptbf[:, j * 128:(j + 1) * 128],
                                        p2b_sb[:, (g0 + j) * 128:(g0 + j + 1) * 128],
                                        ident_b[:, :]),
                                        inc=(sT if j == gn - 1 else None))
                                em.w("scalar", sT)
                                em.o("scalar", lambda e, gn=gn: e.copy(
                                    pT_sb[:, 0:gn * 128], ptbf[:, 0:gn * 128]), inc=sC)
                                consume(ptbf, sC)
                                em.w("tensor", sC)
                                for j in range(gn):
                                    kb = g0 + j
                                    em.o("tensor", lambda e, kb=kb, j=j, h2=h2, qb=qb, pao=pao:
                                         e.matmul(
                                             pao[:, 0:128],
                                             v_sb[:, kb * 256 + h2 * 128: kb * 256 + (h2 + 1) * 128],
                                             pT_sb[:, j * 128:(j + 1) * 128],
                                             start=(kb == 0), stop=(kb == qb)),
                                         inc=(sT if kb == qb else None))
                            em.w("scalar", sT)
                            em.o("scalar", lambda e, h2=h2, qb=qb, pao=pao: e.copy(
                                aoT_sb[:, h2 * SEQ + qb * 128: h2 * SEQ + (qb + 1) * 128],
                                pao[:, 0:128]), inc=sC)
                            consume(pao, sC)

                    if s_seq == 0:
                        for nm, buf in (("qt", qT_sb), ("kt", kT_sb), ("vv", v_sb),
                                        ("aot", aoT_sb)):
                            if nm in dbg_outs:
                                for j in range(4):
                                    em.w("vector", sC)
                                    em.o("vector", lambda e, buf=buf, j=j: e.tensor_copy(
                                        sqd[:, :], buf[:, j * 2048:(j + 1) * 2048]),
                                        inc=sV)
                                    em.w("sync", sV)
                                    em.dma("sync", lambda e, nm=nm, j=j: e.dma_start(
                                        out=dbg_outs[nm][:, j * 2048:(j + 1) * 2048],
                                        in_=sqd[:, :]), qS)
                                    em.w("vector", qS)
                    em.pop()
                    em.scope("D_wo" + str(s_seq))
                    em.w("tensor", sC)
                    for qb in range(32):
                        for dch in range(4):
                            pst = ps[dch % 2]
                            mm_guard(pst)
                            for h2 in range(2):
                                em.o("tensor", lambda e, qb=qb, dch=dch, h2=h2, pst=pst:
                                     e.matmul(
                                         pst[:, :],
                                         aoT_sb[:, h2 * SEQ + qb * 128: h2 * SEQ + (qb + 1) * 128],
                                         wo_b[:, h2 * DM + dch * 512: h2 * DM + (dch + 1) * 512],
                                         start=(h2 == 0), stop=(h2 == 1)),
                                     inc=(sT if h2 == 1 else None))
                            em.w("scalar", sT)
                            em.w("scalar", qS)
                            em.o("scalar", lambda e, dch=dch, pst=pst: e.copy(
                                arstg[:, dch * 512:(dch + 1) * 512], pst[:, :]),
                                inc=sC)
                            consume(pst, sC)
                        em.w("sync", sC)
                        em.dma("sync", lambda e, s=s_seq, qb=qb: e.dma_start(
                            out=ars_in[s].ap()[qb * 128:(qb + 1) * 128, :],
                            in_=arstg[:, :]), qS)
                    em.w("gpsimd", qS)
                    em.w("gpsimd", cc)
                    em.o("gpsimd", lambda e, s=s_seq: e.collective_compute(
                        "ReduceScatter", ALU.add, replica_groups=[list(range(NCORE))],
                        ins=[ars_in[s].ap()], outs=[ars_out[s].ap()]), inc=cc)
                    cc_ars = cc.n
                    em.pop()
                    em.scope("D_x2i" + str(s_seq))
                    for lb in range(4):
                        em.w("sync", cc, cc_ars)
                        em.w("sync", sV)
                        em.dma("sync", lambda e, s=s_seq, lb=lb: e.dma_start(
                            out=ars_sb[:, :],
                            in_=ars_out[s].ap()[lb * 128:(lb + 1) * 128, :]), qS)
                        em.dma("sync", lambda e, s=s_seq, lb=lb: e.dma_start(
                            out=lx1t[:, :],
                            in_=sp_lx1.ap()[s * 512 + lb * 128: s * 512 + (lb + 1) * 128, :]), qS)
                        em.w("vector", qS)
                        em.o("vector", lambda e, lb=lb: e.tensor_add(
                            x2it[:, :], lx1t[:, :], ars_sb[:, :]), inc=sV)
                        em.w("sync", sV)
                        em.dma("sync", lambda e, s=s_seq, lb=lb: e.dma_start(
                            out=sp_x2i.ap()[s * 512 + lb * 128: s * 512 + (lb + 1) * 128, :],
                            in_=x2it[:, :]), qS)
                        if "x2i" in dbg_outs:
                            em.dma("sync", lambda e, s=s_seq, lb=lb: e.dma_start(
                                out=dbg_outs["x2i"][s * 512 + lb * 128: s * 512 + (lb + 1) * 128, :],
                                in_=x2it[:, :]), qS)
                    em.pop()
                    em.barrier(BAR)
                if "E" in phases:
                    _phase_E(nc, em, BAR, s_seq, sp_x2i, dict(
                        l2row=l2row, gwT=gwT, egT=egT, euT=euT, edT=edT,
                        sgT=sgT, suT=suT, sdT=sdT, kcol=kcol,
                        tt_in=tt_in, tt_out=tt_out, cw_in=cw_in, cw_out=cw_out,
                        mrs_in=mrs_in, mrs_out=mrs_out, sp_x1=sp_x1,
                        out_own=out_own, dbg_outs=dbg_outs,
                        ident_f=ident_f, ident_b=ident_b, st=st,
                        ps=ps, ptA=ptA, ptB=ptB, ptbf=ptbf,
                        qS=qS, qG=qG, cc=cc, sT=sT, sV=sV, sC=sC,
                        mm_guard=mm_guard, consume=consume,
                        ln_stats=ln_stats, rms_stats=rms_stats))
                em.barrier(BAR)

    em.w("sync", qG)
    em.w("sync", cc)
    em.w("sync", qS)

    def _replay(eng, ops):
        stack = []
        for item in ops:
            if callable(item):
                item(eng)
            elif item[0] == "scope":
                cm = nc.named_scope(item[1])
                cm.__enter__()
                stack.append(cm)
            else:
                stack.pop().__exit__(None, None, None)
        while stack:
            stack.pop().__exit__(None, None, None)
    with allow_ici_all_to_all():
        with nc.Block() as blk:
            for name in ENGS:
                getattr(blk, name)(lambda eng, ops=tuple(em.q[name]): _replay(eng, ops))
    stk.close()
    return nc, dbg_outs


def _phase_E(nc, em, BAR, s_seq, sp_x2i, z):
    qS, qG, cc, sT, sV, sC = z["qS"], z["qG"], z["cc"], z["sT"], z["sV"], z["sC"]
    mm_guard, consume = z["mm_guard"], z["consume"]
    ps, ptA, ptB, ptbf = z["ps"], z["ptA"], z["ptB"], z["ptbf"]
    st = z["st"]
    ident_f, ident_b = z["ident_f"], z["ident_b"]
    dbg_outs = z["dbg_outs"]
    ln_stats, rms_stats = z["ln_stats"], z["rms_stats"]

    with ExitStack() as pk:
        A = lambda n, s, d: pk.enter_context(nc.sbuf_tensor(n + "E" + str(s_seq), s, d))
        l2_rep = A("l2_rep", [128, DM], F32)
        x2e = A("x2e", [128, DM], F32)
        te = A("te", [128, DM], F32)
        tb = A("tb", [128, DM], BF16)
        sqe = A("sqe", [128, DM], F32)
        tT_b = A("tT_b", [128, NDB * 512], BF16)
        gw_b = A("gw_b", [128, NDB * NE], BF16)
        glog = A("glog", [128, NE], F32)
        mx8 = A("mx8", [128, 8], F32)
        ind = A("ind", [128, NE], F32)
        wsel = A("wsel", [128, NE], F32)
        cwt = A("cwt", [128, NE], F32)
        cwT_sb = A("cwT_sb", [128, OWN], F32)

        em.dma("sync", lambda e: e.dma_start(
            out=l2_rep[:, :], in_=z["l2row"].broadcast_to([128, DM])), qS)
        em.dma("gpsimd", lambda e: e.dma_start(
            out=gw_b[:, :].rearrange("p (b n) -> p b n", b=NDB),
            in_=z["gwT"].rearrange("(b p) n -> p b n", p=128)), qG)

        for lb in range(4):
            em.w("sync", sV)
            em.w("sync", sC)
            em.dma("sync", lambda e, s=s_seq, lb=lb: e.dma_start(
                out=x2e[:, :],
                in_=sp_x2i.ap()[s * 512 + lb * 128: s * 512 + (lb + 1) * 128, :]), qS)
            em.w("vector", qS)
            em.w("scalar", qS)
            em.w("vector", sT)
            rms_stats(x2e, DM, 28, EPS_RMS, sqe)
            em.o("vector", lambda e: e.tensor_scalar(
                sqe[:, :], x2e[:, :], st[:, 29:30], None, ALU.mult))
            em.o("vector", lambda e: e.tensor_mul(te[:, :], sqe[:, :], l2_rep[:, :]))
            em.o("vector", lambda e: e.tensor_copy(tb[:, :], te[:, :]), inc=sV)
            em.w("tensor", sV)
            for db in range(NDB):
                mm_guard(ptbf)
                em.o("tensor", lambda e, db=db: e.transpose(
                    ptbf[:, 0:128], tb[:, db * 128:(db + 1) * 128], ident_b[:, :]), inc=sT)
                em.w("scalar", sT)
                em.o("scalar", lambda e, db=db, lb=lb: e.copy(
                    tT_b[:, db * 512 + lb * 128: db * 512 + (lb + 1) * 128],
                    ptbf[:, 0:128]), inc=sC)
                consume(ptbf, sC)
            em.w("tensor", sC)
            em.w("tensor", qG)
            mm_guard(ptA)
            for db in range(NDB):
                em.o("tensor", lambda e, db=db, lb=lb: e.matmul(
                    ptA[:, 0:NE],
                    tT_b[:, db * 512 + lb * 128: db * 512 + (lb + 1) * 128],
                    gw_b[:, db * NE:(db + 1) * NE],
                    start=(db == 0), stop=(db == NDB - 1)),
                    inc=(sT if db == NDB - 1 else None))
            em.w("vector", sT)
            em.o("vector", lambda e: e.tensor_copy(glog[:, :], ptA[:, 0:NE]), inc=sV)
            consume(ptA, sV)
            em.w("vector", sV)
            if "gl" in dbg_outs:
                em.w("sync", sV)
                em.dma("sync", lambda e, s=s_seq, lb=lb: e.dma_start(
                    out=dbg_outs["gl"][s * 512 + lb * 128: s * 512 + (lb + 1) * 128, :],
                    in_=glog[:, :]), qS)
            em.o("vector", lambda e: e.max(mx8[:, :], glog[:, :]), inc=sV)
            em.w("vector", sV)
            em.o("vector", lambda e: e.tensor_scalar(
                st[:, 40:41], mx8[:, 0:1], -1.0, None, ALU.mult), inc=sV)
            em.w("vector", sV)
            em.o("vector", lambda e: e.tensor_scalar(
                ind[:, :], glog[:, :], mx8[:, 5:6], None, ALU.is_ge), inc=sV)
            em.w("scalar", sV)
            em.o("scalar", lambda e: e.activation(
                wsel[:, :], glog[:, :], AF.Exp, bias=st[:, 40:41]), inc=sC)
            em.w("vector", sC)
            em.o("vector", lambda e: e.tensor_mul(glog[:, :], wsel[:, :], ind[:, :]),
                 inc=sV)
            em.w("vector", sV)
            em.o("vector", lambda e: e.reduce_sum(st[:, 41:42], glog[:, :], axis=AX),
                 inc=sV)
            em.w("vector", sV)
            em.o("vector", lambda e: e.reciprocal(st[:, 42:43], st[:, 41:42]), inc=sV)
            em.w("vector", sV)
            em.o("vector", lambda e: e.tensor_scalar(
                cwt[:, :], glog[:, :], st[:, 42:43], None, ALU.mult), inc=sV)
            if "cw" in dbg_outs:
                em.w("sync", sV)
                em.dma("sync", lambda e, s=s_seq, lb=lb: e.dma_start(
                    out=dbg_outs["cw"][s * 512 + lb * 128: s * 512 + (lb + 1) * 128, :],
                    in_=cwt[:, :]), qS)
            em.w("tensor", sV)
            mm_guard(ptB)
            em.o("tensor", lambda e: e.transpose(
                ptB[0:NE, :], cwt[:, :], ident_f[:, :]), inc=sT)
            em.w("vector", sT)
            em.o("vector", lambda e, lb=lb: e.tensor_copy(
                cwT_sb[0:NE, lb * 128:(lb + 1) * 128], ptB[0:NE, 0:128]), inc=sV)
            consume(ptB, sV)
        em.w("sync", sC)
        em.w("sync", sV)
        em.dma("sync", lambda e, s=s_seq: e.dma_start(
            out=z["tt_in"][s].ap().rearrange("(b p) n -> p b n", p=128),
            in_=tT_b[:, :].rearrange("p (b n) -> p b n", b=NDB)), qS)
        em.dma("sync", lambda e, s=s_seq: e.dma_start(
            out=z["cw_in"][s].ap(), in_=cwT_sb[0:NE, :]), qS)
        em.w("gpsimd", qS)
        em.w("gpsimd", cc)
        em.o("gpsimd", lambda e, s=s_seq: e.collective_compute(
            "AllGather", ALU.bypass, replica_groups=[list(range(NCORE))],
            ins=[z["tt_in"][s].ap()], outs=[z["tt_out"][s].ap()]), inc=cc)
        with allow_ici_all_to_all():
            em.w("gpsimd", cc)
            em.o("gpsimd", lambda e, s=s_seq: e.collective_compute(
                "AllToAll", ALU.bypass, replica_groups=[list(range(NCORE))],
                ins=[z["cw_in"][s].ap()], outs=[z["cw_out"][s].ap()]), inc=cc)
        cc_moe = cc.n

        with ExitStack() as rk:
            B = lambda n, s, d: rk.enter_context(nc.sbuf_tensor(n + "R" + str(s_seq), s, d))
            tT_all = B("tT_all", [128, NDB * 1024], BF16)
            hh_w = B("hh_w", [128, 2 * NIB * 1024], BF16)
            eg_b = B("eg_b", [128, 2 * NDB * 128], BF16)
            eu_b = B("eu_b", [128, 2 * NDB * 128], BF16)
            ed_b = B("ed_b", [128, 4 * EI], BF16)
            cwrep = B("cwrep", [128, 1024], F32)
            sil_b = B("sil_b", [128, 1024], BF16)
            uw_b = B("uw_b", [128, 1024], BF16)
            mstg = B("mstg", [128, 2048], BF16)

            tto = z["tt_out"][s_seq].ap().rearrange("(r b p) n -> p b r n", p=128, b=NDB)
            cwo = z["cw_out"][s_seq].ap().rearrange("(r e) n -> e r n", e=2)
            for qt in range(4):
                em.w("sync", cc, cc_moe)
                em.w("sync", sT)
                for rr in range(2):
                    em.dma("sync", lambda e, qt=qt, rr=rr: e.dma_start(
                        out=tT_all[:, :].rearrange("p (b m) -> p b m", b=NDB)
                        [:, :, rr * 512:(rr + 1) * 512],
                        in_=tto[:, :, 2 * qt + rr, :]), qS)
                for e2 in range(2):
                    em.w("sync", sV)
                    em.dma("sync", lambda e, e2=e2, qt=qt: e.dma_start(
                        out=cwrep[:, :].rearrange("p (r n) -> p r n", r=2),
                        in_=cwo[e2:e2 + 1, 2 * qt:2 * qt + 2, :]
                        .broadcast_to([128, 2, 512])), qS)
                    cw_ld = qS.n
                    ld_hist = {}
                    mm_hist = {}
                    st_base = sT.n
                    for ib in range(NIB + 1):
                        if ib < NIB:
                            par = ib % 2
                            em.w("gpsimd", sT, mm_hist.get(ib - 2, st_base))
                            em.dma("gpsimd", lambda e, e2=e2, ib=ib, par=par: e.dma_start(
                                out=eg_b[:, par * 2048:(par + 1) * 2048]
                                .rearrange("p (b n) -> p b n", b=NDB),
                                in_=z["egT"][e2:e2 + 1, :, ib * 128:(ib + 1) * 128]
                                .rearrange("o (b p) n -> p b (o n)", p=128)), qG)
                            em.dma("gpsimd", lambda e, e2=e2, ib=ib, par=par: e.dma_start(
                                out=eu_b[:, par * 2048:(par + 1) * 2048]
                                .rearrange("p (b n) -> p b n", b=NDB),
                                in_=z["euT"][e2:e2 + 1, :, ib * 128:(ib + 1) * 128]
                                .rearrange("o (b p) n -> p b (o n)", p=128)), qG)
                            ld_hist[ib] = qG.n
                        if ib == 0:
                            continue
                        jb = ib - 1
                        par = jb % 2
                        em.w("tensor", qG, ld_hist[jb])
                        em.w("tensor", qS)
                        for lch in range(2):
                            pg, pu = ps[lch * 2], ps[lch * 2 + 1]
                            mm_guard(pg)
                            mm_guard(pu)
                            for db in range(NDB):
                                em.o("tensor", lambda e, db=db, lch=lch, pg=pg, par=par: e.matmul(
                                    pg[:, :],
                                    eg_b[:, par * 2048 + db * 128: par * 2048 + (db + 1) * 128],
                                    tT_all[:, db * 1024 + lch * 512: db * 1024 + (lch + 1) * 512],
                                    start=(db == 0), stop=(db == NDB - 1)),
                                    inc=(sT if db == NDB - 1 else None))
                            for db in range(NDB):
                                em.o("tensor", lambda e, db=db, lch=lch, pu=pu, par=par: e.matmul(
                                    pu[:, :],
                                    eu_b[:, par * 2048 + db * 128: par * 2048 + (db + 1) * 128],
                                    tT_all[:, db * 1024 + lch * 512: db * 1024 + (lch + 1) * 512],
                                    start=(db == 0), stop=(db == NDB - 1)),
                                    inc=(sT if db == NDB - 1 else None))
                        mm_hist[jb] = sT.n
                        for lch in range(2):
                            pg, pu = ps[lch * 2], ps[lch * 2 + 1]
                            em.w("scalar", sT, mm_hist[jb] - 2 * (1 - lch))
                            em.w("scalar", sV)
                            em.o("scalar", lambda e, pg=pg, lch=lch: e.activation(
                                sil_b[:, lch * 512:(lch + 1) * 512], pg[:, :], AF.Silu),
                                inc=sC)
                            consume(pg, sC)
                            em.w("vector", sT, mm_hist[jb] - 2 * (1 - lch))
                            em.w("vector", sC)
                            em.w("vector", qS, cw_ld)
                            em.o("vector", lambda e, lch=lch, pu=pu: e.scalar_tensor_tensor(
                                uw_b[:, lch * 512:(lch + 1) * 512], pu[:, :], 1.0,
                                cwrep[:, lch * 512:(lch + 1) * 512],
                                ALU.mult, ALU.mult), inc=sV)
                            consume(pu, sV)
                            em.o("vector", lambda e, e2=e2, jb=jb, lch=lch: e.tensor_mul(
                                hh_w[:, (e2 * NIB + jb) * 1024 + lch * 512:
                                     (e2 * NIB + jb) * 1024 + (lch + 1) * 512],
                                uw_b[:, lch * 512:(lch + 1) * 512],
                                sil_b[:, lch * 512:(lch + 1) * 512]), inc=sV)
                dld_hist = {}
                dmm_hist = {}
                dma_hist = {}
                dst_base = sT.n
                dq_base = qS.n
                for dob in range(NDB + 1):
                    if dob < NDB:
                        dpar = dob % 2
                        em.w("gpsimd", sT, dmm_hist.get(dob - 2, dst_base))
                        for ee in range(2):
                            em.dma("gpsimd", lambda e, dob=dob, ee=ee, dpar=dpar: e.dma_start(
                                out=ed_b[:, (dpar * 2 + ee) * EI:(dpar * 2 + ee + 1) * EI]
                                .rearrange("p (b n) -> p b n", b=NIB),
                                in_=z["edT"][ee:ee + 1, :, dob * 128:(dob + 1) * 128]
                                .rearrange("o (b p) n -> p b (o n)", p=128)), qG)
                        dld_hist[dob] = qG.n
                    if dob == 0:
                        continue
                    job = dob - 1
                    dpar = job % 2
                    em.w("tensor", qG, dld_hist[job])
                    em.w("tensor", sV)
                    for lch in range(2):
                        pst = ps[2 + lch]
                        mm_guard(pst)
                        for e2 in range(2):
                            for ib in range(NIB):
                                em.o("tensor", lambda e, e2=e2, ib=ib, lch=lch, pst=pst, dpar=dpar:
                                     e.matmul(
                                         pst[:, :],
                                         ed_b[:, (dpar * 2 + e2) * EI + ib * 128:
                                              (dpar * 2 + e2) * EI + (ib + 1) * 128],
                                         hh_w[:, (e2 * NIB + ib) * 1024 + lch * 512:
                                              (e2 * NIB + ib) * 1024 + (lch + 1) * 512],
                                         start=(e2 == 0 and ib == 0),
                                         stop=(e2 == 1 and ib == NIB - 1)),
                                     inc=(sT if (e2 == 1 and ib == NIB - 1) else None))
                    dmm_hist[job] = sT.n
                    for lch in range(2):
                        pst = ps[2 + lch]
                        em.w("vector", sT, dmm_hist[job] - (1 - lch))
                        em.w("vector", qS, dma_hist.get(job - 2, dq_base))
                        em.o("vector", lambda e, lch=lch, pst=pst, dpar=dpar: e.tensor_copy(
                            mstg[:, dpar * 1024 + lch * 512: dpar * 1024 + (lch + 1) * 512],
                            pst[:, :]), inc=sV)
                        consume(pst, sV)
                    em.w("sync", sV)
                    em.dma("sync", lambda e, s=s_seq, job=job, qt=qt, dpar=dpar: e.dma_start(
                        out=z["mrs_in"][s].ap().rearrange("(j m) n -> j m n", m=DM)
                        [2 * qt:2 * qt + 2, job * 128:(job + 1) * 128, :]
                        .rearrange("j m n -> m j n"),
                        in_=mstg[:, dpar * 1024:(dpar + 1) * 1024]
                        .rearrange("p (j n) -> p j n", j=2)), qS)
                    dma_hist[job] = qS.n
            em.w("gpsimd", qS)
            em.w("gpsimd", cc)
            em.o("gpsimd", lambda e, s=s_seq: e.collective_compute(
                "ReduceScatter", ALU.add, replica_groups=[list(range(NCORE))],
                ins=[z["mrs_in"][s].ap()], outs=[z["mrs_out"][s].ap()]), inc=cc)
        cc_mrs = cc.n

        with ExitStack() as hk:
            C = lambda n, s, d: hk.enter_context(nc.sbuf_tensor(n + "S" + str(s_seq), s, d))
            sg_b = C("sg_b", [128, NDB * 128], BF16)
            su_b = C("su_b", [128, NDB * 128], BF16)
            sd_b = C("sd_b", [128, DM], BF16)
            hhs = C("hhs", [128, NSB * 512], BF16)
            ssil = C("ssil", [128, 512], BF16)
            comb = C("comb", [128, 4 * DM], F32)
            x2ic = C("x2ic", [128, 512], F32)
            rt_sb = C("rt_sb", [128, DM], BF16)
            mrs_sb = C("mrs_sb", [128, DM], BF16)
            ffn_t = C("ffn_t", [128, DM], F32)
            monl2 = C("monl2", [128, DM], F32)
            x1e = C("x1e", [128, DM], F32)
            sqe2 = C("sqe2", [128, DM], F32)

            for sib in range(NSB):
                em.w("gpsimd", sT)
                em.dma("gpsimd", lambda e, sib=sib: e.dma_start(
                    out=sg_b[:, :].rearrange("p (b n) -> p b n", b=NDB),
                    in_=z["sgT"][:, sib * 128:(sib + 1) * 128]
                    .rearrange("(b p) n -> p b n", p=128)), qG)
                em.dma("gpsimd", lambda e, sib=sib: e.dma_start(
                    out=su_b[:, :].rearrange("p (b n) -> p b n", b=NDB),
                    in_=z["suT"][:, sib * 128:(sib + 1) * 128]
                    .rearrange("(b p) n -> p b n", p=128)), qG)
                em.w("tensor", qG)
                pg, pu = ps[0], ps[1]
                mm_guard(pg)
                for db in range(NDB):
                    em.o("tensor", lambda e, db=db, pg=pg: e.matmul(
                        pg[:, :], sg_b[:, db * 128:(db + 1) * 128],
                        tT_b[:, db * 512:(db + 1) * 512],
                        start=(db == 0), stop=(db == NDB - 1)),
                        inc=(sT if db == NDB - 1 else None))
                mm_guard(pu)
                for db in range(NDB):
                    em.o("tensor", lambda e, db=db, pu=pu: e.matmul(
                        pu[:, :], su_b[:, db * 128:(db + 1) * 128],
                        tT_b[:, db * 512:(db + 1) * 512],
                        start=(db == 0), stop=(db == NDB - 1)),
                        inc=(sT if db == NDB - 1 else None))
                em.w("scalar", sT)
                em.w("scalar", sV)
                em.o("scalar", lambda e, pg=pg: e.activation(
                    ssil[:, :], pg[:, :], AF.Silu), inc=sC)
                consume(pg, sC)
                em.w("vector", sT)
                em.w("vector", sC)
                em.o("vector", lambda e, sib=sib, pu=pu: e.scalar_tensor_tensor(
                    hhs[:, sib * 512:(sib + 1) * 512], pu[:, :], 1.0,
                    ssil[:, :], ALU.mult, ALU.mult), inc=sV)
                consume(pu, sV)
            if "hhs" in dbg_outs:
                for j in range(4):
                    em.w("vector", sV)
                    em.o("vector", lambda e, j=j: e.tensor_copy(
                        sqe2[:, 0:512], hhs[:, j * 512:(j + 1) * 512]), inc=sV)
                    em.w("sync", sV)
                    em.dma("sync", lambda e, s=s_seq, j=j: e.dma_start(
                        out=dbg_outs["hhs"][s * 128:(s + 1) * 128, j * 512:(j + 1) * 512],
                        in_=sqe2[:, 0:512]), qS)
                    em.w("vector", qS)
            for dq in range(4):
                for sib in range(NSB):
                    em.w("gpsimd", sT)
                    em.dma("gpsimd", lambda e, sib=sib, dq=dq: e.dma_start(
                        out=sd_b[:, 0:512],
                        in_=z["sdT"][sib * 128:(sib + 1) * 128, dq * 512:(dq + 1) * 512]), qG)
                    em.w("tensor", qG)
                    em.w("tensor", sV)
                    for lb in range(4):
                        pst = ps[lb]
                        if sib == 0:
                            mm_guard(pst)
                        em.o("tensor", lambda e, sib=sib, lb=lb, pst=pst: e.matmul(
                            pst[:, :], hhs[:, sib * 512 + lb * 128: sib * 512 + (lb + 1) * 128],
                            sd_b[:, 0:512],
                            start=(sib == 0), stop=(sib == NSB - 1)),
                            inc=(sT if lb == 3 else None))
                em.w("vector", sT)
                for lb in range(4):
                    em.w("sync", sV)
                    em.dma("sync", lambda e, s=s_seq, lb=lb, dq=dq: e.dma_start(
                        out=x2ic[:, :],
                        in_=sp_x2i.ap()[s * 512 + lb * 128: s * 512 + (lb + 1) * 128,
                                        dq * 512:(dq + 1) * 512]), qS)
                    em.w("vector", qS)
                    em.o("vector", lambda e, lb=lb, dq=dq: e.tensor_add(
                        comb[:, lb * DM + dq * 512: lb * DM + (dq + 1) * 512],
                        x2ic[:, :], ps[lb][:, :]), inc=sV)
                    consume(ps[lb], sV)
            em.w("sync", cc, cc_mrs)
            for lb in range(4):
                em.w("sync", sV)
                em.dma("sync", lambda e, s=s_seq, lb=lb: e.dma_start(
                    out=mrs_sb[:, :].rearrange("p (b n) -> p b n", b=NDB),
                    in_=z["mrs_out"][s].ap().rearrange("(b p) n -> p b n", p=128)
                    [:, :, lb * 128:(lb + 1) * 128]), qS)
                em.w("tensor", qS)
                em.w("tensor", sC)
                for db in range(NDB):
                    mm_guard(ptbf)
                    em.o("tensor", lambda e, db=db: e.transpose(
                        ptbf[:, 0:128], mrs_sb[:, db * 128:(db + 1) * 128], ident_b[:, :]),
                        inc=sT)
                    em.w("scalar", sT)
                    em.o("scalar", lambda e, db=db: e.copy(
                        rt_sb[:, db * 128:(db + 1) * 128], ptbf[:, 0:128]), inc=sC)
                    consume(ptbf, sC)
                if "shr" in dbg_outs:
                    em.w("sync", sV)
                    em.dma("sync", lambda e, s=s_seq, lb=lb: e.dma_start(
                        out=dbg_outs["shr"][s * 512 + lb * 128: s * 512 + (lb + 1) * 128, :],
                        in_=comb[:, lb * DM:(lb + 1) * DM]), qS)
                em.w("vector", sC)
                em.o("vector", lambda e, lb=lb: e.tensor_add(
                    ffn_t[:, :], comb[:, lb * DM:(lb + 1) * DM], rt_sb[:, :]), inc=sV)
                if "routed" in dbg_outs:
                    em.w("sync", sV)
                    em.dma("sync", lambda e, s=s_seq, lb=lb: e.dma_start(
                        out=dbg_outs["routed"][s * 512 + lb * 128: s * 512 + (lb + 1) * 128, :],
                        in_=ffn_t[:, :]), qS)
                ln_stats(ffn_t, DM, 34, EPS_LN, sqe2)
                em.o("vector", lambda e: e.tensor_scalar(
                    monl2[:, :], ffn_t[:, :], st[:, 36:37], st[:, 39:40],
                    ALU.subtract, ALU.mult), inc=sV)
                em.w("sync", sV)
                em.dma("sync", lambda e, s=s_seq, lb=lb: e.dma_start(
                    out=x1e[:, :],
                    in_=z["sp_x1"].ap()[s * 512 + lb * 128: s * 512 + (lb + 1) * 128, :]), qS)
                em.w("vector", qS)
                em.o("vector", lambda e: e.tensor_add(sqe2[:, :], monl2[:, :], x1e[:, :]),
                     inc=sV)
                em.w("sync", sV)
                em.dma("sync", lambda e, s=s_seq, lb=lb: e.dma_start(
                    out=z["out_own"][s * 512 + lb * 128: s * 512 + (lb + 1) * 128, :],
                    in_=sqe2[:, :]), qS)


def make_core_inputs(inputs, c):
    T = lambda a: np.ascontiguousarray(a, dtype=np.float32)
    x = inputs["L0_attn_x"]
    d = {}
    d["x_own"] = T(np.concatenate(
        [x[0, 512 * c:512 * (c + 1)], x[1, 512 * c:512 * (c + 1)]], axis=0))
    d["A_slc"] = T(inputs["L0_attn_A_log"][256 * c:256 * (c + 1)])
    d["Dcol"] = T(np.asarray(inputs["L0_attn_D_param"]).reshape(NDB, 128).T)
    d["nwrow"] = T(np.asarray(inputs["L0_attn_norm_w"])[None, :])
    d["nbrow"] = T(np.asarray(inputs["L0_attn_norm_b"])[None, :])
    d["dtbrow"] = T(np.asarray(inputs["L0_attn_dt_proj_b"])[None, :])
    d["l1row"] = T(np.asarray(inputs["L0_ffn_ln1_w"])[None, :])
    d["l2row"] = T(np.asarray(inputs["L0_ffn_ln2_w"])[None, :])
    d["xpwT"] = T(np.asarray(inputs["L0_attn_x_proj_w"]).T)
    d["dtwT"] = T(np.asarray(inputs["L0_attn_dt_proj_w"]).T)
    d["opwT"] = T(np.asarray(inputs["L0_attn_out_proj_w"]).T)
    d["wq_own"] = T(np.asarray(inputs["L0_ffn_wq"])[:, 256 * c:256 * (c + 1)])
    d["wk_own"] = T(np.asarray(inputs["L0_ffn_wk"])[:, 256 * c:256 * (c + 1)])
    d["wv_own"] = T(np.asarray(inputs["L0_ffn_wv"])[:, 256 * c:256 * (c + 1)])
    d["wo_own"] = T(np.asarray(inputs["L0_ffn_wo"])[256 * c:256 * (c + 1), :])
    d["gwT"] = T(np.asarray(inputs["L0_ffn_gate_w"]).T)
    eg = np.asarray(inputs["L0_ffn_eg_w"])
    eu = np.asarray(inputs["L0_ffn_eu_w"])
    ed = np.asarray(inputs["L0_ffn_ed_w"])
    d["egT"] = T(np.stack([eg[2 * c].T, eg[2 * c + 1].T]))
    d["euT"] = T(np.stack([eu[2 * c].T, eu[2 * c + 1].T]))
    d["edT"] = T(np.stack([ed[2 * c].T, ed[2 * c + 1].T]))
    d["sgT"] = T(np.asarray(inputs["L0_ffn_sg_w"]).T)
    d["suT"] = T(np.asarray(inputs["L0_ffn_su_w"]).T)
    d["sdT"] = T(np.asarray(inputs["L0_ffn_sd_w"]).T)
    return d


def kernel(**inputs):
    nc, _ = build()
    in_maps = [make_core_inputs(inputs, c) for c in range(NCORE)]
    res = run_bass_kernel_spmd(nc, in_maps, core_ids=list(range(NCORE)))
    out = np.zeros((NSEQ, SEQ, DM), np.float32)
    for c in range(NCORE):
        o = res.results[c]["out_own"]
        out[0, 512 * c:512 * (c + 1)] = o[0:OWN]
        out[1, 512 * c:512 * (c + 1)] = o[OWN:OWNT]
    return out


def run_debug(inputs, phases, dbg):
    nc, dbg_outs = build(phases=phases, dbg=dbg)
    in_maps = [make_core_inputs(inputs, c) for c in range(NCORE)]
    res = run_bass_kernel_spmd(nc, in_maps, core_ids=list(range(NCORE)))
    return res


# version 06
# speedup vs baseline: 1.5298x; 1.0311x over previous
import math
import numpy as np
import concourse.bass as bass
import concourse.mybir as mybir
from concourse.bass_utils import run_bass_kernel_spmd
from concourse.aot_env import allow_ici_all_to_all
from contextlib import ExitStack

F32 = mybir.dt.float32
BF16 = mybir.dt.bfloat16
FP16 = mybir.dt.float16
AF = mybir.ActivationFunctionType
ALU = mybir.AluOpType
AX = mybir.AxisListType.X

NCORE = 8
SEQ = 4096
NSEQ = 2
DM = 2048
OWN = 512
OWNT = NSEQ * OWN
DSL = 256
NST = 64
RK = 128
HD = 128
NE = 16
EI = 1408
SI = 2816
EPS_LN = 1e-5
EPS_RMS = 1e-6
NDB = DM // 128
NIB = EI // 128
NSB = SI // 128
QSCALE = 1.0 / math.sqrt(HD)

ENGS = ("sync", "gpsimd", "tensor", "vector", "scalar")


class S:
    def __init__(self, nc, name):
        self.sem = nc.alloc_semaphore(name)
        self.n = 0
        self.name = name


class Em:
    def __init__(self):
        self.q = {k: [] for k in ENGS}

    def scope(self, name):
        for e in ENGS:
            self.q[e].append(("scope", name))

    def pop(self):
        for e in ENGS:
            self.q[e].append(("pop", None))

    def w(self, eng, sem, val=None):
        v = sem.n if val is None else val
        if v > 0:
            self.q[eng].append(lambda e, s=sem.sem, v=v: e.wait_ge(s, v))

    def o(self, eng, f, inc=None, inck=1):
        if inc is None:
            self.q[eng].append(f)
        else:
            inc.n += inck
            self.q[eng].append(lambda e, f=f, s=inc.sem, k=inck: f(e).then_inc(s, k))

    def dma(self, eng, f, sem):
        self.o(eng, f, inc=sem, inck=16)

    def barrier(self, bars, engs=ENGS):
        marks = {}
        for e in engs:
            self.o(e, lambda x: x.drain(), inc=bars[e])
            marks[e] = bars[e].n
        for e in engs:
            for e2 in engs:
                if e2 != e:
                    self.w(e, bars[e2], marks[e2])


def build(phases="ABCDE", dbg=()):
    nc = bass.Bass("TRN2", target_bir_lowering=False)

    def inp(name, shape, dtype=F32):
        return nc.dram_tensor(name, shape, dtype, kind="ExternalInput").ap()

    x_own = inp("x_own", [OWNT, DM])
    A_slc = inp("A_slc", [DSL, NST])
    Dcol = inp("Dcol", [128, NDB])
    nwrow = inp("nwrow", [1, DM])
    nbrow = inp("nbrow", [1, DM])
    dtbrow = inp("dtbrow", [1, DM])
    l1row = inp("l1row", [1, DM])
    l2row = inp("l2row", [1, DM])
    xpwT = inp("xpwT", [DM, 2 * NST + RK])
    dtwT = inp("dtwT", [RK, DM])
    opwT = inp("opwT", [DM, DM])
    wq_own = inp("wq_own", [DM, 2 * HD])
    wk_own = inp("wk_own", [DM, 2 * HD])
    wv_own = inp("wv_own", [DM, 2 * HD])
    wo_own = inp("wo_own", [2 * HD, DM])
    gwT = inp("gwT", [DM, NE])
    egT = inp("egT", [2, DM, EI])
    euT = inp("euT", [2, DM, EI])
    edT = inp("edT", [2, EI, DM])
    sgT = inp("sgT", [DM, SI])
    suT = inp("suT", [DM, SI])
    sdT = inp("sdT", [SI, DM])

    out_own = nc.dram_tensor("out_own", [OWNT, DM], F32, kind="ExternalOutput").ap()
    dbg_outs = {}
    for nm, shp in (("lx", [OWNT, DM]), ("xn", [OWNT, DM]), ("dt", [OWNT, DM]),
                    ("y", [NSEQ * DSL, SEQ]), ("x1", [OWNT, DM]),
                    ("hln", [OWNT, DM]), ("attn", [OWNT, DM]),
                    ("cw", [OWNT, NE]), ("x2i", [OWNT, DM]),
                    ("qt", [128, 2 * SEQ]), ("kt", [128, 2 * SEQ]),
                    ("htb", [128, NDB * 512]), ("hti", [128, 512]), ("hbt", [128, DM]), ("stg", [128, DM]), ("gl", [OWNT, NE]), ("shr", [OWNT, DM]), ("hhs", [NSEQ * 128, 2048]),
                    ("vv", [128, 32 * 256]), ("aot", [128, 2 * SEQ]),
                    ("routed", [OWNT, DM]), ("shared", [OWNT, DM])):
        if nm in dbg:
            dbg_outs[nm] = nc.dram_tensor("dbg_" + nm, shp, F32,
                                          kind="ExternalOutput").ap()

    e1_in = nc.dram_tensor("e1_in", [NCORE * 640, 1024], FP16)
    e1_out = nc.dram_tensor("e1_out", [NCORE * 640, 1024], FP16)
    e2_in = [nc.dram_tensor(f"e2_in{s}", [DM, OWN], BF16) for s in range(NSEQ)]
    e2_out = [nc.dram_tensor(f"e2_out{s}", [DM, OWN], BF16) for s in range(NSEQ)]
    ht_in = [nc.dram_tensor(f"ht_in{s}", [DM, OWN], BF16) for s in range(NSEQ)]
    ht_out = [nc.dram_tensor(f"ht_out{s}", [NCORE * DM, OWN], BF16) for s in range(NSEQ)]
    ars_in = [nc.dram_tensor(f"ars_in{s}", [SEQ, DM], BF16) for s in range(NSEQ)]
    ars_out = [nc.dram_tensor(f"ars_out{s}", [OWN, DM], BF16) for s in range(NSEQ)]
    tt_in = [nc.dram_tensor(f"tt_in{s}", [DM, OWN], BF16) for s in range(NSEQ)]
    tt_out = [nc.dram_tensor(f"tt_out{s}", [NCORE * DM, OWN], BF16) for s in range(NSEQ)]
    cw_in = [nc.dram_tensor(f"cw_in{s}", [NE, OWN], F32) for s in range(NSEQ)]
    cw_out = [nc.dram_tensor(f"cw_out{s}", [NE, OWN], F32) for s in range(NSEQ)]
    mrs_in = [nc.dram_tensor(f"mrs_in{s}", [NCORE * DM, OWN], BF16) for s in range(NSEQ)]
    mrs_out = [nc.dram_tensor(f"mrs_out{s}", [DM, OWN], BF16) for s in range(NSEQ)]
    sp_lx = nc.dram_tensor("sp_lx", [OWNT, DM], F32)
    sp_x1 = nc.dram_tensor("sp_x1", [OWNT, DM], F32)
    sp_lx1 = nc.dram_tensor("sp_lx1", [OWNT, DM], F32)
    sp_x2i = nc.dram_tensor("sp_x2i", [OWNT, DM], F32)

    ident_dram = nc.inline_tensor(np.eye(128, dtype=np.float32), "identf")

    qS = S(nc, "qS")
    qG = S(nc, "qG")
    cc = S(nc, "cc")
    sT = S(nc, "sT")
    sV = S(nc, "sV")
    sC = S(nc, "sC")
    sSy = S(nc, "sSy")
    sGp = S(nc, "sGp")
    BAR = {"sync": sSy, "gpsimd": sGp, "tensor": sT, "vector": sV, "scalar": sC}

    em = Em()

    stk = ExitStack()
    ident_f = stk.enter_context(nc.sbuf_tensor("ident_f", [128, 128], F32))
    ident_b = stk.enter_context(nc.sbuf_tensor("ident_b", [128, 128], BF16))
    ident_h = stk.enter_context(nc.sbuf_tensor("ident_h", [128, 128], FP16))
    Dcol_sb = stk.enter_context(nc.sbuf_tensor("Dcol_sb", [128, NDB], F32))
    A_sb = stk.enter_context(nc.sbuf_tensor("A_sb", [128, 2 * NST], F32))
    st = stk.enter_context(nc.sbuf_tensor("stats", [128, 48], F32))
    kcol = stk.enter_context(nc.sbuf_tensor("kcol", [128, 4], F32))
    ps = [stk.enter_context(nc.psum_tensor(f"ps{i}", [128, 512], F32)) for i in range(4)]
    ptA = stk.enter_context(nc.psum_tensor("ptA", [128, 128], F32))
    ptB = stk.enter_context(nc.psum_tensor("ptB", [128, 128], F32))
    pt16 = stk.enter_context(nc.psum_tensor("pt16", [128, 128], FP16))
    ptbf = stk.enter_context(nc.psum_tensor("ptbf", [128, 1024], BF16))

    ps_guard = {}

    def mm_guard(pst):
        g = ps_guard.get(id(pst))
        if g:
            em.w("tensor", g[0], g[1])

    def consume(pst, sem):
        ps_guard[id(pst)] = (sem, sem.n)

    em.dma("sync", lambda e: e.dma_start(out=ident_f[:, :], in_=ident_dram.ap()), qS)
    em.dma("sync", lambda e: e.dma_start(out=Dcol_sb[:, :], in_=Dcol), qS)
    em.dma("sync", lambda e: e.dma_start(
        out=A_sb[:, :].rearrange("p (b n) -> p b n", b=2),
        in_=A_slc.rearrange("(b p) n -> p b n", p=128)), qS)
    em.w("vector", qS)
    em.o("vector", lambda e: e.tensor_copy(ident_b[:, :], ident_f[:, :]))
    em.o("vector", lambda e: e.memset(kcol[:, 0:1], EPS_LN))
    em.o("vector", lambda e: e.memset(kcol[:, 1:2], EPS_RMS))
    em.o("vector", lambda e: e.memset(kcol[:, 2:3], QSCALE))
    em.o("vector", lambda e: e.tensor_copy(ident_h[:, :], ident_f[:, :]), inc=sV)
    em.w("scalar", sV)
    em.w("scalar", qS)
    em.o("scalar", lambda e: e.activation(A_sb[:, :], A_sb[:, :], AF.Exp))
    em.o("scalar", lambda e: e.activation(A_sb[:, :], A_sb[:, :], AF.Copy, scale=-1.0),
         inc=sC)

    def tv(f):
        em.w("vector", sV)
        em.o("vector", f, inc=sV)

    def ln_stats(xt, F, c0, eps, scratch):
        em.w("scalar", sV)
        em.w("vector", sC)
        em.o("vector", lambda e: e.reduce_sum(st[:, c0:c0 + 1], xt[:, 0:F], axis=AX),
             inc=sV)
        em.o("scalar", lambda e: e.activation(
            scratch[:, 0:F], xt[:, 0:F], AF.Square, accum_out=st[:, c0 + 1:c0 + 2]),
            inc=sC)
        em.w("vector", sC)
        tv(lambda e: e.tensor_scalar(
            st[:, c0 + 2:c0 + 3], st[:, c0:c0 + 1], 1.0 / F, None, ALU.mult))
        tv(lambda e: e.tensor_mul(
            st[:, c0 + 3:c0 + 4], st[:, c0 + 2:c0 + 3], st[:, c0 + 2:c0 + 3]))
        tv(lambda e: e.tensor_scalar(
            st[:, c0 + 4:c0 + 5], st[:, c0 + 1:c0 + 2], 1.0 / F, None, ALU.mult))
        tv(lambda e: e.tensor_sub(
            st[:, c0 + 5:c0 + 6], st[:, c0 + 4:c0 + 5], st[:, c0 + 3:c0 + 4]))
        em.w("scalar", sV)
        em.o("scalar", lambda e: e.activation(
            st[:, c0 + 4:c0 + 5], st[:, c0 + 5:c0 + 6], AF.Sqrt, bias=kcol[:, 0:1]),
            inc=sC)
        em.w("vector", sC)
        tv(lambda e: e.reciprocal(st[:, c0 + 5:c0 + 6], st[:, c0 + 4:c0 + 5]))
        em.w("vector", sV)

    def rms_stats(xt, F, c0, eps, scratch):
        em.w("scalar", sV)
        em.o("scalar", lambda e: e.activation(
            scratch[:, 0:F], xt[:, 0:F], AF.Square, accum_out=st[:, c0:c0 + 1]),
            inc=sC)
        em.w("vector", sC)
        tv(lambda e: e.tensor_scalar(
            st[:, c0 + 2:c0 + 3], st[:, c0:c0 + 1], 1.0 / F, None, ALU.mult))
        em.w("scalar", sV)
        em.o("scalar", lambda e: e.activation(
            st[:, c0:c0 + 1], st[:, c0 + 2:c0 + 3], AF.Sqrt, bias=kcol[:, 1:2]), inc=sC)
        em.w("vector", sC)
        tv(lambda e: e.reciprocal(st[:, c0 + 1:c0 + 2], st[:, c0:c0 + 1]))
        em.w("vector", sV)

    if "A" in phases:
        em.scope("phA")
        with ExitStack() as pk:
            A = lambda n, s, d: pk.enter_context(nc.sbuf_tensor(n, s, d))
            xa = A("xa", [128, DM], F32)
            lxa = A("lxa", [128, DM], F32)
            sqa = A("sqa", [128, DM], F32)
            ln2a = A("ln2a", [128, DM], F32)
            xna = A("xna", [128, DM], F32)
            xna_b = A("xna_b", [128, DM], BF16)
            xnT_b = A("xnT_b", [128, NDB * 128], BF16)
            nw_rep = A("nw_rep", [128, DM], F32)
            nb_rep = A("nb_rep", [128, DM], F32)
            dtb_rep = A("dtb_rep", [128, DM], F32)
            xpw_b = A("xpw_b", [128, NDB * 256], BF16)
            dtw_b = A("dtw_b", [128, DM], BF16)
            pj_del = A("pj_del", [128, 128], BF16)
            pbc_h = A("pbc_h", [128, 128], FP16)
            dta = A("dta", [128, DM], F32)
            spa = A("spa", [128, DM], F32)
            dtxa = A("dtxa", [128, DM], F32)
            dt_h = A("dt_h", [128, DM], FP16)
            dtx_h = A("dtx_h", [128, DM], FP16)
            stag = A("stag", [128, NCORE * 5 * 128], FP16)

            em.dma("sync", lambda e: e.dma_start(out=nw_rep[:, :], in_=nwrow.broadcast_to([128, DM])), qS)
            em.dma("sync", lambda e: e.dma_start(out=nb_rep[:, :], in_=nbrow.broadcast_to([128, DM])), qS)
            em.dma("sync", lambda e: e.dma_start(out=dtb_rep[:, :], in_=dtbrow.broadcast_to([128, DM])), qS)
            em.dma("gpsimd", lambda e: e.dma_start(
                out=xpw_b[:, :].rearrange("p (b n) -> p b n", b=NDB),
                in_=xpwT.rearrange("(b p) n -> p b n", p=128)), qG)
            em.dma("gpsimd", lambda e: e.dma_start(out=dtw_b[:, :], in_=dtwT), qG)

            for t in range(8):
                em.w("sync", sV)
                em.w("sync", sC)
                em.dma("sync", lambda e, t=t: e.dma_start(
                    out=xa[:, :], in_=x_own[t * 128:(t + 1) * 128, :]), qS)
                em.w("vector", qS)
                em.w("vector", sT)
                em.w("scalar", qS)
                em.w("scalar", sT)
                ln_stats(xa, DM, 0, EPS_LN, sqa)
                em.o("vector", lambda e: e.tensor_scalar(
                    lxa[:, :], xa[:, :], st[:, 2:3], st[:, 5:6],
                    ALU.subtract, ALU.mult), inc=sV)
                em.w("sync", sV)
                em.dma("sync", lambda e, t=t: e.dma_start(
                    out=sp_lx.ap()[t * 128:(t + 1) * 128, :], in_=lxa[:, :]), qS)
                if "lx" in dbg_outs:
                    em.dma("sync", lambda e, t=t: e.dma_start(
                        out=dbg_outs["lx"][t * 128:(t + 1) * 128, :], in_=lxa[:, :]), qS)
                ln_stats(lxa, DM, 6, EPS_LN, sqa)
                em.o("vector", lambda e: e.tensor_scalar(
                    ln2a[:, :], lxa[:, :], st[:, 8:9], st[:, 11:12],
                    ALU.subtract, ALU.mult))
                em.o("vector", lambda e: e.tensor_mul(sqa[:, :], ln2a[:, :], nw_rep[:, :]))
                em.o("vector", lambda e: e.tensor_add(xna[:, :], sqa[:, :], nb_rep[:, :]))
                em.o("vector", lambda e: e.tensor_copy(xna_b[:, :], xna[:, :]), inc=sV)
                if "xn" in dbg_outs:
                    em.w("sync", sV)
                    em.dma("sync", lambda e, t=t: e.dma_start(
                        out=dbg_outs["xn"][t * 128:(t + 1) * 128, :], in_=xna[:, :]), qS)
                em.w("tensor", sV)
                em.w("tensor", qG)
                for db in range(NDB):
                    mm_guard(ptbf)
                    em.o("tensor", lambda e, db=db: e.transpose(
                        ptbf[:, 0:128], xna_b[:, db * 128:(db + 1) * 128], ident_b[:, :]),
                        inc=sT)
                    em.w("scalar", sT)
                    em.o("scalar", lambda e, db=db: e.copy(
                        xnT_b[:, db * 128:(db + 1) * 128], ptbf[:, 0:128]), inc=sC)
                    consume(ptbf, sC)
                em.w("tensor", sC)
                for pr in range(2):
                    pst = ptA if pr == 0 else ptB
                    mm_guard(pst)
                    for db in range(NDB):
                        em.o("tensor", lambda e, pr=pr, db=db, pst=pst: e.matmul(
                            pst[:, :],
                            xpw_b[:, db * 256 + pr * 128: db * 256 + (pr + 1) * 128],
                            xnT_b[:, db * 128:(db + 1) * 128],
                            start=(db == 0), stop=(db == NDB - 1)),
                            inc=(sT if db == NDB - 1 else None))
                    em.w("vector", sT)
                    dst = pj_del if pr == 0 else pbc_h
                    em.o("vector", lambda e, dst=dst, pst=pst: e.tensor_copy(
                        dst[:, :], pst[:, :]), inc=sV)
                    consume(pst, sV)
                em.w("scalar", sV)
                for sh in range(NCORE):
                    em.o("scalar", lambda e, sh=sh: e.copy(
                        stag[:, (sh * 5 + 4) * 128:(sh * 5 + 5) * 128], pbc_h[:, :]),
                        inc=(sC if sh == NCORE - 1 else None))
                em.w("tensor", sV)
                for dch in range(4):
                    pst = ps[dch]
                    mm_guard(pst)
                    em.o("tensor", lambda e, dch=dch, pst=pst: e.matmul(
                        pst[:, :], pj_del[:, :], dtw_b[:, dch * 512:(dch + 1) * 512],
                        start=True, stop=True), inc=sT)
                em.w("vector", sT)
                for dch in range(4):
                    em.o("vector", lambda e, dch=dch: e.tensor_add(
                        dta[:, dch * 512:(dch + 1) * 512], ps[dch][:, :],
                        dtb_rep[:, dch * 512:(dch + 1) * 512]),
                        inc=(sV if dch == 3 else None))
                for dch in range(4):
                    consume(ps[dch], sV)
                em.w("scalar", sV)
                em.o("scalar", lambda e: e.activation(spa[:, :], dta[:, :], AF.Abs))
                em.o("scalar", lambda e: e.activation(spa[:, :], spa[:, :], AF.Exp, scale=-1.0))
                em.o("scalar", lambda e: e.activation(spa[:, :], spa[:, :], AF.Ln, bias=1.0),
                     inc=sC)
                em.w("vector", sC)
                em.o("vector", lambda e: e.tensor_relu(sqa[:, :], dta[:, :]))
                em.o("vector", lambda e: e.tensor_add(dta[:, :], sqa[:, :], spa[:, :]),
                     inc=sV)
                if "dt" in dbg_outs:
                    em.w("sync", sV)
                    em.dma("sync", lambda e, t=t: e.dma_start(
                        out=dbg_outs["dt"][t * 128:(t + 1) * 128, :], in_=dta[:, :]), qS)
                em.o("vector", lambda e: e.tensor_mul(dtxa[:, :], dta[:, :], xna[:, :]))
                em.o("vector", lambda e: e.tensor_copy(dt_h[:, :], dta[:, :]))
                em.o("vector", lambda e: e.tensor_copy(dtx_h[:, :], dtxa[:, :]), inc=sV)
                em.w("tensor", sV)
                em.w("scalar", qS)
                for which, src in ((0, dt_h), (1, dtx_h)):
                    for db in range(NDB):
                        sh, half = db // 2, db % 2
                        piece = which * 2 + half
                        mm_guard(pt16)
                        em.o("tensor", lambda e, src=src, db=db: e.transpose(
                            pt16[:, :], src[:, db * 128:(db + 1) * 128], ident_h[:, :]),
                            inc=sT)
                        em.w("scalar", sT)
                        em.o("scalar", lambda e, sh=sh, piece=piece: e.copy(
                            stag[:, (sh * 5 + piece) * 128:(sh * 5 + piece + 1) * 128],
                            pt16[:, :]), inc=sC)
                        consume(pt16, sC)
                em.w("sync", sC)
                em.w("sync", qS)
                em.dma("sync", lambda e, t=t: e.dma_start(
                    out=e1_in.ap().rearrange("(f p) n -> p f n", p=128)
                    [:, :, t * 128:(t + 1) * 128],
                    in_=stag[:, :].rearrange("p (f n) -> p f n", n=128)), qS)
                em.w("sync", qS)

            em.w("gpsimd", qS)
            with allow_ici_all_to_all():
                em.w("gpsimd", cc)
                em.o("gpsimd", lambda e: e.collective_compute(
                    "AllToAll", ALU.bypass, replica_groups=[list(range(NCORE))],
                    ins=[e1_in.ap()], outs=[e1_out.ap()]), inc=cc)
            em.pop()
            em.barrier(BAR)

    if "B" in phases:
        em.scope("phB")
        cc_e1 = cc.n
        with ExitStack() as pk:
            A = lambda n, s, d: pk.enter_context(nc.sbuf_tensor(n, s, d))
            dtT = A("dtT", [128, 2 * SEQ], FP16)
            dtxT = A("dtxT", [128, 2 * SEQ], FP16)
            Brep = A("Brep", [128, 2 * SEQ], FP16)
            Crep = A("Crep", [128, 2 * SEQ], FP16)
            dAb = A("dAb", [128, 2 * SEQ], F32)
            ub = A("ub", [128, SEQ], FP16)
            hb = A("hb", [128, SEQ], FP16)
            qscan = A("qscan", [128, SEQ], FP16)
            yb = A("yb", [128, 4 * SEQ], FP16)
            ybf = A("ybf", [128, SEQ], BF16)
            zero1 = A("zero1", [128, 1], F32)
            em.o("vector", lambda e: e.memset(zero1[:, :], 0.0), inc=sV)

            e1o = e1_out.ap().rearrange("(r m) n -> r m n", m=640)
            for s_seq in range(NSEQ):
                em.w("sync", cc, cc_e1)
                em.w("sync", sV)
                for k in range(2):
                    em.dma("sync", lambda e, k=k, s=s_seq: e.dma_start(
                        out=dtT[:, k * SEQ:(k + 1) * SEQ].rearrange("p (r n) -> p r n", r=8),
                        in_=e1o[:, k * 128:(k + 1) * 128, s * 512:(s + 1) * 512]
                        .rearrange("r m n -> m r n")), qS)
                    em.dma("sync", lambda e, k=k, s=s_seq: e.dma_start(
                        out=dtxT[:, k * SEQ:(k + 1) * SEQ].rearrange("p (r n) -> p r n", r=8),
                        in_=e1o[:, 256 + k * 128:256 + (k + 1) * 128, s * 512:(s + 1) * 512]
                        .rearrange("r m n -> m r n")), qS)
                for k in range(2):
                    em.o("vector", lambda e, k=k: e.memset(
                        yb[:, (k * 2) * SEQ:(k * 2 + 1) * SEQ], 0.0),
                        inc=(sV if k == 1 else None))
                ld_done = qS.n
                rep_done = {}
                yv_hist = {}
                step_hist = {}
                sv_base = sV.n
                for n in range(NST + 1):
                    if n < NST:
                        par = n % 2
                        if n >= 2:
                            em.w("sync", sV, yv_hist[n - 2])
                        em.dma("sync", lambda e, n=n, s=s_seq, par=par: e.dma_start(
                            out=Brep[:, par * SEQ:(par + 1) * SEQ]
                            .rearrange("p (r n) -> p r n", r=8),
                            in_=e1o[:, 512 + n:513 + n, s * 512:(s + 1) * 512]
                            .rearrange("r m n -> m r n").broadcast_to([128, 8, 512])), qS)
                        em.dma("sync", lambda e, n=n, s=s_seq, par=par: e.dma_start(
                            out=Crep[:, par * SEQ:(par + 1) * SEQ]
                            .rearrange("p (r n) -> p r n", r=8),
                            in_=e1o[0:1, 512 + NST + n:513 + NST + n, s * 512:(s + 1) * 512]
                            .broadcast_to([128, 8, 512])), qS)
                        rep_done[n] = qS.n
                    if n == 0:
                        continue
                    nn = n - 1
                    par = nn % 2
                    for k in range(2):
                        sp = (nn * 2 + k) % 2
                        em.w("scalar", qS, ld_done)
                        em.w("scalar", sV, step_hist.get(nn * 2 + k - 2, sv_base))
                        em.o("scalar", lambda e, k=k, nn=nn, sp=sp: e.activation(
                            dAb[:, sp * SEQ:(sp + 1) * SEQ],
                            dtT[:, k * SEQ:(k + 1) * SEQ], AF.Exp,
                            scale=A_sb[:, k * NST + nn:k * NST + nn + 1]), inc=sC)
                        em.w("vector", sC)
                        em.w("vector", qS, rep_done[nn])
                        em.o("vector", lambda e, k=k, par=par: e.tensor_mul(
                            ub[:, :], dtxT[:, k * SEQ:(k + 1) * SEQ],
                            Brep[:, par * SEQ:(par + 1) * SEQ]))
                        em.o("vector", lambda e, sp=sp: e.tensor_tensor_scan(
                            hb[:, :], dAb[:, sp * SEQ:(sp + 1) * SEQ], ub[:, :],
                            zero1[:, 0:1], ALU.mult, ALU.add), inc=sV)
                        em.w("vector", sV)
                        em.o("vector", lambda e, par=par: e.tensor_mul(
                            qscan[:, :], hb[:, :], Crep[:, par * SEQ:(par + 1) * SEQ]))
                        pp, qq = (nn % 2), ((nn + 1) % 2)
                        em.o("vector", lambda e, k=k, pp=pp, qq=qq: e.tensor_add(
                            yb[:, (k * 2 + qq) * SEQ:(k * 2 + qq + 1) * SEQ],
                            yb[:, (k * 2 + pp) * SEQ:(k * 2 + pp + 1) * SEQ], qscan[:, :]),
                            inc=sV)
                        step_hist[nn * 2 + k] = sV.n
                    yv_hist[nn] = sV.n
                for k in range(2):
                    em.o("vector", lambda e, k=k: e.tensor_copy(
                        ybf[:, :], yb[:, (k * 2) * SEQ:(k * 2 + 1) * SEQ]), inc=sV)
                    if "y" in dbg_outs:
                        em.o("vector", lambda e, k=k: e.tensor_copy(
                            dAb[:, 0:SEQ], yb[:, (k * 2) * SEQ:(k * 2 + 1) * SEQ]), inc=sV)
                        em.w("sync", sV)
                        em.dma("sync", lambda e, k=k, s=s_seq: e.dma_start(
                            out=dbg_outs["y"][s * DSL + k * 128: s * DSL + (k + 1) * 128, :],
                            in_=dAb[:, 0:SEQ]), qS)
                    em.w("sync", sV)
                    em.dma("sync", lambda e, k=k, s=s_seq: e.dma_start(
                        out=e2_in[s].ap().rearrange("(j m) n -> j m n", m=256)
                        [:, k * 128:(k + 1) * 128, :].rearrange("j m n -> m j n"),
                        in_=ybf[:, :].rearrange("p (j n) -> p j n", j=8)), qS)
                    em.w("vector", qS)
                em.w("gpsimd", qS)
                em.w("gpsimd", cc)
                with allow_ici_all_to_all():
                    em.o("gpsimd", lambda e, s=s_seq: e.collective_compute(
                        "AllToAll", ALU.bypass, replica_groups=[list(range(NCORE))],
                        ins=[e2_in[s].ap()], outs=[e2_out[s].ap()]), inc=cc)
            em.pop()
            em.barrier(BAR)

    if "C" in phases:
        em.scope("phC")
        cc_e2 = cc.n
        with ExitStack() as pk:
            A = lambda n, s, d: pk.enter_context(nc.sbuf_tensor(n, s, d))
            ypx_g = A("ypx_g", [128, NDB * 512], BF16)
            lxc = A("lxc", [128, DM], F32)
            ypxT_b = A("ypxT_b", [128, NDB * 512], BF16)
            opw_b = A("opw_b", [128, NDB * 512], BF16)
            mo_all = A("mo_all", [128, 4 * DM], F32)
            xc = A("xc", [128, DM], F32)
            sqc = A("sqc", [128, DM], F32)
            monl = A("monl", [128, DM], F32)
            x1c = A("x1c", [128, DM], F32)

            for s_seq in range(NSEQ):
                em.w("sync", cc, cc_e2)
                em.w("sync", sT)
                em.w("sync", sV)
                em.dma("sync", lambda e, s=s_seq: e.dma_start(
                    out=ypx_g[:, :].rearrange("p (b n) -> p b n", b=NDB),
                    in_=e2_out[s].ap().rearrange("(b p) n -> p b n", p=128)), qS)
                for lb in range(4):
                    em.w("sync", sT)
                    em.dma("sync", lambda e, s=s_seq, lb=lb: e.dma_start(
                        out=lxc[:, :],
                        in_=sp_lx.ap()[s * 512 + lb * 128: s * 512 + (lb + 1) * 128, :]), qS)
                    em.w("tensor", qS)
                    em.w("vector", qS)
                    for db in range(NDB):
                        pst = ptA if db % 2 == 0 else ptB
                        mm_guard(pst)
                        em.o("tensor", lambda e, db=db, pst=pst: e.transpose(
                            pst[:, :], lxc[:, db * 128:(db + 1) * 128], ident_f[:, :]),
                            inc=sT)
                        em.w("vector", sT)
                        em.o("vector", lambda e, db=db, lb=lb, pst=pst: e.scalar_tensor_tensor(
                            ypxT_b[:, db * 512 + lb * 128: db * 512 + (lb + 1) * 128],
                            pst[:, :], Dcol_sb[:, db:db + 1],
                            ypx_g[:, db * 512 + lb * 128: db * 512 + (lb + 1) * 128],
                            ALU.mult, ALU.add), inc=sV)
                        consume(pst, sV)
                for dch in range(4):
                    em.w("gpsimd", sT)
                    em.dma("gpsimd", lambda e, dch=dch: e.dma_start(
                        out=opw_b[:, :].rearrange("p (b n) -> p b n", b=NDB),
                        in_=opwT[:, dch * 512:(dch + 1) * 512]
                        .rearrange("(b p) n -> p b n", p=128)), qG)
                    em.w("tensor", qG)
                    em.w("tensor", sV)
                    for lb in range(4):
                        pst = ps[lb]
                        mm_guard(pst)
                        for db in range(NDB):
                            em.o("tensor", lambda e, db=db, lb=lb, pst=pst: e.matmul(
                                pst[:, :],
                                ypxT_b[:, db * 512 + lb * 128: db * 512 + (lb + 1) * 128],
                                opw_b[:, db * 512:(db + 1) * 512],
                                start=(db == 0), stop=(db == NDB - 1)),
                                inc=(sT if db == NDB - 1 else None))
                    em.w("scalar", sT)
                    for lb in range(4):
                        em.o("scalar", lambda e, lb=lb, dch=dch: e.copy(
                            mo_all[:, lb * DM + dch * 512: lb * DM + (dch + 1) * 512],
                            ps[lb][:, :]), inc=(sC if lb == 3 else None))
                    for lb in range(4):
                        consume(ps[lb], sC)
                for lb in range(4):
                    em.w("sync", sV)
                    em.dma("sync", lambda e, s=s_seq, lb=lb: e.dma_start(
                        out=xc[:, :],
                        in_=x_own[s * 512 + lb * 128: s * 512 + (lb + 1) * 128, :]), qS)
                    em.w("vector", qS)
                    em.w("vector", sC)
                    mot = mo_all[:, lb * DM:(lb + 1) * DM]
                    ln_stats(mot, DM, 12, EPS_LN, sqc)
                    em.o("vector", lambda e, mot=mot: e.tensor_scalar(
                        monl[:, :], mot[:, :], st[:, 14:15], st[:, 17:18],
                        ALU.subtract, ALU.mult))
                    em.o("vector", lambda e: e.tensor_add(x1c[:, :], xc[:, :], monl[:, :]),
                         inc=sV)
                    em.w("sync", sV)
                    em.dma("sync", lambda e, s=s_seq, lb=lb: e.dma_start(
                        out=sp_x1.ap()[s * 512 + lb * 128: s * 512 + (lb + 1) * 128, :],
                        in_=x1c[:, :]), qS)
                    if "x1" in dbg_outs:
                        em.dma("sync", lambda e, s=s_seq, lb=lb: e.dma_start(
                            out=dbg_outs["x1"][s * 512 + lb * 128: s * 512 + (lb + 1) * 128, :],
                            in_=x1c[:, :]), qS)
            em.pop()
            em.barrier(BAR)

    if "D" in phases:
        for s_seq in range(NSEQ):
            with ExitStack() as sk:
                with ExitStack() as pk:
                    A = lambda n, s, d: pk.enter_context(nc.sbuf_tensor(n + str(s_seq), s, d))
                    xd = A("xd", [128, DM], F32)
                    sqd = A("sqd", [128, DM], F32)
                    hb_t = A("hb_t", [128, DM], BF16)
                    l1_rep = A("l1_rep", [128, DM], F32)
                    stg = A("stg", [128, NDB * 128], BF16)
                    wq_b = A("wq_b", [128, NDB * 256], BF16)
                    wk_b = A("wk_b", [128, NDB * 256], BF16)
                    wv_b = A("wv_b", [128, NDB * 256], BF16)
                    wo_b = A("wo_b", [128, 2 * DM], BF16)
                    hTb = A("hTb", [128, NDB * 512], BF16)
                    qT_sb = A("qT_sb", [128, 2 * SEQ], BF16)
                    kT_sb = A("kT_sb", [128, 2 * SEQ], BF16)
                    vtmp = A("vtmp", [128, 512], BF16)
                    v_sb = A("v_sb", [128, 32 * 256], BF16)
                    sc_sb = A("sc_sb", [128, SEQ], F32)
                    p2_sb = A("p2_sb", [128, SEQ], BF16)
                    p2b_sb = A("p2b_sb", [128, SEQ], BF16)
                    pT_sb = A("pT_sb", [128, 512], BF16)
                    aoT_sb = A("aoT_sb", [128, 2 * SEQ], BF16)
                    arstg = A("arstg", [128, DM], BF16)
                    ars_sb = A("ars_sb", [128, DM], BF16)
                    mask_sb = A("mask_sb", [128, 4 * 512], BF16)
                    lx1t = A("lx1t", [128, DM], F32)
                    x2it = A("x2it", [128, DM], F32)

                    import ml_dtypes as _mld
                    mrow = np.zeros((4, 128, 512), np.float32)
                    for qo in range(4):
                        for i in range(128):
                            mrow[qo, i, qo * 128 + i + 1:] = -1e9
                    mask_dram = nc.inline_tensor(
                        mrow.transpose(1, 0, 2).reshape(128, 4 * 512)
                        .astype(_mld.bfloat16), f"mask{s_seq}")
                    em.dma("sync", lambda e: e.dma_start(
                        out=mask_sb[:, :], in_=mask_dram.ap()), qS)
                    em.dma("sync", lambda e: e.dma_start(
                        out=l1_rep[:, :], in_=l1row.broadcast_to([128, DM])), qS)
                    em.dma("gpsimd", lambda e: e.dma_start(
                        out=wq_b[:, :].rearrange("p (b n) -> p b n", b=NDB),
                        in_=wq_own.rearrange("(b p) n -> p b n", p=128)), qG)
                    em.dma("gpsimd", lambda e: e.dma_start(
                        out=wk_b[:, :].rearrange("p (b n) -> p b n", b=NDB),
                        in_=wk_own.rearrange("(b p) n -> p b n", p=128)), qG)
                    em.dma("gpsimd", lambda e: e.dma_start(
                        out=wv_b[:, :].rearrange("p (b n) -> p b n", b=NDB),
                        in_=wv_own.rearrange("(b p) n -> p b n", p=128)), qG)
                    em.dma("gpsimd", lambda e: e.dma_start(
                        out=wo_b[:, :].rearrange("p (b n) -> p b n", b=2),
                        in_=wo_own.rearrange("(b p) n -> p b n", p=128)), qG)

                    em.scope("D_h" + str(s_seq))
                    for lb in range(4):
                        em.w("sync", sV)
                        em.w("sync", sC)
                        em.dma("sync", lambda e, s=s_seq, lb=lb: e.dma_start(
                            out=xd[:, :],
                            in_=sp_x1.ap()[s * 512 + lb * 128: s * 512 + (lb + 1) * 128, :]), qS)
                        em.w("vector", qS)
                        em.w("vector", sT)
                        em.w("scalar", qS)
                        ln_stats(xd, DM, 18, EPS_LN, sqd)
                        em.o("vector", lambda e: e.tensor_scalar(
                            lx1t[:, :], xd[:, :], st[:, 20:21], st[:, 23:24],
                            ALU.subtract, ALU.mult), inc=sV)
                        em.w("sync", sV)
                        em.dma("sync", lambda e, s=s_seq, lb=lb: e.dma_start(
                            out=sp_lx1.ap()[s * 512 + lb * 128: s * 512 + (lb + 1) * 128, :],
                            in_=lx1t[:, :]), qS)
                        rms_stats(lx1t, DM, 24, EPS_RMS, sqd)
                        em.o("vector", lambda e: e.tensor_scalar(
                            sqd[:, :], lx1t[:, :], st[:, 25:26], None, ALU.mult))
                        em.o("vector", lambda e: e.tensor_mul(xd[:, :], sqd[:, :], l1_rep[:, :]))
                        em.o("vector", lambda e: e.tensor_copy(hb_t[:, :], xd[:, :]), inc=sV)
                        if "hln" in dbg_outs:
                            em.w("sync", sV)
                            em.dma("sync", lambda e, s=s_seq, lb=lb: e.dma_start(
                                out=dbg_outs["hln"][s * 512 + lb * 128: s * 512 + (lb + 1) * 128, :],
                                in_=xd[:, :]), qS)
                        em.w("tensor", sV)
                        em.w("scalar", qS)
                        if s_seq == 0 and lb == 0 and "hbt" in dbg_outs:
                            em.w("vector", sV)
                            em.o("vector", lambda e: e.tensor_copy(sqd[:, :], hb_t[:, :]),
                                 inc=sV)
                            em.w("sync", sV)
                            em.dma("sync", lambda e: e.dma_start(
                                out=dbg_outs["hbt"], in_=sqd[:, :]), qS)
                            em.w("vector", qS)
                        for db in range(NDB):
                            mm_guard(ptbf)
                            em.o("tensor", lambda e, db=db: e.transpose(
                                ptbf[:, 0:128], hb_t[:, db * 128:(db + 1) * 128], ident_b[:, :]),
                                inc=sT)
                            em.w("scalar", sT)
                            em.o("scalar", lambda e, db=db: e.copy(
                                stg[:, db * 128:(db + 1) * 128], ptbf[:, 0:128]), inc=sC)
                            consume(ptbf, sC)
                        if s_seq == 0 and lb == 0 and "stg" in dbg_outs:
                            em.w("vector", sC)
                            em.o("vector", lambda e: e.tensor_copy(sqd[:, :], stg[:, :]),
                                 inc=sV)
                            em.w("sync", sV)
                            em.dma("sync", lambda e: e.dma_start(
                                out=dbg_outs["stg"], in_=sqd[:, :]), qS)
                            em.w("vector", qS)
                            em.w("scalar", qS)
                        em.w("sync", sC)
                        em.w("sync", qS)
                        em.dma("sync", lambda e, s=s_seq, lb=lb: e.dma_start(
                            out=ht_in[s].ap().rearrange("(b p) n -> p b n", p=128)
                            [:, :, lb * 128:(lb + 1) * 128],
                            in_=stg[:, :].rearrange("p (b n) -> p b n", b=NDB)), qS)
                        em.w("sync", qS)
                    if s_seq == 0 and "hti" in dbg_outs:
                        em.w("sync", qS)
                        em.dma("sync", lambda e: e.dma_start(
                            out=hTb[:, 0:512].rearrange("p n -> p n"),
                            in_=ht_in[0].ap()[0:128, :]), qS)
                        em.w("sync", qS)
                        for j in range(1):
                            em.w("vector", qS)
                            em.o("vector", lambda e: e.tensor_copy(
                                sqd[:, 0:512], hTb[:, 0:512]), inc=sV)
                            em.w("sync", sV)
                            em.dma("sync", lambda e: e.dma_start(
                                out=dbg_outs["hti"], in_=sqd[:, 0:512]), qS)
                            em.w("vector", qS)
                    em.w("gpsimd", qS)
                    em.w("gpsimd", cc)
                    em.o("gpsimd", lambda e, s=s_seq: e.collective_compute(
                        "AllGather", ALU.bypass, replica_groups=[list(range(NCORE))],
                        ins=[ht_in[s].ap()], outs=[ht_out[s].ap()]), inc=cc)
                    cc_ht = cc.n

                    em.pop()
                    em.scope("D_qkv" + str(s_seq))
                    hto = ht_out[s_seq].ap().rearrange("(r b p) n -> r p b n", p=128, b=NDB)
                    for r in range(NCORE):
                        em.w("sync", cc, cc_ht)
                        em.w("sync", sT)
                        em.dma("sync", lambda e, r=r, hto=hto: e.dma_start(
                            out=hTb[:, :].rearrange("p (b n) -> p b n", b=NDB),
                            in_=hto[r]), qS)
                        if r == 0 and s_seq == 0 and "htb" in dbg_outs:
                            for j in range(4):
                                em.w("vector", qS)
                                em.o("vector", lambda e, j=j: e.tensor_copy(
                                    sqd[:, :], hTb[:, j * 2048:(j + 1) * 2048]), inc=sV)
                                em.w("sync", sV)
                                em.dma("sync", lambda e, j=j: e.dma_start(
                                    out=dbg_outs["htb"][:, j * 2048:(j + 1) * 2048],
                                    in_=sqd[:, :]), qS)
                                em.w("vector", qS)
                        em.w("tensor", qS)
                        em.w("tensor", qG)
                        for ti, (wt, dst, scale) in enumerate(
                                ((wq_b, qT_sb, QSCALE), (wk_b, kT_sb, 1.0), (wv_b, None, 1.0))):
                            for h2 in range(2):
                                pst = ps[(ti * 2 + h2) % 4]
                                mm_guard(pst)
                                for db in range(NDB):
                                    em.o("tensor", lambda e, db=db, h2=h2, wt=wt, pst=pst: e.matmul(
                                        pst[:, :],
                                        wt[:, db * 256 + h2 * 128: db * 256 + (h2 + 1) * 128],
                                        hTb[:, db * 512:(db + 1) * 512],
                                        start=(db == 0), stop=(db == NDB - 1)),
                                        inc=(sT if db == NDB - 1 else None))
                                em.w("scalar", sT)
                                if ti < 2:
                                    em.o("scalar", lambda e, h2=h2, r=r, dst=dst, scale=scale, pst=pst:
                                         e.activation(
                                             dst[:, h2 * SEQ + r * 512: h2 * SEQ + (r + 1) * 512],
                                             pst[:, :], AF.Copy,
                                             scale=(kcol[:, 2:3] if scale != 1.0 else 1.0)), inc=sC)
                                    consume(pst, sC)
                                else:
                                    em.w("scalar", sT)
                                    em.w("scalar", sT)
                                    em.o("scalar", lambda e, pst=pst: e.copy(
                                        vtmp[:, :], pst[:, :]), inc=sC)
                                    consume(pst, sC)
                                    em.w("tensor", sC)
                                    for j in range(4):
                                        mm_guard(ptbf)
                                        em.o("tensor", lambda e, j=j: e.transpose(
                                            ptbf[:, 0:128], vtmp[:, j * 128:(j + 1) * 128],
                                            ident_b[:, :]), inc=sT)
                                        em.w("scalar", sT)
                                        em.o("scalar", lambda e, r=r, j=j, h2=h2: e.copy(
                                            v_sb[:, (r * 4 + j) * 256 + h2 * 128:
                                                 (r * 4 + j) * 256 + (h2 + 1) * 128],
                                            ptbf[:, 0:128]), inc=sC)
                                        consume(ptbf, sC)

                    em.pop()
                    em.scope("D_attn" + str(s_seq))
                    em.w("tensor", sC)
                    for h2 in range(2):
                        for qb in range(32):
                            kvlen = (qb + 1) * 128
                            nch = (kvlen + 511) // 512
                            em.w("tensor", sV)
                            for ch in range(nch):
                                cl = min(512, kvlen - ch * 512)
                                pst = ps[ch % 2]
                                mm_guard(pst)
                                em.o("tensor", lambda e, h2=h2, qb=qb, ch=ch, cl=cl, pst=pst:
                                     e.matmul(
                                         pst[:, 0:cl],
                                         qT_sb[:, h2 * SEQ + qb * 128:h2 * SEQ + (qb + 1) * 128],
                                         kT_sb[:, h2 * SEQ + ch * 512: h2 * SEQ + ch * 512 + cl],
                                         start=True, stop=True), inc=sT)
                                em.w("vector", sT)
                                is_diag = (ch == qb // 4)
                                if is_diag:
                                    qo = qb % 4
                                    em.o("vector", lambda e, ch=ch, cl=cl, qo=qo, pst=pst:
                                         e.tensor_add(
                                             sc_sb[:, ch * 512:ch * 512 + cl], pst[:, 0:cl],
                                             mask_sb[:, qo * 512:qo * 512 + cl]), inc=sV)
                                else:
                                    em.o("vector", lambda e, ch=ch, cl=cl, pst=pst:
                                         e.tensor_copy(
                                             sc_sb[:, ch * 512:ch * 512 + cl], pst[:, 0:cl]),
                                         inc=sV)
                                consume(pst, sV)
                            em.o("vector", lambda e, kvlen=kvlen: e.reduce_max(
                                st[:, 44:45], sc_sb[:, 0:kvlen], axis=AX, negate=True),
                                inc=sV)
                            em.w("scalar", sV)
                            em.o("scalar", lambda e, kvlen=kvlen: e.activation(
                                p2_sb[:, 0:kvlen], sc_sb[:, 0:kvlen], AF.Exp,
                                bias=st[:, 44:45], accum_out=st[:, 45:46]), inc=sC)
                            em.w("vector", sC)
                            em.o("vector", lambda e: e.reciprocal(st[:, 46:47], st[:, 45:46]),
                                 inc=sV)
                            em.w("vector", sV)
                            em.o("vector", lambda e, kvlen=kvlen: e.tensor_scalar(
                                p2b_sb[:, 0:kvlen], p2_sb[:, 0:kvlen], st[:, 46:47], None,
                                ALU.mult), inc=sV)
                            em.w("tensor", sV)
                            pao = ps[2]
                            mm_guard(pao)
                            nkb = qb + 1
                            for g0 in range(0, nkb, 4):
                                gn = min(4, nkb - g0)
                                mm_guard(ptbf)
                                for j in range(gn):
                                    em.o("tensor", lambda e, g0=g0, j=j: e.transpose(
                                        ptbf[:, j * 128:(j + 1) * 128],
                                        p2b_sb[:, (g0 + j) * 128:(g0 + j + 1) * 128],
                                        ident_b[:, :]),
                                        inc=(sT if j == gn - 1 else None))
                                em.w("scalar", sT)
                                em.o("scalar", lambda e, gn=gn: e.copy(
                                    pT_sb[:, 0:gn * 128], ptbf[:, 0:gn * 128]), inc=sC)
                                consume(ptbf, sC)
                                em.w("tensor", sC)
                                for j in range(gn):
                                    kb = g0 + j
                                    em.o("tensor", lambda e, kb=kb, j=j, h2=h2, qb=qb, pao=pao:
                                         e.matmul(
                                             pao[:, 0:128],
                                             v_sb[:, kb * 256 + h2 * 128: kb * 256 + (h2 + 1) * 128],
                                             pT_sb[:, j * 128:(j + 1) * 128],
                                             start=(kb == 0), stop=(kb == qb)),
                                         inc=(sT if kb == qb else None))
                            em.w("scalar", sT)
                            em.o("scalar", lambda e, h2=h2, qb=qb, pao=pao: e.copy(
                                aoT_sb[:, h2 * SEQ + qb * 128: h2 * SEQ + (qb + 1) * 128],
                                pao[:, 0:128]), inc=sC)
                            consume(pao, sC)

                    if s_seq == 0:
                        for nm, buf in (("qt", qT_sb), ("kt", kT_sb), ("vv", v_sb),
                                        ("aot", aoT_sb)):
                            if nm in dbg_outs:
                                for j in range(4):
                                    em.w("vector", sC)
                                    em.o("vector", lambda e, buf=buf, j=j: e.tensor_copy(
                                        sqd[:, :], buf[:, j * 2048:(j + 1) * 2048]),
                                        inc=sV)
                                    em.w("sync", sV)
                                    em.dma("sync", lambda e, nm=nm, j=j: e.dma_start(
                                        out=dbg_outs[nm][:, j * 2048:(j + 1) * 2048],
                                        in_=sqd[:, :]), qS)
                                    em.w("vector", qS)
                    em.pop()
                    em.scope("D_wo" + str(s_seq))
                    em.w("tensor", sC)
                    for qb in range(32):
                        for dch in range(4):
                            pst = ps[dch % 2]
                            mm_guard(pst)
                            for h2 in range(2):
                                em.o("tensor", lambda e, qb=qb, dch=dch, h2=h2, pst=pst:
                                     e.matmul(
                                         pst[:, :],
                                         aoT_sb[:, h2 * SEQ + qb * 128: h2 * SEQ + (qb + 1) * 128],
                                         wo_b[:, h2 * DM + dch * 512: h2 * DM + (dch + 1) * 512],
                                         start=(h2 == 0), stop=(h2 == 1)),
                                     inc=(sT if h2 == 1 else None))
                            em.w("scalar", sT)
                            em.w("scalar", qS)
                            em.o("scalar", lambda e, dch=dch, pst=pst: e.copy(
                                arstg[:, dch * 512:(dch + 1) * 512], pst[:, :]),
                                inc=sC)
                            consume(pst, sC)
                        em.w("sync", sC)
                        em.dma("sync", lambda e, s=s_seq, qb=qb: e.dma_start(
                            out=ars_in[s].ap()[qb * 128:(qb + 1) * 128, :],
                            in_=arstg[:, :]), qS)
                    em.w("gpsimd", qS)
                    em.w("gpsimd", cc)
                    em.o("gpsimd", lambda e, s=s_seq: e.collective_compute(
                        "ReduceScatter", ALU.add, replica_groups=[list(range(NCORE))],
                        ins=[ars_in[s].ap()], outs=[ars_out[s].ap()]), inc=cc)
                    cc_ars = cc.n
                    em.pop()
                    em.scope("D_x2i" + str(s_seq))
                    for lb in range(4):
                        em.w("sync", cc, cc_ars)
                        em.w("sync", sV)
                        em.dma("sync", lambda e, s=s_seq, lb=lb: e.dma_start(
                            out=ars_sb[:, :],
                            in_=ars_out[s].ap()[lb * 128:(lb + 1) * 128, :]), qS)
                        em.dma("sync", lambda e, s=s_seq, lb=lb: e.dma_start(
                            out=lx1t[:, :],
                            in_=sp_lx1.ap()[s * 512 + lb * 128: s * 512 + (lb + 1) * 128, :]), qS)
                        em.w("vector", qS)
                        em.o("vector", lambda e, lb=lb: e.tensor_add(
                            x2it[:, :], lx1t[:, :], ars_sb[:, :]), inc=sV)
                        em.w("sync", sV)
                        em.dma("sync", lambda e, s=s_seq, lb=lb: e.dma_start(
                            out=sp_x2i.ap()[s * 512 + lb * 128: s * 512 + (lb + 1) * 128, :],
                            in_=x2it[:, :]), qS)
                        if "x2i" in dbg_outs:
                            em.dma("sync", lambda e, s=s_seq, lb=lb: e.dma_start(
                                out=dbg_outs["x2i"][s * 512 + lb * 128: s * 512 + (lb + 1) * 128, :],
                                in_=x2it[:, :]), qS)
                    em.pop()
                    em.barrier(BAR)
                if "E" in phases:
                    _phase_E(nc, em, BAR, s_seq, sp_x2i, dict(
                        l2row=l2row, gwT=gwT, egT=egT, euT=euT, edT=edT,
                        sgT=sgT, suT=suT, sdT=sdT, kcol=kcol,
                        tt_in=tt_in, tt_out=tt_out, cw_in=cw_in, cw_out=cw_out,
                        mrs_in=mrs_in, mrs_out=mrs_out, sp_x1=sp_x1,
                        out_own=out_own, dbg_outs=dbg_outs,
                        ident_f=ident_f, ident_b=ident_b, st=st,
                        ps=ps, ptA=ptA, ptB=ptB, ptbf=ptbf,
                        qS=qS, qG=qG, cc=cc, sT=sT, sV=sV, sC=sC,
                        mm_guard=mm_guard, consume=consume,
                        ln_stats=ln_stats, rms_stats=rms_stats))
                em.barrier(BAR)

    em.w("sync", qG)
    em.w("sync", cc)
    em.w("sync", qS)

    def _replay(eng, ops):
        stack = []
        for item in ops:
            if callable(item):
                item(eng)
            elif item[0] == "scope":
                cm = nc.named_scope(item[1])
                cm.__enter__()
                stack.append(cm)
            else:
                stack.pop().__exit__(None, None, None)
        while stack:
            stack.pop().__exit__(None, None, None)
    with allow_ici_all_to_all():
        with nc.Block() as blk:
            for name in ENGS:
                getattr(blk, name)(lambda eng, ops=tuple(em.q[name]): _replay(eng, ops))
    stk.close()
    return nc, dbg_outs


def _phase_E(nc, em, BAR, s_seq, sp_x2i, z):
    qS, qG, cc, sT, sV, sC = z["qS"], z["qG"], z["cc"], z["sT"], z["sV"], z["sC"]
    mm_guard, consume = z["mm_guard"], z["consume"]
    ps, ptA, ptB, ptbf = z["ps"], z["ptA"], z["ptB"], z["ptbf"]
    st = z["st"]
    ident_f, ident_b = z["ident_f"], z["ident_b"]
    dbg_outs = z["dbg_outs"]
    ln_stats, rms_stats = z["ln_stats"], z["rms_stats"]

    with ExitStack() as pk:
        A = lambda n, s, d: pk.enter_context(nc.sbuf_tensor(n + "E" + str(s_seq), s, d))
        l2_rep = A("l2_rep", [128, DM], F32)
        x2e = A("x2e", [128, DM], F32)
        te = A("te", [128, DM], F32)
        tb = A("tb", [128, DM], BF16)
        sqe = A("sqe", [128, DM], F32)
        tT_b = A("tT_b", [128, NDB * 512], BF16)
        gw_b = A("gw_b", [128, NDB * NE], BF16)
        glog = A("glog", [128, NE], F32)
        mx8 = A("mx8", [128, 8], F32)
        ind = A("ind", [128, NE], F32)
        wsel = A("wsel", [128, NE], F32)
        cwt = A("cwt", [128, NE], F32)
        cwT_sb = A("cwT_sb", [128, OWN], F32)

        em.dma("sync", lambda e: e.dma_start(
            out=l2_rep[:, :], in_=z["l2row"].broadcast_to([128, DM])), qS)
        em.dma("gpsimd", lambda e: e.dma_start(
            out=gw_b[:, :].rearrange("p (b n) -> p b n", b=NDB),
            in_=z["gwT"].rearrange("(b p) n -> p b n", p=128)), qG)

        for lb in range(4):
            em.w("sync", sV)
            em.w("sync", sC)
            em.dma("sync", lambda e, s=s_seq, lb=lb: e.dma_start(
                out=x2e[:, :],
                in_=sp_x2i.ap()[s * 512 + lb * 128: s * 512 + (lb + 1) * 128, :]), qS)
            em.w("vector", qS)
            em.w("scalar", qS)
            em.w("vector", sT)
            rms_stats(x2e, DM, 28, EPS_RMS, sqe)
            em.o("vector", lambda e: e.tensor_scalar(
                sqe[:, :], x2e[:, :], st[:, 29:30], None, ALU.mult))
            em.o("vector", lambda e: e.tensor_mul(te[:, :], sqe[:, :], l2_rep[:, :]))
            em.o("vector", lambda e: e.tensor_copy(tb[:, :], te[:, :]), inc=sV)
            em.w("tensor", sV)
            for db in range(NDB):
                mm_guard(ptbf)
                em.o("tensor", lambda e, db=db: e.transpose(
                    ptbf[:, 0:128], tb[:, db * 128:(db + 1) * 128], ident_b[:, :]), inc=sT)
                em.w("scalar", sT)
                em.o("scalar", lambda e, db=db, lb=lb: e.copy(
                    tT_b[:, db * 512 + lb * 128: db * 512 + (lb + 1) * 128],
                    ptbf[:, 0:128]), inc=sC)
                consume(ptbf, sC)
            em.w("tensor", sC)
            em.w("tensor", qG)
            mm_guard(ptA)
            for db in range(NDB):
                em.o("tensor", lambda e, db=db, lb=lb: e.matmul(
                    ptA[:, 0:NE],
                    tT_b[:, db * 512 + lb * 128: db * 512 + (lb + 1) * 128],
                    gw_b[:, db * NE:(db + 1) * NE],
                    start=(db == 0), stop=(db == NDB - 1)),
                    inc=(sT if db == NDB - 1 else None))
            em.w("vector", sT)
            em.o("vector", lambda e: e.tensor_copy(glog[:, :], ptA[:, 0:NE]), inc=sV)
            consume(ptA, sV)
            em.w("vector", sV)
            if "gl" in dbg_outs:
                em.w("sync", sV)
                em.dma("sync", lambda e, s=s_seq, lb=lb: e.dma_start(
                    out=dbg_outs["gl"][s * 512 + lb * 128: s * 512 + (lb + 1) * 128, :],
                    in_=glog[:, :]), qS)
            em.o("vector", lambda e: e.max(mx8[:, :], glog[:, :]), inc=sV)
            em.w("vector", sV)
            em.o("vector", lambda e: e.tensor_scalar(
                st[:, 40:41], mx8[:, 0:1], -1.0, None, ALU.mult), inc=sV)
            em.w("vector", sV)
            em.o("vector", lambda e: e.tensor_scalar(
                ind[:, :], glog[:, :], mx8[:, 5:6], None, ALU.is_ge), inc=sV)
            em.w("scalar", sV)
            em.o("scalar", lambda e: e.activation(
                wsel[:, :], glog[:, :], AF.Exp, bias=st[:, 40:41]), inc=sC)
            em.w("vector", sC)
            em.o("vector", lambda e: e.tensor_mul(glog[:, :], wsel[:, :], ind[:, :]),
                 inc=sV)
            em.w("vector", sV)
            em.o("vector", lambda e: e.reduce_sum(st[:, 41:42], glog[:, :], axis=AX),
                 inc=sV)
            em.w("vector", sV)
            em.o("vector", lambda e: e.reciprocal(st[:, 42:43], st[:, 41:42]), inc=sV)
            em.w("vector", sV)
            em.o("vector", lambda e: e.tensor_scalar(
                cwt[:, :], glog[:, :], st[:, 42:43], None, ALU.mult), inc=sV)
            if "cw" in dbg_outs:
                em.w("sync", sV)
                em.dma("sync", lambda e, s=s_seq, lb=lb: e.dma_start(
                    out=dbg_outs["cw"][s * 512 + lb * 128: s * 512 + (lb + 1) * 128, :],
                    in_=cwt[:, :]), qS)
            em.w("tensor", sV)
            mm_guard(ptB)
            em.o("tensor", lambda e: e.transpose(
                ptB[0:NE, :], cwt[:, :], ident_f[:, :]), inc=sT)
            em.w("vector", sT)
            em.o("vector", lambda e, lb=lb: e.tensor_copy(
                cwT_sb[0:NE, lb * 128:(lb + 1) * 128], ptB[0:NE, 0:128]), inc=sV)
            consume(ptB, sV)
        em.w("sync", sC)
        em.w("sync", sV)
        em.dma("sync", lambda e, s=s_seq: e.dma_start(
            out=z["tt_in"][s].ap().rearrange("(b p) n -> p b n", p=128),
            in_=tT_b[:, :].rearrange("p (b n) -> p b n", b=NDB)), qS)
        em.dma("sync", lambda e, s=s_seq: e.dma_start(
            out=z["cw_in"][s].ap(), in_=cwT_sb[0:NE, :]), qS)
        em.w("gpsimd", qS)
        em.w("gpsimd", cc)
        em.o("gpsimd", lambda e, s=s_seq: e.collective_compute(
            "AllGather", ALU.bypass, replica_groups=[list(range(NCORE))],
            ins=[z["tt_in"][s].ap()], outs=[z["tt_out"][s].ap()]), inc=cc)
        with allow_ici_all_to_all():
            em.w("gpsimd", cc)
            em.o("gpsimd", lambda e, s=s_seq: e.collective_compute(
                "AllToAll", ALU.bypass, replica_groups=[list(range(NCORE))],
                ins=[z["cw_in"][s].ap()], outs=[z["cw_out"][s].ap()]), inc=cc)
        cc_moe = cc.n

        with ExitStack() as rk:
            B = lambda n, s, d: rk.enter_context(nc.sbuf_tensor(n + "R" + str(s_seq), s, d))
            tT_all = B("tT_all", [128, NDB * 1024], BF16)
            hh_w = B("hh_w", [128, 2 * NIB * 1024], BF16)
            eg_b = B("eg_b", [128, 2 * NDB * 128], BF16)
            eu_b = B("eu_b", [128, 2 * NDB * 128], BF16)
            ed_b = B("ed_b", [128, 4 * EI], BF16)
            cwrep = B("cwrep", [128, 1024], F32)
            sil_b = B("sil_b", [128, 1024], BF16)
            uw_b = B("uw_b", [128, 1024], BF16)
            mstg = B("mstg", [128, 2048], BF16)

            tto = z["tt_out"][s_seq].ap().rearrange("(r b p) n -> p b r n", p=128, b=NDB)
            cwo = z["cw_out"][s_seq].ap().rearrange("(r e) n -> e r n", e=2)
            for qt in range(4):
                em.w("sync", cc, cc_moe)
                em.w("sync", sT)
                for rr in range(2):
                    em.dma("sync", lambda e, qt=qt, rr=rr: e.dma_start(
                        out=tT_all[:, :].rearrange("p (b m) -> p b m", b=NDB)
                        [:, :, rr * 512:(rr + 1) * 512],
                        in_=tto[:, :, 2 * qt + rr, :]), qS)
                for e2 in range(2):
                    em.w("sync", sV)
                    em.dma("sync", lambda e, e2=e2, qt=qt: e.dma_start(
                        out=cwrep[:, :].rearrange("p (r n) -> p r n", r=2),
                        in_=cwo[e2:e2 + 1, 2 * qt:2 * qt + 2, :]
                        .broadcast_to([128, 2, 512])), qS)
                    cw_ld = qS.n
                    ld_hist = {}
                    mm_hist = {}
                    st_base = sT.n
                    for ib in range(NIB + 1):
                        if ib < NIB:
                            par = ib % 2
                            em.w("gpsimd", sT, mm_hist.get(ib - 2, st_base))
                            em.dma("gpsimd", lambda e, e2=e2, ib=ib, par=par: e.dma_start(
                                out=eg_b[:, par * 2048:(par + 1) * 2048]
                                .rearrange("p (b n) -> p b n", b=NDB),
                                in_=z["egT"][e2:e2 + 1, :, ib * 128:(ib + 1) * 128]
                                .rearrange("o (b p) n -> p b (o n)", p=128)), qG)
                            em.dma("gpsimd", lambda e, e2=e2, ib=ib, par=par: e.dma_start(
                                out=eu_b[:, par * 2048:(par + 1) * 2048]
                                .rearrange("p (b n) -> p b n", b=NDB),
                                in_=z["euT"][e2:e2 + 1, :, ib * 128:(ib + 1) * 128]
                                .rearrange("o (b p) n -> p b (o n)", p=128)), qG)
                            ld_hist[ib] = qG.n
                        if ib == 0:
                            continue
                        jb = ib - 1
                        par = jb % 2
                        em.w("tensor", qG, ld_hist[jb])
                        em.w("tensor", qS)
                        for lch in range(2):
                            pg, pu = ps[lch * 2], ps[lch * 2 + 1]
                            mm_guard(pg)
                            mm_guard(pu)
                            for db in range(NDB):
                                em.o("tensor", lambda e, db=db, lch=lch, pg=pg, par=par: e.matmul(
                                    pg[:, :],
                                    eg_b[:, par * 2048 + db * 128: par * 2048 + (db + 1) * 128],
                                    tT_all[:, db * 1024 + lch * 512: db * 1024 + (lch + 1) * 512],
                                    start=(db == 0), stop=(db == NDB - 1)),
                                    inc=(sT if db == NDB - 1 else None))
                            for db in range(NDB):
                                em.o("tensor", lambda e, db=db, lch=lch, pu=pu, par=par: e.matmul(
                                    pu[:, :],
                                    eu_b[:, par * 2048 + db * 128: par * 2048 + (db + 1) * 128],
                                    tT_all[:, db * 1024 + lch * 512: db * 1024 + (lch + 1) * 512],
                                    start=(db == 0), stop=(db == NDB - 1)),
                                    inc=(sT if db == NDB - 1 else None))
                        mm_hist[jb] = sT.n
                        for lch in range(2):
                            pg, pu = ps[lch * 2], ps[lch * 2 + 1]
                            em.w("scalar", sT, mm_hist[jb] - 2 * (1 - lch))
                            em.w("scalar", sV)
                            em.o("scalar", lambda e, pg=pg, lch=lch: e.activation(
                                sil_b[:, lch * 512:(lch + 1) * 512], pg[:, :], AF.Silu),
                                inc=sC)
                            consume(pg, sC)
                            em.w("vector", sT, mm_hist[jb] - 2 * (1 - lch))
                            em.w("vector", sC)
                            em.w("vector", qS, cw_ld)
                            em.o("vector", lambda e, lch=lch, pu=pu: e.scalar_tensor_tensor(
                                uw_b[:, lch * 512:(lch + 1) * 512], pu[:, :], 1.0,
                                cwrep[:, lch * 512:(lch + 1) * 512],
                                ALU.mult, ALU.mult), inc=sV)
                            consume(pu, sV)
                            em.o("vector", lambda e, e2=e2, jb=jb, lch=lch: e.tensor_mul(
                                hh_w[:, (e2 * NIB + jb) * 1024 + lch * 512:
                                     (e2 * NIB + jb) * 1024 + (lch + 1) * 512],
                                uw_b[:, lch * 512:(lch + 1) * 512],
                                sil_b[:, lch * 512:(lch + 1) * 512]), inc=sV)
                dld_hist = {}
                dmm_hist = {}
                dma_hist = {}
                dst_base = sT.n
                dq_base = qS.n
                for dob in range(NDB + 1):
                    if dob < NDB:
                        dpar = dob % 2
                        em.w("gpsimd", sT, dmm_hist.get(dob - 2, dst_base))
                        for ee in range(2):
                            em.dma("gpsimd", lambda e, dob=dob, ee=ee, dpar=dpar: e.dma_start(
                                out=ed_b[:, (dpar * 2 + ee) * EI:(dpar * 2 + ee + 1) * EI]
                                .rearrange("p (b n) -> p b n", b=NIB),
                                in_=z["edT"][ee:ee + 1, :, dob * 128:(dob + 1) * 128]
                                .rearrange("o (b p) n -> p b (o n)", p=128)), qG)
                        dld_hist[dob] = qG.n
                    if dob == 0:
                        continue
                    job = dob - 1
                    dpar = job % 2
                    em.w("tensor", qG, dld_hist[job])
                    if job == 0:
                        em.w("tensor", sV)
                    for lch in range(2):
                        pst = ps[(job % 2) * 2 + lch]
                        mm_guard(pst)
                        for e2 in range(2):
                            for ib in range(NIB):
                                em.o("tensor", lambda e, e2=e2, ib=ib, lch=lch, pst=pst, dpar=dpar:
                                     e.matmul(
                                         pst[:, :],
                                         ed_b[:, (dpar * 2 + e2) * EI + ib * 128:
                                              (dpar * 2 + e2) * EI + (ib + 1) * 128],
                                         hh_w[:, (e2 * NIB + ib) * 1024 + lch * 512:
                                              (e2 * NIB + ib) * 1024 + (lch + 1) * 512],
                                         start=(e2 == 0 and ib == 0),
                                         stop=(e2 == 1 and ib == NIB - 1)),
                                     inc=(sT if (e2 == 1 and ib == NIB - 1) else None))
                    dmm_hist[job] = sT.n
                    for lch in range(2):
                        pst = ps[(job % 2) * 2 + lch]
                        em.w("vector", sT, dmm_hist[job] - (1 - lch))
                        em.w("vector", qS, dma_hist.get(job - 2, dq_base))
                        em.o("vector", lambda e, lch=lch, pst=pst, dpar=dpar: e.tensor_copy(
                            mstg[:, dpar * 1024 + lch * 512: dpar * 1024 + (lch + 1) * 512],
                            pst[:, :]), inc=sV)
                        consume(pst, sV)
                    em.w("sync", sV)
                    em.dma("sync", lambda e, s=s_seq, job=job, qt=qt, dpar=dpar: e.dma_start(
                        out=z["mrs_in"][s].ap().rearrange("(j m) n -> j m n", m=DM)
                        [2 * qt:2 * qt + 2, job * 128:(job + 1) * 128, :]
                        .rearrange("j m n -> m j n"),
                        in_=mstg[:, dpar * 1024:(dpar + 1) * 1024]
                        .rearrange("p (j n) -> p j n", j=2)), qS)
                    dma_hist[job] = qS.n
            em.w("gpsimd", qS)
            em.w("gpsimd", cc)
            em.o("gpsimd", lambda e, s=s_seq: e.collective_compute(
                "ReduceScatter", ALU.add, replica_groups=[list(range(NCORE))],
                ins=[z["mrs_in"][s].ap()], outs=[z["mrs_out"][s].ap()]), inc=cc)
        cc_mrs = cc.n

        with ExitStack() as hk:
            C = lambda n, s, d: hk.enter_context(nc.sbuf_tensor(n + "S" + str(s_seq), s, d))
            sg_b = C("sg_b", [128, NDB * 128], BF16)
            su_b = C("su_b", [128, NDB * 128], BF16)
            sd_b = C("sd_b", [128, DM], BF16)
            hhs = C("hhs", [128, NSB * 512], BF16)
            ssil = C("ssil", [128, 512], BF16)
            comb = C("comb", [128, 4 * DM], F32)
            x2ic = C("x2ic", [128, 512], F32)
            rt_sb = C("rt_sb", [128, DM], BF16)
            mrs_sb = C("mrs_sb", [128, DM], BF16)
            ffn_t = C("ffn_t", [128, DM], F32)
            monl2 = C("monl2", [128, DM], F32)
            x1e = C("x1e", [128, DM], F32)
            sqe2 = C("sqe2", [128, DM], F32)

            for sib in range(NSB):
                em.w("gpsimd", sT)
                em.dma("gpsimd", lambda e, sib=sib: e.dma_start(
                    out=sg_b[:, :].rearrange("p (b n) -> p b n", b=NDB),
                    in_=z["sgT"][:, sib * 128:(sib + 1) * 128]
                    .rearrange("(b p) n -> p b n", p=128)), qG)
                em.dma("gpsimd", lambda e, sib=sib: e.dma_start(
                    out=su_b[:, :].rearrange("p (b n) -> p b n", b=NDB),
                    in_=z["suT"][:, sib * 128:(sib + 1) * 128]
                    .rearrange("(b p) n -> p b n", p=128)), qG)
                em.w("tensor", qG)
                pg, pu = ps[0], ps[1]
                mm_guard(pg)
                for db in range(NDB):
                    em.o("tensor", lambda e, db=db, pg=pg: e.matmul(
                        pg[:, :], sg_b[:, db * 128:(db + 1) * 128],
                        tT_b[:, db * 512:(db + 1) * 512],
                        start=(db == 0), stop=(db == NDB - 1)),
                        inc=(sT if db == NDB - 1 else None))
                mm_guard(pu)
                for db in range(NDB):
                    em.o("tensor", lambda e, db=db, pu=pu: e.matmul(
                        pu[:, :], su_b[:, db * 128:(db + 1) * 128],
                        tT_b[:, db * 512:(db + 1) * 512],
                        start=(db == 0), stop=(db == NDB - 1)),
                        inc=(sT if db == NDB - 1 else None))
                em.w("scalar", sT)
                em.w("scalar", sV)
                em.o("scalar", lambda e, pg=pg: e.activation(
                    ssil[:, :], pg[:, :], AF.Silu), inc=sC)
                consume(pg, sC)
                em.w("vector", sT)
                em.w("vector", sC)
                em.o("vector", lambda e, sib=sib, pu=pu: e.scalar_tensor_tensor(
                    hhs[:, sib * 512:(sib + 1) * 512], pu[:, :], 1.0,
                    ssil[:, :], ALU.mult, ALU.mult), inc=sV)
                consume(pu, sV)
            if "hhs" in dbg_outs:
                for j in range(4):
                    em.w("vector", sV)
                    em.o("vector", lambda e, j=j: e.tensor_copy(
                        sqe2[:, 0:512], hhs[:, j * 512:(j + 1) * 512]), inc=sV)
                    em.w("sync", sV)
                    em.dma("sync", lambda e, s=s_seq, j=j: e.dma_start(
                        out=dbg_outs["hhs"][s * 128:(s + 1) * 128, j * 512:(j + 1) * 512],
                        in_=sqe2[:, 0:512]), qS)
                    em.w("vector", qS)
            for dq in range(4):
                sld_hist = {}
                smm_hist = {}
                sst_base = sT.n
                for sibx in range(NSB + 1):
                    if sibx < NSB:
                        spar = sibx % 2
                        em.w("gpsimd", sT, smm_hist.get(sibx - 2, sst_base))
                        em.dma("gpsimd", lambda e, sibx=sibx, dq=dq, spar=spar: e.dma_start(
                            out=sd_b[:, spar * 512:(spar + 1) * 512],
                            in_=z["sdT"][sibx * 128:(sibx + 1) * 128,
                                         dq * 512:(dq + 1) * 512]), qG)
                        sld_hist[sibx] = qG.n
                    if sibx == 0:
                        continue
                    sib = sibx - 1
                    spar = sib % 2
                    em.w("tensor", qG, sld_hist[sib])
                    if sib == 0:
                        em.w("tensor", sV)
                    for lb in range(4):
                        pst = ps[lb]
                        if sib == 0:
                            mm_guard(pst)
                        em.o("tensor", lambda e, sib=sib, lb=lb, pst=pst, spar=spar: e.matmul(
                            pst[:, :], hhs[:, sib * 512 + lb * 128: sib * 512 + (lb + 1) * 128],
                            sd_b[:, spar * 512:(spar + 1) * 512],
                            start=(sib == 0), stop=(sib == NSB - 1)),
                            inc=(sT if lb == 3 else None))
                    smm_hist[sib] = sT.n
                em.w("vector", sT)
                for lb in range(4):
                    em.w("sync", sV)
                    em.dma("sync", lambda e, s=s_seq, lb=lb, dq=dq: e.dma_start(
                        out=x2ic[:, :],
                        in_=sp_x2i.ap()[s * 512 + lb * 128: s * 512 + (lb + 1) * 128,
                                        dq * 512:(dq + 1) * 512]), qS)
                    em.w("vector", qS)
                    em.o("vector", lambda e, lb=lb, dq=dq: e.tensor_add(
                        comb[:, lb * DM + dq * 512: lb * DM + (dq + 1) * 512],
                        x2ic[:, :], ps[lb][:, :]), inc=sV)
                    consume(ps[lb], sV)
            em.w("sync", cc, cc_mrs)
            for lb in range(4):
                em.w("sync", sV)
                em.dma("sync", lambda e, s=s_seq, lb=lb: e.dma_start(
                    out=mrs_sb[:, :].rearrange("p (b n) -> p b n", b=NDB),
                    in_=z["mrs_out"][s].ap().rearrange("(b p) n -> p b n", p=128)
                    [:, :, lb * 128:(lb + 1) * 128]), qS)
                em.w("tensor", qS)
                em.w("tensor", sC)
                for db in range(NDB):
                    mm_guard(ptbf)
                    em.o("tensor", lambda e, db=db: e.transpose(
                        ptbf[:, 0:128], mrs_sb[:, db * 128:(db + 1) * 128], ident_b[:, :]),
                        inc=sT)
                    em.w("scalar", sT)
                    em.o("scalar", lambda e, db=db: e.copy(
                        rt_sb[:, db * 128:(db + 1) * 128], ptbf[:, 0:128]), inc=sC)
                    consume(ptbf, sC)
                if "shr" in dbg_outs:
                    em.w("sync", sV)
                    em.dma("sync", lambda e, s=s_seq, lb=lb: e.dma_start(
                        out=dbg_outs["shr"][s * 512 + lb * 128: s * 512 + (lb + 1) * 128, :],
                        in_=comb[:, lb * DM:(lb + 1) * DM]), qS)
                em.w("vector", sC)
                em.o("vector", lambda e, lb=lb: e.tensor_add(
                    ffn_t[:, :], comb[:, lb * DM:(lb + 1) * DM], rt_sb[:, :]), inc=sV)
                if "routed" in dbg_outs:
                    em.w("sync", sV)
                    em.dma("sync", lambda e, s=s_seq, lb=lb: e.dma_start(
                        out=dbg_outs["routed"][s * 512 + lb * 128: s * 512 + (lb + 1) * 128, :],
                        in_=ffn_t[:, :]), qS)
                ln_stats(ffn_t, DM, 34, EPS_LN, sqe2)
                em.o("vector", lambda e: e.tensor_scalar(
                    monl2[:, :], ffn_t[:, :], st[:, 36:37], st[:, 39:40],
                    ALU.subtract, ALU.mult), inc=sV)
                em.w("sync", sV)
                em.dma("sync", lambda e, s=s_seq, lb=lb: e.dma_start(
                    out=x1e[:, :],
                    in_=z["sp_x1"].ap()[s * 512 + lb * 128: s * 512 + (lb + 1) * 128, :]), qS)
                em.w("vector", qS)
                em.o("vector", lambda e: e.tensor_add(sqe2[:, :], monl2[:, :], x1e[:, :]),
                     inc=sV)
                em.w("sync", sV)
                em.dma("sync", lambda e, s=s_seq, lb=lb: e.dma_start(
                    out=z["out_own"][s * 512 + lb * 128: s * 512 + (lb + 1) * 128, :],
                    in_=sqe2[:, :]), qS)


def make_core_inputs(inputs, c):
    T = lambda a: np.ascontiguousarray(a, dtype=np.float32)
    x = inputs["L0_attn_x"]
    d = {}
    d["x_own"] = T(np.concatenate(
        [x[0, 512 * c:512 * (c + 1)], x[1, 512 * c:512 * (c + 1)]], axis=0))
    d["A_slc"] = T(inputs["L0_attn_A_log"][256 * c:256 * (c + 1)])
    d["Dcol"] = T(np.asarray(inputs["L0_attn_D_param"]).reshape(NDB, 128).T)
    d["nwrow"] = T(np.asarray(inputs["L0_attn_norm_w"])[None, :])
    d["nbrow"] = T(np.asarray(inputs["L0_attn_norm_b"])[None, :])
    d["dtbrow"] = T(np.asarray(inputs["L0_attn_dt_proj_b"])[None, :])
    d["l1row"] = T(np.asarray(inputs["L0_ffn_ln1_w"])[None, :])
    d["l2row"] = T(np.asarray(inputs["L0_ffn_ln2_w"])[None, :])
    d["xpwT"] = T(np.asarray(inputs["L0_attn_x_proj_w"]).T)
    d["dtwT"] = T(np.asarray(inputs["L0_attn_dt_proj_w"]).T)
    d["opwT"] = T(np.asarray(inputs["L0_attn_out_proj_w"]).T)
    d["wq_own"] = T(np.asarray(inputs["L0_ffn_wq"])[:, 256 * c:256 * (c + 1)])
    d["wk_own"] = T(np.asarray(inputs["L0_ffn_wk"])[:, 256 * c:256 * (c + 1)])
    d["wv_own"] = T(np.asarray(inputs["L0_ffn_wv"])[:, 256 * c:256 * (c + 1)])
    d["wo_own"] = T(np.asarray(inputs["L0_ffn_wo"])[256 * c:256 * (c + 1), :])
    d["gwT"] = T(np.asarray(inputs["L0_ffn_gate_w"]).T)
    eg = np.asarray(inputs["L0_ffn_eg_w"])
    eu = np.asarray(inputs["L0_ffn_eu_w"])
    ed = np.asarray(inputs["L0_ffn_ed_w"])
    d["egT"] = T(np.stack([eg[2 * c].T, eg[2 * c + 1].T]))
    d["euT"] = T(np.stack([eu[2 * c].T, eu[2 * c + 1].T]))
    d["edT"] = T(np.stack([ed[2 * c].T, ed[2 * c + 1].T]))
    d["sgT"] = T(np.asarray(inputs["L0_ffn_sg_w"]).T)
    d["suT"] = T(np.asarray(inputs["L0_ffn_su_w"]).T)
    d["sdT"] = T(np.asarray(inputs["L0_ffn_sd_w"]).T)
    return d


def kernel(**inputs):
    nc, _ = build()
    in_maps = [make_core_inputs(inputs, c) for c in range(NCORE)]
    res = run_bass_kernel_spmd(nc, in_maps, core_ids=list(range(NCORE)))
    out = np.zeros((NSEQ, SEQ, DM), np.float32)
    for c in range(NCORE):
        o = res.results[c]["out_own"]
        out[0, 512 * c:512 * (c + 1)] = o[0:OWN]
        out[1, 512 * c:512 * (c + 1)] = o[OWN:OWNT]
    return out


def run_debug(inputs, phases, dbg):
    nc, dbg_outs = build(phases=phases, dbg=dbg)
    in_maps = [make_core_inputs(inputs, c) for c in range(NCORE)]
    res = run_bass_kernel_spmd(nc, in_maps, core_ids=list(range(NCORE)))
    return res
